# version 07
# baseline (speedup 1.0000x reference)
import numpy as np
import concourse.bass as bass
import concourse.mybir as mybir
from concourse.bass_utils import run_bass_kernel_spmd
import concourse.bass_utils as _BU

if not getattr(_BU, "_ldw_patched", False):
    _orig_run_command = _BU.run_command

    def _run_command_ldw(argv, **kwargs):
        try:
            argv = ["--enable-ldw-opt=true" if a == "--enable-ldw-opt=false" else a
                    for a in argv]
        except TypeError:
            pass
        return _orig_run_command(argv, **kwargs)

    _BU.run_command = _run_command_ldw
    _BU._ldw_patched = True

F32 = mybir.dt.float32
BF16 = mybir.dt.bfloat16
AF = mybir.ActivationFunctionType
ALU = mybir.AluOpType
AX = mybir.AxisListType

D = 2048
TK = 2048
OWN = 1024
C16 = D // 128
NHL = 8
HD = 128
SCALE = HD ** -0.5
EPS = 1e-5
NE = 16
LNG = 256
NLG = TK // LNG
ROPE_BASE = 10000.0


def _host_consts():
    t = np.arange(TK, dtype=np.float64)
    inv = ROPE_BASE ** (-np.arange(0, HD, 2, dtype=np.float64) / HD)
    ang = np.outer(np.concatenate([inv, inv]), t)
    cosT = np.cos(ang).astype(np.float32)
    sinT = np.sin(ang).astype(np.float32)
    P = np.zeros((128, 128), dtype=np.float32)
    for m in range(64):
        P[m + 64, m] = -1.0
        P[m, m + 64] = 1.0
    id128 = np.eye(128, dtype=np.float32)
    M128 = np.triu(np.full((128, 128), -1e30, dtype=np.float32), k=1)
    ones_row = np.ones((1, 128), dtype=np.float32)
    mvec = np.full((128, 1), 1.0 / TK, dtype=np.float32)
    rvec = np.full((128, 1), 1.0 / HD, dtype=np.float32)
    iota16 = np.tile(np.arange(16, dtype=np.float32), (128, 1))
    iorev = 15.0 - iota16
    return dict(cosT=cosT, sinT=sinT, P=P, id128=id128, M128=M128,
                ones_row=ones_row, mvec=mvec, rvec=rvec,
                iota16=iota16, iorev=iorev)


class KB:
    def __init__(self, nc):
        self.nc = nc
        self.cnt = {}
        self.sem = {}

    def mksem(self, ctx, *names):
        for n in names:
            self.sem[n] = ctx.enter_context(self.nc.semaphore(n))
            self.cnt[n] = 0

    def st(self, eng, sem, inst, n=1):
        inst.then_inc(self.sem[sem], n)
        self.cnt[sem] += n
        eng.wait_ge(self.sem[sem], self.cnt[sem])
        return self.cnt[sem]

    def inc(self, sem, inst, n=1):
        inst.then_inc(self.sem[sem], n)
        self.cnt[sem] += n
        return self.cnt[sem]

    def uid(self):
        self._uid = getattr(self, "_uid", 0) + 1
        return f"r{self._uid}"

    def w(self, eng, sem, at=None):
        eng.wait_ge(self.sem[sem], self.cnt[sem] if at is None else at)


class _ScShim:

    def __init__(self, sc, dmy, kb):
        self._sc = sc
        self._dmy = dmy
        self._kb = kb
        self._pinned = False

    def _ensure(self):
        if not self._pinned:
            self._pinned = True
            d = self._dmy
            kb = self._kb
            kb.st(self._sc, "se", self._sc.activation(
                d[0:1, 0:1], d[0:1, 2:3], AF.Exp, scale=0.0, bias=1.0))
            kb.st(self._sc, "se", self._sc.activation(
                d[0:1, 1:2], d[0:1, 2:3], AF.Ln, scale=0.0, bias=1.0))

    def activation(self, *a, **k):
        self._ensure()
        return self._sc.activation(*a, **k)

    def copy(self, *a, **k):
        self._ensure()
        return self._sc.copy(*a, **k)

    def mul(self, *a, **k):
        self._ensure()
        return self._sc.mul(*a, **k)

    def add(self, *a, **k):
        self._ensure()
        return self._sc.add(*a, **k)

    def __getattr__(self, n):
        return getattr(self._sc, n)


class _BlkProxy:
    def __init__(self, blk, dmy, kb):
        self._blk = blk
        self._dmy = dmy
        self._kb = kb

    def scalar(self, f):
        self._blk.scalar(lambda sc: f(_ScShim(sc, self._dmy, self._kb)))

    def __getattr__(self, n):
        return getattr(self._blk, n)


def bc(ap, like_ap):
    _, out = bass.broadcast_tensor_aps(like_ap, ap)
    return out


def bc3(ps_ap, nmid):
    a = ps_ap
    return bass.AP(a.tensor, a.offset, [a.ap[0], [0, nmid], a.ap[1]])


def build(upto="E", taps=()):
    nc = bass.Bass("TRN2", target_bir_lowering=False)
    kb = KB(nc)
    CH = _host_consts()

    xT = nc.dram_tensor("xT", [NLG, 128, C16, LNG], F32, kind="ExternalInput").ap()
    xo = nc.dram_tensor("xo", [2, 128, C16, 512], F32, kind="ExternalInput").ap()
    wqkvg = nc.dram_tensor("wqkvg", [NHL, 128, 4, C16, 128], F32, kind="ExternalInput").ap()
    wo = nc.dram_tensor("wo", [128, 4, C16, 512], F32, kind="ExternalInput").ap()
    nw = nc.dram_tensor("nw", [128, 1], F32, kind="ExternalInput").ap()
    rwt = nc.dram_tensor("rwt", [128, C16, NE], F32, kind="ExternalInput").ap()
    ebias = nc.dram_tensor("ebias", [NE, 1], F32, kind="ExternalInput").ap()
    wup = nc.dram_tensor("wup", [NE, 4, 128, C16, 256], F32, kind="ExternalInput").ap()
    wdn = nc.dram_tensor("wdn", [NE, 4, 128, 8, 512], F32, kind="ExternalInput").ap()
    sup = nc.dram_tensor("sup", [8, 128, C16, 512], F32, kind="ExternalInput").ap()
    sdn = nc.dram_tensor("sdn", [4, 4, 128, 8, 512], F32, kind="ExternalInput").ap()
    outT = nc.dram_tensor("outT", [128, C16, OWN], F32, kind="ExternalOutput").ap()

    ccin = nc.dram_tensor("ccin", [NHL, 128, TK], BF16, kind="Internal")
    ccout = nc.dram_tensor("ccout", [2 * NHL, 128, TK], BF16, kind="Internal")

    cosT_h = nc.inline_tensor(CH["cosT"], name="cosT").ap()
    sinT_h = nc.inline_tensor(CH["sinT"], name="sinT").ap()
    P_h = nc.inline_tensor(CH["P"], name="Pperm").ap()
    id_h = nc.inline_tensor(CH["id128"], name="id128").ap()
    M128_h = nc.inline_tensor(CH["M128"], name="M128").ap()
    ones_h = nc.inline_tensor(CH["ones_row"], name="onesrow").ap()
    mvec_h = nc.inline_tensor(CH["mvec"], name="mvec").ap()
    rvec_h = nc.inline_tensor(CH["rvec"], name="rvec").ap()
    iota_h = nc.inline_tensor(CH["iota16"], name="iota16").ap()
    iorev_h = nc.inline_tensor(CH["iorev"], name="iorev").ap()

    tap_handles = {}

    def mktap(name, shape, dtype):
        tap_handles[name] = nc.dram_tensor("tap_" + name, shape, dtype,
                                           kind="ExternalOutput").ap()
        return tap_handles[name]

    from contextlib import ExitStack
    with ExitStack() as X:
        E = X.enter_context

        SB = lambda *a, **k: E(nc.sbuf_tensor(*a, side="left", **k))
        P_sb = SB("P_sb", [128, 128], BF16)
        idb_sb = SB("idb_sb", [128, 128], BF16)
        idf_sb = SB("idf_sb", [128, 128], F32)
        ones_sb = SB("ones_sb", [1, 128], F32)
        onesb_sb = SB("onesb_sb", [1, 128], BF16)
        mvec_sb = SB("mvec_sb", [128, 1], BF16)
        rvec_sb = SB("rvec_sb", [128, 1], BF16)
        nw_sb = SB("nw_sb", [128, 1], F32)
        eb_sb = SB("eb_sb", [NE, 1], F32)
        iota_sb = SB("iota_sb", [128, 16], F32)
        iorev_sb = SB("iorev_sb", [128, 16], F32)
        cmb_sb = SB("cmb_sb", [NE, TK], F32)
        rwt_sb = SB("rwt_sb", [128, C16, NE], F32)
        xn_own = SB("xn_own", [128, C16, OWN], BF16)

        dmy_sb = SB("dmy_sb", [1, 4], F32)
        kb.mksem(X, "wg", "wa", "wb", "wc", "wd", "we", "wf", "la", "lb", "wx", "sq", "pe", "ve", "se", "gp", "cc")

        blk = _BlkProxy(E(nc.Block()), dmy_sb, kb)

        def _pid(sc):
            kb.par_reg = sc.alloc_register("parH")
            pid = sc.partition_id()
            sc.reg_alu(kb.par_reg, pid, 2, ALU.mod)
        blk.scalar(_pid)

        def _cl(sy):
            for dst, src in ((idf_sb, id_h), (ones_sb, ones_h), (nw_sb, nw),
                             (eb_sb, ebias), (iota_sb, iota_h), (iorev_sb, iorev_h),
                             (rwt_sb, rwt)):
                kb.inc("sq", sy.dma_start(out=dst[:], in_=src), 16)
        blk.sync(_cl)

        def _clg(gp):
            for dst, src in ((P_sb, P_h), (idb_sb, id_h), (mvec_sb, mvec_h),
                             (rvec_sb, rvec_h)):
                kb.inc("wg", gp.dma_start(out=dst[:], in_=src), 16)
            kb.inc("wg", gp.dma_start(out=onesb_sb[:], in_=ones_h), 16)
        blk.gpsimd(_clg)
        CONST_WG = kb.cnt["wg"]
        CONST_SQ = kb.cnt["sq"]

        with (
            nc.sbuf_tensor("xn_sb", [128, C16, TK], BF16, side="right") as xn_sb,
            nc.sbuf_tensor("logit_sb", [NE, TK], F32, side="right") as logit_sb,
        ):
            with (
                nc.sbuf_tensor("xf0", [128, C16, LNG], F32, side="right") as xf0,
                nc.sbuf_tensor("xf1", [128, C16, LNG], F32, side="right") as xf1,
                nc.sbuf_tensor("xb", [128, C16, LNG], BF16, side="right") as xb,
                nc.sbuf_tensor("xsq", [128, C16, LNG], BF16, side="right") as xsq,
                nc.sbuf_tensor("xnf", [128, C16, LNG], F32, side="right") as xnf,
                nc.sbuf_tensor("murow", [1, LNG], F32, side="right") as murow,
                nc.sbuf_tensor("msqrow", [1, LNG], F32, side="right") as msqrow,
                nc.sbuf_tensor("varrow", [1, LNG], F32, side="right") as varrow,
                nc.sbuf_tensor("rstd", [1, LNG], F32, side="right") as rstd,
                nc.psum_tensor("psMu", [1, LNG], F32) as psMu,
                nc.psum_tensor("psMsq", [1, LNG], F32) as psMsq,
                nc.psum_tensor("psMuB", [128, LNG], F32) as psMuB,
                nc.psum_tensor("psRB", [128, LNG], F32) as psRB,
                nc.psum_tensor("psL", [NE, LNG], F32) as psL,
            ):
                xfs = (xf0, xf1)
                xf_w = {}

                def _ld01(sc):
                    kb.inc("la", sc.dma_start(out=xf0[:], in_=xT[0]), 16)
                    kb.inc("lb", sc.dma_start(out=xf1[:], in_=xT[1]), 16)
                blk.scalar(_ld01)
                xf_w[0] = ("la", kb.cnt["la"])
                xf_w[1] = ("lb", kb.cnt["lb"])

                for g in range(NLG):
                    gs = slice(g * LNG, (g + 1) * LNG)
                    xf = xfs[g % 2]

                    def _cast(sc, xf=xf, gw=xf_w[g]):
                        kb.w(sc, gw[0], gw[1])
                        kb.st(sc, "se", sc.copy(xb[:], xf[:]))
                        kb.st(sc, "se", sc.activation(xsq[:], xf[:], AF.Square))
                    blk.scalar(_cast)
                    se_cast = kb.cnt["se"]

                    def _stats(te, g=g, se_cast=se_cast):
                        kb.w(te, "se", se_cast)
                        if g == 0:
                            kb.w(te, "wg", CONST_WG)
                            kb.w(te, "sq", CONST_SQ)
                        if g > 0:
                            kb.w(te, "ve")
                        for c in range(C16):
                            i = te.matmul(psMu[:], mvec_sb[:], xb[:, c, :],
                                          start=(c == 0), stop=(c == C16 - 1))
                        kb.inc("pe", i)
                        for c in range(C16):
                            i = te.matmul(psMsq[:], mvec_sb[:], xsq[:, c, :],
                                          start=(c == 0), stop=(c == C16 - 1))
                        kb.inc("pe", i)
                    blk.tensor(_stats)
                    pe_stats = kb.cnt["pe"]

                    def _rows(v, pe_stats=pe_stats):
                        kb.w(v, "pe", pe_stats)
                        kb.st(v, "ve", v.tensor_copy(murow[:], psMu[:]))
                        kb.st(v, "ve", v.tensor_copy(msqrow[:], psMsq[:]))
                        kb.st(v, "ve", v.tensor_tensor(varrow[:], murow[:], murow[:], ALU.mult))
                        kb.st(v, "ve", v.tensor_tensor(varrow[:], msqrow[:], varrow[:], ALU.subtract))
                        kb.st(v, "ve", v.tensor_scalar_add(varrow[:], varrow[:], EPS))
                    blk.vector(_rows)
                    ve_rows = kb.cnt["ve"]

                    def _sqrt(sc, ve_rows=ve_rows):
                        kb.w(sc, "ve", ve_rows)
                        kb.st(sc, "se", sc.activation(rstd[:], varrow[:], AF.Ln))
                        kb.st(sc, "se", sc.activation(rstd[:], rstd[:], AF.Exp,
                                                      scale=-0.5))
                    blk.scalar(_sqrt)
                    se_sqrt = kb.cnt["se"]

                    def _bcast(te, se_sqrt=se_sqrt, g=g):
                        kb.w(te, "se", se_sqrt)
                        if g > 0:
                            kb.w(te, "ve")
                        kb.inc("pe", te.matmul(psMuB[:], ones_sb[:], murow[:],
                                               start=True, stop=True))
                        kb.inc("pe", te.matmul(psRB[:], ones_sb[:], rstd[:],
                                               start=True, stop=True))
                    blk.tensor(_bcast)
                    pe_b = kb.cnt["pe"]

                    def _norm(v, pe_b=pe_b, xf=xf):
                        kb.w(v, "pe", pe_b)
                        kb.st(v, "ve", v.tensor_tensor(xnf[:], xf[:], bc3(psMuB[:], C16),
                                                       ALU.subtract))
                        kb.st(v, "ve", v.tensor_tensor(xnf[:], xnf[:], bc3(psRB[:], C16),
                                                       ALU.mult))
                    blk.vector(_norm)
                    ve_norm = kb.cnt["ve"]

                    if g + 2 < NLG:
                        def _ld2(sc, g=g, ve_norm=ve_norm):
                            kb.w(sc, "ve", ve_norm)
                            kb.inc(("la", "lb")[g % 2],
                                   sc.dma_start(out=xfs[g % 2][:], in_=xT[g + 2]), 16)
                        blk.scalar(_ld2)
                        xf_w[g + 2] = (("la", "lb")[g % 2],
                                       kb.cnt[("la", "lb")[g % 2]])

                    def _store(sc, g=g, ve_norm=ve_norm):
                        kb.w(sc, "ve", ve_norm)
                        kb.st(sc, "se", sc.copy(xn_sb[:, :, gs], xnf[:]))
                    blk.scalar(_store)

                    def _router(te, g=g, ve_norm=ve_norm):
                        kb.w(te, "ve", ve_norm)
                        if g > 0:
                            kb.w(te, "ve")
                        for c in range(C16):
                            i = te.matmul(psL[:], rwt_sb[:, c, :], xnf[:, c, :],
                                          start=(c == 0), stop=(c == C16 - 1))
                        kb.inc("pe", i)
                    blk.tensor(_router)
                    pe_r = kb.cnt["pe"]

                    def _evl(v, g=g, pe_r=pe_r):
                        kb.w(v, "pe", pe_r)
                        kb.st(v, "ve", v.tensor_copy(logit_sb[:, gs], psL[:]))
                    blk.vector(_evl)

            if "xn" in taps or "logits" in taps:
                txn = mktap("xn", [128, C16, TK], BF16) if "xn" in taps else None
                tlg = mktap("logits", [NE, TK], F32) if "logits" in taps else None

                def _tap(sy):
                    kb.w(sy, "se")
                    kb.w(sy, "ve")
                    if txn is not None:
                        kb.inc("sq", sy.dma_start(out=txn, in_=xn_sb[:]), 16)
                    if tlg is not None:
                        kb.inc("sq", sy.dma_start(out=tlg, in_=logit_sb[:]), 16)
                blk.sync(_tap)

            if upto >= "B":
                emit_routing(kb, blk, nc, X, logit_sb, cmb_sb, eb_sb,
                             iota_sb, iorev_sb, idf_sb, ones_sb, taps, mktap)

            if upto >= "C":
                emit_attention(kb, blk, nc, X, xn_sb, xn_own, wqkvg,
                               cosT_h, sinT_h, P_sb, idb_sb, idf_sb, ones_sb,
                               M128_h, nw_sb, rvec_sb, onesb_sb, ccin,
                               taps, mktap)

        if upto >= "D":
            acc_sb = SB("acc_sb", [128, C16, OWN], F32)
            emit_cc(kb, blk, nc, ccin, ccout)
            if upto >= "E":
                emit_moe(kb, blk, nc, X, acc_sb, xn_own, cmb_sb, wup, wdn,
                         ones_sb, xo, taps, mktap)
            emit_oproj(kb, blk, nc, X, acc_sb, xo, wo, ccin, ccout, taps, mktap,
                       init_acc=(upto < "E"))
            if upto >= "E":
                emit_shared(kb, blk, nc, X, acc_sb, xn_own, sup, sdn, taps, mktap)

            def _out(sy):
                kb.w(sy, "ve")
                kb.w(sy, "se")
                kb.inc("sq", sy.dma_start(out=outT, in_=acc_sb[:]), 16)
                kb.w(sy, "sq")
            blk.sync(_out)
        else:
            def _fin(sy):
                kb.w(sy, "sq")
                kb.w(sy, "wx")
                kb.w(sy, "gp")
            blk.sync(_fin)

    return nc, tap_handles


def emit_routing(kb, blk, nc, X, logit_sb, cmb_sb, eb_sb, iota_sb, iorev_sb,
                 idf_sb, ones_sb, taps, mktap):
    E = X.enter_context
    with (
        nc.sbuf_tensor("sig_sb", [NE, TK], F32, side="right") as sig_sb,
        nc.sbuf_tensor("sbb_sb", [NE, TK], F32, side="right") as sbb_sb,
        nc.sbuf_tensor("ss_t", [128, 16], F32, side="right") as ss_t,
        nc.sbuf_tensor("sb_t", [128, 16], F32, side="right") as sb_t,
        nc.sbuf_tensor("r1", [128, 4], F32, side="right") as r1,
        nc.sbuf_tensor("r2", [128, 4], F32, side="right") as r2,
        nc.sbuf_tensor("r3", [128, 4], F32, side="right") as r3,
        nc.sbuf_tensor("r4", [128, 4], F32, side="right") as r4,
        nc.sbuf_tensor("gsum", [128, 4], F32, side="right") as gsum,
        nc.sbuf_tensor("gsec", [128, 1], F32, side="right") as gsec,
        nc.sbuf_tensor("msk", [128, 16], F32, side="right") as msk,
        nc.sbuf_tensor("oh1", [128, 16], F32, side="right") as oh1,
        nc.sbuf_tensor("oh2", [128, 16], F32, side="right") as oh2,
        nc.sbuf_tensor("mrow", [128, 1], F32, side="right") as mrow,
        nc.sbuf_tensor("w1", [128, 1], F32, side="right") as w1,
        nc.sbuf_tensor("w2", [128, 1], F32, side="right") as w2,
        nc.sbuf_tensor("cmt", [128, 16], F32, side="right") as cmt,
        nc.psum_tensor("psTr", [128, 512], F32) as psTr,
        nc.psum_tensor("psTb", [NE, 512], F32) as psTb,
    ):
        def _sig(sc):
            kb.w(sc, "ve")
            kb.w(sc, "pe")
            kb.w(sc, "wx")
            kb.st(sc, "se", sc.activation(sig_sb[:], logit_sb[:], AF.Exp,
                                          scale=-1.0))
        blk.scalar(_sig)
        se_e = kb.cnt["se"]

        def _sigv(v, se_e=se_e):
            kb.w(v, "se", se_e)
            kb.st(v, "ve", v.tensor_scalar_add(sig_sb[:], sig_sb[:], 1.0))
            kb.st(v, "ve", v.reciprocal(sig_sb[:], sig_sb[:]))
        blk.vector(_sigv)
        ve_sig = kb.cnt["ve"]

        def _sig2(sc, ve_sig=ve_sig):
            kb.w(sc, "ve", ve_sig)
            kb.st(sc, "se", sc.add(sbb_sb[:], sig_sb[:], eb_sb[:]))
        blk.scalar(_sig2)
        se_sig = kb.cnt["se"]

        for tt in range(TK // 128):
            ts = slice(tt * 128, (tt + 1) * 128)

            def _tr(te, ts=ts, se_sig=se_sig, tt=tt):
                kb.w(te, "se", se_sig)
                if tt > 0:
                    kb.w(te, "ve")
                kb.inc("pe", te.transpose(psTr[:, 0:16], sig_sb[:, ts],
                                          idf_sb[0:16, 0:16]))
                kb.inc("pe", te.transpose(psTr[:, 16:32], sbb_sb[:, ts],
                                          idf_sb[0:16, 0:16]))
            blk.tensor(_tr)
            pe_tr = kb.cnt["pe"]

            def _route(v, pe_tr=pe_tr):
                S = lambda inst: kb.st(v, "ve", inst)
                kb.w(v, "pe", pe_tr)
                S(v.tensor_copy(ss_t[:], psTr[:, 0:16]))
                S(v.tensor_copy(sb_t[:], psTr[:, 16:32]))
                a = [sb_t[:, i::4] for i in range(4)]
                S(v.tensor_max(r1[:], a[0], a[1]))
                S(v.tensor_tensor(r2[:], a[0], a[1], ALU.min))
                S(v.tensor_max(r3[:], a[2], a[3]))
                S(v.tensor_tensor(r4[:], a[2], a[3], ALU.min))
                S(v.tensor_max(gsum[:], r1[:], r3[:]))
                S(v.tensor_tensor(r1[:], r1[:], r3[:], ALU.min))
                S(v.tensor_max(r2[:], r2[:], r4[:]))
                S(v.tensor_max(r1[:], r1[:], r2[:]))
                S(v.tensor_add(gsum[:], gsum[:], r1[:]))
                b = [gsum[:, i:i + 1] for i in range(4)]
                S(v.tensor_max(r2[:, 0:1], b[0], b[1]))
                S(v.tensor_tensor(r2[:, 1:2], b[0], b[1], ALU.min))
                S(v.tensor_max(r2[:, 2:3], b[2], b[3]))
                S(v.tensor_tensor(r2[:, 3:4], b[2], b[3], ALU.min))
                S(v.tensor_tensor(r3[:, 0:1], r2[:, 0:1], r2[:, 2:3], ALU.min))
                S(v.tensor_max(r3[:, 1:2], r2[:, 1:2], r2[:, 3:4]))
                S(v.tensor_max(gsec[:], r3[:, 0:1], r3[:, 1:2]))
                S(v.tensor_tensor(r4[:], gsum[:], bc(gsec.ap(), gsum.ap()), ALU.is_ge))
                gm = bass.AP(r4, 0, [r4.ap().ap[0], [1, 4], [0, 4]])
                sb4 = bass.AP(sb_t, 0, [sb_t.ap().ap[0], [4, 4], [1, 4]])
                mk4 = bass.AP(msk, 0, [msk.ap().ap[0], [4, 4], [1, 4]])
                S(v.tensor_tensor(mk4, sb4, gm, ALU.mult))
                S(v.reduce_max(mrow[:], msk[:], axis=AX.X))
                S(v.tensor_tensor(oh1[:], msk[:], bc(mrow.ap(), msk.ap()), ALU.is_equal))
                S(v.tensor_tensor(oh1[:], oh1[:], iorev_sb[:], ALU.mult))
                S(v.reduce_max(mrow[:], oh1[:], axis=AX.X))
                S(v.tensor_scalar(mrow[:], mrow[:], -1.0, 15.0, ALU.mult, ALU.add))
                S(v.tensor_tensor(oh1[:], iota_sb[:], bc(mrow.ap(), iota_sb.ap()),
                                  ALU.is_equal))
                S(v.scalar_tensor_tensor(msk[:], oh1[:], -1e30, msk[:],
                                         ALU.mult, ALU.add))
                S(v.reduce_max(mrow[:], msk[:], axis=AX.X))
                S(v.tensor_tensor(oh2[:], msk[:], bc(mrow.ap(), msk.ap()), ALU.is_equal))
                S(v.tensor_tensor(oh2[:], oh2[:], iorev_sb[:], ALU.mult))
                S(v.reduce_max(mrow[:], oh2[:], axis=AX.X))
                S(v.tensor_scalar(mrow[:], mrow[:], -1.0, 15.0, ALU.mult, ALU.add))
                S(v.tensor_tensor(oh2[:], iota_sb[:], bc(mrow.ap(), iota_sb.ap()),
                                  ALU.is_equal))
                S(v.tensor_tensor(cmt[:], oh1[:], ss_t[:], ALU.mult))
                S(v.reduce_sum(w1[:], cmt[:], axis=AX.X))
                S(v.tensor_tensor(cmt[:], oh2[:], ss_t[:], ALU.mult))
                S(v.reduce_sum(w2[:], cmt[:], axis=AX.X))
                S(v.tensor_add(mrow[:], w1[:], w2[:]))
                S(v.tensor_scalar_add(mrow[:], mrow[:], 1e-20))
                S(v.reciprocal(mrow[:], mrow[:]))
                S(v.tensor_tensor(w1[:], w1[:], mrow[:], ALU.mult))
                S(v.tensor_tensor(w2[:], w2[:], mrow[:], ALU.mult))
                S(v.tensor_tensor(cmt[:], oh1[:], bc(w1.ap(), oh1.ap()), ALU.mult))
                S(v.scalar_tensor_tensor(oh2[:], oh2[:], 1.0, bc(w2.ap(), oh2.ap()),
                                         ALU.mult, ALU.mult))
                S(v.tensor_add(cmt[:], cmt[:], oh2[:]))
            blk.vector(_route)
            ve_rt = kb.cnt["ve"]

            def _trb(te, ts=ts, ve_rt=ve_rt):
                kb.w(te, "ve", ve_rt)
                kb.inc("pe", te.transpose(psTb[:, 0:128], cmt[:], idf_sb[:]))
            blk.tensor(_trb)
            pe_tb = kb.cnt["pe"]

            def _evc(v, ts=ts, pe_tb=pe_tb):
                kb.w(v, "pe", pe_tb)
                kb.st(v, "ve", v.tensor_copy(cmb_sb[:, ts], psTb[0:NE, 0:128]))
            blk.vector(_evc)

    if "cmb" in taps:
        t = mktap("cmb", [NE, TK], F32)

        def _tapc(sy):
            kb.w(sy, "ve")
            kb.inc("sq", sy.dma_start(out=t, in_=cmb_sb[:]), 16)
        blk.sync(_tapc)


def emit_attention(kb, blk, nc, X, xn_sb, xn_own, wqkvg, cosT_h, sinT_h,
                   P_sb, idb_sb, idf_sb, ones_sb, M128_h, nw_sb, rvec_sb,
                   onesb_sb, ccin, taps, mktap):
    NT = TK // 128
    from contextlib import ExitStack
    with ExitStack() as XA:
        cos_sb = XA.enter_context(nc.sbuf_tensor("cos_sb", [128, TK], F32, side="right"))
        sin_sb = XA.enter_context(nc.sbuf_tensor("sin_sb", [128, TK], F32, side="right"))
        M128_sb = XA.enter_context(nc.sbuf_tensor("M128_sb", [128, 128], F32, side="right"))
        wblk0 = XA.enter_context(nc.sbuf_tensor("wblk0", [128, 2, C16, 128], BF16, side="right"))
        wblk1 = XA.enter_context(nc.sbuf_tensor("wblk1", [128, 2, C16, 128], BF16, side="right"))
        qro = XA.enter_context(nc.sbuf_tensor("qro", [128, TK], BF16, side="right"))
        kro = XA.enter_context(nc.sbuf_tensor("kro", [128, TK], BF16, side="right"))
        vT = XA.enter_context(nc.sbuf_tensor("vT", [128, NT, 128], BF16, side="right"))
        gbf = XA.enter_context(nc.sbuf_tensor("gbf", [128, TK], F32, side="right"))
        stage = XA.enter_context(nc.sbuf_tensor("stage", [128, 512], BF16, side="right"))
        qsc1 = XA.enter_context(nc.sbuf_tensor("qsc1", [128, 512], F32, side="right"))
        qsc2 = XA.enter_context(nc.sbuf_tensor("qsc2", [128, 512], F32, side="right"))
        e0 = XA.enter_context(nc.sbuf_tensor("e0", [128, TK], BF16, side="right"))
        e1 = XA.enter_context(nc.sbuf_tensor("e1", [128, TK], BF16, side="right"))
        eT2 = XA.enter_context(nc.sbuf_tensor("eT2", [128, 2, NT, 128], BF16, side="right"))
        mcols = XA.enter_context(nc.sbuf_tensor("mcols", [128, 8], F32, side="right"))
        bcols = XA.enter_context(nc.sbuf_tensor("bcols", [128, 8], F32, side="right"))
        scols = XA.enter_context(nc.sbuf_tensor("scols", [128, 8], F32, side="right"))
        acols = XA.enter_context(nc.sbuf_tensor("acols", [128, 2], F32, side="right"))
        srow_sb = XA.enter_context(nc.sbuf_tensor("srow_sb", [1, 256], F32, side="right"))
        arow_sb = XA.enter_context(nc.sbuf_tensor("arow_sb", [1, 256], F32, side="right"))
        crow_sb = XA.enter_context(nc.sbuf_tensor("crow_sb", [1, 256], F32, side="right"))
        sq_sb = XA.enter_context(nc.sbuf_tensor("sq_sb", [128, 256], BF16, side="right"))
        av2_sb = XA.enter_context(nc.sbuf_tensor("av2_sb", [128, 256], F32, side="right"))
        t1_sb = XA.enter_context(nc.sbuf_tensor("t1_sb", [128, 256], F32, side="right"))
        sgt_sb = XA.enter_context(nc.sbuf_tensor("sgt_sb", [128, 256], F32, side="right"))
        gat_sb = XA.enter_context(nc.sbuf_tensor("gat_sb", [128, TK], BF16, side="right"))
        psS = XA.enter_context(nc.psum_tensor("psS", [128, 2, 1024], F32))
        psAV = XA.enter_context(nc.psum_tensor("psAV", [128, 512], F32))
        psRow = XA.enter_context(nc.psum_tensor("psRow", [128, 512], F32))
        psT0 = XA.enter_context(nc.psum_tensor("psT0", [128, 512], BF16))
        psT1 = XA.enter_context(nc.psum_tensor("psT1", [128, 512], BF16))
        psTb = (psT0, psT1)
        psAV0 = psAV[:, 0:128]
        psAV1 = psAV[:, 128:256]
        e_par = (e0, e1)
        av_ps = (psAV0, psAV1)
        av_full = psAV
        wbs = (wblk0, wblk1)
        G = {}
        now = {s: kb.cnt[s] for s in ("pe", "ve", "se", "gp", "wx", "sq")}
        for k in ("psS0", "psS1", "psTb0", "psTb1", "av", "psRa", "psRb"):
            G[k] = [("pe", now["pe"]), ("ve", now["ve"]), ("se", now["se"])]

        def need(eng, key):
            for sem, cnt in G.get(key, ()):
                kb.w(eng, sem, cnt)
            G[key] = []

        def guard(key, sem):
            G.setdefault(key, []).append((sem, kb.cnt[sem]))

        def _acl(sy):
            for s in ("pe", "ve", "se", "gp", "wx", "sq"):
                kb.w(sy, s, now[s])
            kb.inc("sq", sy.dma_start(out=cos_sb[:], in_=cosT_h), 16)
            kb.inc("sq", sy.dma_start(out=sin_sb[:], in_=sinT_h), 16)
            kb.inc("sq", sy.dma_start(out=M128_sb[:], in_=M128_h), 16)
        blk.sync(_acl)
        ACL = kb.cnt["sq"]

        def _wl0(gp):
            for s in ("pe", "ve", "se", "sq", "wx"):
                kb.w(gp, s, now[s])
            kb.inc("wa", gp.dma_start(out=wblk0[:], in_=wqkvg[0][:, 0:2]), 16)
            kb.inc("wb", gp.dma_start(out=wblk1[:], in_=wqkvg[0][:, 2:4]), 16)
        blk.gpsimd(_wl0)

        for g in range(NHL):

            for mat, name in ((0, "q"), (1, "k"), (2, "v"), (3, "g")):
                for i in range(4):
                    seg = slice(i * 512, (i + 1) * 512)
                    s = i % 2
                    pp = psS[:, s, 0:512]
                    pr = psS[:, s, 512:1024]
                    skey = f"psS{s}"

                    wb = wbs[mat // 2]

                    def _mm(te, mat=mat, seg=seg, pp=pp, skey=skey, g=g, wb=wb,
                            i=i):
                        if i == 0:
                            kb.w(te, "wa")
                            kb.w(te, "wb")
                        need(te, skey)
                        for c in range(C16):
                            inst = te.matmul(pp, wb[:, mat % 2, c, :],
                                             xn_sb[:, c, seg],
                                             start=(c == 0), stop=(c == C16 - 1))
                        kb.inc("pe", inst)
                    blk.tensor(_mm)
                    pe_mm = kb.cnt["pe"]

                    if name in ("q", "k"):
                        def _stg(sc, pp=pp, pe_mm=pe_mm):
                            kb.w(sc, "pe", pe_mm)
                            need(sc, "stage")
                            kb.st(sc, "se", sc.copy(stage[:], pp))
                        blk.scalar(_stg)
                        se_stage = kb.cnt["se"]

                        def _rot(te, pr=pr, se_stage=se_stage):
                            kb.w(te, "se", se_stage)
                            kb.inc("pe", te.matmul(pr, P_sb[:], stage[:],
                                                   start=True, stop=True))
                            guard("stage", "pe")
                        blk.tensor(_rot)
                        pe_rot = kb.cnt["pe"]

                        def _mul(v, pp=pp, pr=pr, seg=seg, pe_rot=pe_rot,
                                 skey=skey, g=g, i=i):
                            if g == 0 and i == 0:
                                kb.w(v, "sq", ACL)
                            kb.w(v, "pe", pe_rot)
                            need(v, "qsc")
                            kb.st(v, "ve", v.tensor_tensor(qsc1[:], pp,
                                                           cos_sb[:, seg], ALU.mult))
                            kb.st(v, "ve", v.tensor_tensor(qsc2[:], pr,
                                                           sin_sb[:, seg], ALU.mult))
                            guard(skey, "ve")
                        blk.vector(_mul)
                        ve_mul = kb.cnt["ve"]

                        dst, dkey = (qro, "qro") if name == "q" else (kro, "kro")

                        def _add(gp, dst=dst, dkey=dkey, seg=seg, ve_mul=ve_mul, i=i):
                            kb.w(gp, "ve", ve_mul)
                            if i == 0:
                                need(gp, dkey)
                            kb.st(gp, "gp", gp.tensor_add(dst[:, seg], qsc1[:], qsc2[:]))
                            guard("qsc", "gp")
                        blk.gpsimd(_add)
                        if name == "k" and i == 3 and g + 1 < NHL:
                            guard("wblk0", "pe")

                            def _wlq(gp, g=g):
                                need(gp, "wblk0")
                                kb.inc("wa", gp.dma_start(
                                    out=wblk0[:], in_=wqkvg[g + 1][:, 0:2]), 16)
                            blk.gpsimd(_wlq)
                    elif name == "v":
                        def _stg(sc, pp=pp, pe_mm=pe_mm, skey=skey):
                            kb.w(sc, "pe", pe_mm)
                            need(sc, "stage")
                            kb.st(sc, "se", sc.copy(stage[:], pp))
                            guard(skey, "se")
                        blk.scalar(_stg)
                        se_stage = kb.cnt["se"]

                        for j in range(4):
                            tc = i * 4 + j
                            pb = psTb[(j // 2) % 2]
                            po = (j % 2) * 128

                            def _tp(te, j=j, se_stage=se_stage, pb=pb, po=po):
                                if j % 2 == 0:
                                    need(te, f"psTb{(j // 2) % 2}")
                                if j == 0:
                                    kb.w(te, "se", se_stage)
                                kb.inc("pe", te.transpose(
                                    pb[:, po:po + 128],
                                    stage[:, j * 128:(j + 1) * 128], idb_sb[:]))
                                if j == 3:
                                    guard("stage", "pe")
                            blk.tensor(_tp)
                            pe_tp = kb.cnt["pe"]

                            if j % 2 == 1:
                                def _ev(v, j=j, tc=tc, pe_tp=pe_tp, pb=pb):
                                    kb.w(v, "pe", pe_tp)
                                    if tc <= 1:
                                        need(v, "vT")
                                    kb.st(v, "ve", v.tensor_copy(
                                        vT[:, tc - 1:tc + 1, :], pb[:, 0:256]))
                                    guard(f"psTb{(j // 2) % 2}", "ve")
                                blk.vector(_ev)
                    else:
                        def _stg(sc, pp=pp, seg=seg, pe_mm=pe_mm, skey=skey, i=i):
                            kb.w(sc, "pe", pe_mm)
                            if i == 0:
                                need(sc, "gbf")
                            kb.st(sc, "se", sc.copy(gbf[:, seg], pp))
                            guard(skey, "se")
                        blk.scalar(_stg)
                        if i == 3 and g + 1 < NHL:
                            guard("wblk1", "pe")

                            def _wlv(gp, g=g):
                                need(gp, "wblk1")
                                kb.inc("wb", gp.dma_start(
                                    out=wblk1[:], in_=wqkvg[g + 1][:, 2:4]), 16)
                            blk.gpsimd(_wlv)


            cc_chunk = 0
            for b2 in range(NT // 2):
                q0 = 2 * b2
                tile_info = []
                for qj in (0, 1):
                    gq = q0 + qj
                    par = gq % 2
                    ebuf = e_par[par]
                    qseg = slice(gq * 128, (gq + 1) * 128)
                    nkt = gq + 1
                    two = nkt > 8
                    chunks = [(0, min(nkt, 8))] + ([(8, nkt)] if two else [])
                    mb4 = 4 * qj
                    mA = mcols[:, mb4:mb4 + 1]
                    mB = mcols[:, mb4 + 1:mb4 + 2]
                    mfin = mcols[:, mb4 + 2:mb4 + 3]
                    bA = bcols[:, mb4:mb4 + 1]
                    bfin = bcols[:, mb4 + 2:mb4 + 3]
                    sA = scols[:, mb4:mb4 + 1]
                    sB = scols[:, mb4 + 1:mb4 + 2]
                    sfin = scols[:, mb4 + 2:mb4 + 3]
                    acol = acols[:, qj:qj + 1]

                    for ci, (k0, k1) in enumerate(chunks):
                        w = (k1 - k0) * 128
                        slot = cc_chunk % 2
                        cc_chunk += 1
                        mc = (mA, mB)[ci]

                        def _smm(te, k0=k0, slot=slot, qseg=qseg, w=w,
                                 firstb=(qj == 0 and ci == 0)):
                            if firstb:
                                kb.w(te, "gp")
                            need(te, f"psS{slot}")
                            off = 0
                            while off < w:
                                ww = min(512, w - off)
                                inst = te.matmul(
                                    psS[:, slot, off:off + ww], qro[:, qseg],
                                    kro[:, k0 * 128 + off:k0 * 128 + off + ww],
                                    start=True, stop=True)
                                off += ww
                            kb.inc("pe", inst)
                        blk.tensor(_smm)
                        pe_s = kb.cnt["pe"]
                        diag = (k1 - 1 == gq)

                        def _msk(v, slot=slot, w=w, pe_s=pe_s, diag=diag, mc=mc,
                                 first=(g == 0 and gq == 0)):
                            if first:
                                kb.w(v, "sq", ACL)
                            kb.w(v, "pe", pe_s)
                            if diag:
                                kb.st(v, "ve", v.tensor_add(
                                    psS[:, slot, w - 128:w],
                                    psS[:, slot, w - 128:w], M128_sb[:]))
                            kb.st(v, "ve", v.reduce_max(mc, psS[:, slot, 0:w],
                                                        axis=AX.X))
                        blk.vector(_msk)

                        if ci == 0:
                            def _bA(v, bA=bA, mA=mA):
                                kb.st(v, "ve", v.tensor_scalar_mul(bA, mA, -SCALE))
                            blk.vector(_bA)
                            ve_b = kb.cnt["ve"]
                            bias_use = bA
                        else:
                            def _mrg1(v, mA=mA, mB=mB, mfin=mfin, bfin=bfin):
                                kb.w(v, "pe")
                                kb.st(v, "ve", v.tensor_max(mfin, mA, mB))
                                kb.st(v, "ve", v.tensor_scalar_mul(bfin, mfin, -SCALE))
                            blk.vector(_mrg1)
                            ve_b = kb.cnt["ve"]
                            bias_use = bfin

                        def _exp(sc, slot=slot, w=w, k0=k0, ve_b=ve_b, ebuf=ebuf,
                                 bias_use=bias_use, scol=(sA, sB)[ci], par=par):
                            kb.w(sc, "ve", ve_b)
                            if k0 == 0:
                                need(sc, f"e{par}")
                            kb.st(sc, "se", sc.activation(
                                ebuf[:, k0 * 128:k0 * 128 + w], psS[:, slot, 0:w],
                                AF.Exp, bias=bias_use, scale=SCALE, accum_out=scol))
                            guard(f"psS{slot}", "se")
                        blk.scalar(_exp)
                    se_exp = kb.cnt["se"]

                    if two:
                        def _alp(sc, mA=mA, bfin=bfin, acol=acol):
                            kb.st(sc, "se", sc.activation(acol, mA, AF.Exp,
                                                          bias=bfin, scale=SCALE))
                        blk.scalar(_alp)
                        se_a = kb.cnt["se"]

                        def _mrg2(v, ebuf=ebuf, acol=acol, sA=sA, sB=sB,
                                  sfin=sfin, se_a=se_a):
                            kb.w(v, "se", se_a)
                            kb.st(v, "ve", v.tensor_scalar_mul(
                                ebuf[:, 0:1024], ebuf[:, 0:1024], acol))
                            kb.st(v, "ve", v.tensor_tensor(sfin, sA, acol, ALU.mult))
                            kb.st(v, "ve", v.tensor_add(sfin, sfin, sB))
                        blk.vector(_mrg2)
                    else:
                        def _mrg0(v, sA=sA, sfin=sfin, se_exp=se_exp):
                            kb.w(v, "se", se_exp)
                            kb.w(v, "pe")
                            kb.st(v, "ve", v.tensor_copy(sfin, sA))
                        blk.vector(_mrg0)
                    tile_info.append((qj, par, nkt, kb.cnt["ve"], kb.cnt["se"]))

                for (qj, par, nkt, ve_esc, se_tile) in tile_info:
                    ebuf = e_par[par]
                    for kt in range(nkt):
                        bkn = (kt // 2) % 2
                        pb = psTb[bkn]
                        po = (kt % 2) * 128

                        def _tp(te, kt=kt, ebuf=ebuf, ve_esc=ve_esc,
                                se_tile=se_tile, pb=pb, po=po, bkn=bkn):
                            if kt % 2 == 0:
                                need(te, f"psTb{bkn}")
                            if kt == 0:
                                kb.w(te, "se", se_tile)
                                kb.w(te, "ve", ve_esc)
                            kb.inc("pe", te.transpose(
                                pb[:, po:po + 128],
                                ebuf[:, kt * 128:(kt + 1) * 128], idb_sb[:]))
                        blk.tensor(_tp)
                        pe_tp = kb.cnt["pe"]

                        if kt % 2 == 1 or kt == nkt - 1:
                            k0e = kt - (kt % 2)
                            wid = (kt % 2) + 1

                            def _ev(v, kt=kt, qj=qj, pe_tp=pe_tp, pb=pb,
                                    k0e=k0e, wid=wid, bkn=bkn):
                                kb.w(v, "pe", pe_tp)
                                if k0e == 0:
                                    need(v, f"eT{qj}")
                                kb.st(v, "ve", v.tensor_copy(
                                    eT2[:, qj, k0e:k0e + wid, :],
                                    pb[:, 0:wid * 128]))
                                guard(f"psTb{bkn}", "ve")
                            blk.vector(_ev)
                    guard(f"e{par}", "pe")

                def _z(v, q0=q0):
                    kb.st(v, "ve", v.memset(eT2[:, 0, q0 + 1, :], 0))
                blk.vector(_z)
                ve_eT = kb.cnt["ve"]
                lim1 = q0 + 2

                def _av(te, lim1=lim1, ve_eT=ve_eT):
                    kb.w(te, "ve", ve_eT)
                    need(te, "av")
                    for kt in range(lim1):
                        inst = te.matmul(av_full[:, 0:256], vT[:, kt, :],
                                         eT2[:, 0:2, kt, :],
                                         start=(kt == 0),
                                         stop=(kt == lim1 - 1))
                    kb.inc("pe", inst)
                    guard("eT0", "pe")
                    guard("eT1", "pe")
                blk.tensor(_av)
                pe_av = kb.cnt["pe"]

                bseg = slice(q0 * 128, (q0 + 2) * 128)
                sfin0 = scols[:, 2:3]
                sfin1 = scols[:, 6:7]
                s2ps = psRow[0:1, 256:512]
                sr0 = psRow[0:1, 0:128]
                sr1 = psRow[0:1, 128:256]
                cbps = psRow[:, 0:256]

                def _gsq(v, pe_av=pe_av, bseg=bseg):
                    kb.w(v, "pe", pe_av)
                    need(v, "av")
                    kb.st(v, "ve", v.tensor_copy(av2_sb[:], av_full[:, 0:256]))
                    guard("av", "ve")
                    kb.st(v, "ve", v.tensor_tensor(sq_sb[:], av2_sb[:], av2_sb[:],
                                                   ALU.mult))
                blk.vector(_gsq)
                ve_gsq = kb.cnt["ve"]

                def _gs2(te, ve_gsq=ve_gsq, sfin0=sfin0, sfin1=sfin1):
                    kb.w(te, "ve", ve_gsq)
                    need(te, "psRa")
                    kb.inc("pe", te.matmul(s2ps, rvec_sb[:], sq_sb[:],
                                           start=True, stop=True))
                    kb.inc("pe", te.transpose(sr0, sfin0, idf_sb[:]))
                    kb.inc("pe", te.transpose(sr1, sfin1, idf_sb[:]))
                blk.tensor(_gs2)
                pe_s2 = kb.cnt["pe"]

                def _grow(v, pe_s2=pe_s2):
                    kb.w(v, "pe", pe_s2)
                    kb.st(v, "ve", v.tensor_copy(srow_sb[:], psRow[0:1, 0:256]))
                    kb.st(v, "ve", v.tensor_tensor(arow_sb[:], srow_sb[:],
                                                   srow_sb[:], ALU.mult))
                    kb.st(v, "ve", v.scalar_tensor_tensor(arow_sb[:], arow_sb[:],
                                                          EPS, s2ps, ALU.mult,
                                                          ALU.add))
                    guard("psRa", "ve")
                blk.vector(_grow)
                ve_row = kb.cnt["ve"]

                def _gln(sc, ve_row=ve_row):
                    kb.w(sc, "ve", ve_row)
                    kb.st(sc, "se", sc.activation(crow_sb[:], arow_sb[:], AF.Ln))
                    kb.st(sc, "se", sc.activation(crow_sb[:], crow_sb[:], AF.Exp,
                                                  scale=-0.5))
                    kb.st(sc, "se", sc.activation(sgt_sb[:], gbf[:, bseg], AF.Exp,
                                                  scale=-1.0))
                blk.scalar(_gln)
                se_ln = kb.cnt["se"]

                def _gcb(te, se_ln=se_ln):
                    kb.w(te, "se", se_ln)
                    need(te, "psRb")
                    kb.inc("pe", te.matmul(psRow[:, 0:128], ones_sb[:],
                                           crow_sb[:, 0:128], start=True, stop=True))
                    kb.inc("pe", te.matmul(psRow[:, 128:256], ones_sb[:],
                                           crow_sb[:, 128:256], start=True, stop=True))
                blk.tensor(_gcb)
                pe_cb = kb.cnt["pe"]

                def _gt(v, pe_cb=pe_cb, bseg=bseg,
                        first=(b2 == 0)):
                    kb.w(v, "pe", pe_cb)
                    if first:
                        need(v, "gat")
                    kb.st(v, "ve", v.tensor_scalar_add(sgt_sb[:], sgt_sb[:], 1.0))
                    kb.st(v, "ve", v.reciprocal(sgt_sb[:], sgt_sb[:]))
                    kb.st(v, "ve", v.tensor_tensor(sgt_sb[:], sgt_sb[:],
                                                   gbf[:, bseg], ALU.mult))
                    kb.st(v, "ve", v.tensor_scalar_mul(t1_sb[:], av2_sb[:],
                                                       nw_sb[:]))
                    kb.st(v, "ve", v.tensor_tensor(t1_sb[:], t1_sb[:], cbps,
                                                   ALU.mult))
                    guard("psRb", "ve")
                    kb.st(v, "ve", v.tensor_tensor(gat_sb[:, bseg], t1_sb[:],
                                                   sgt_sb[:], ALU.mult))
                blk.vector(_gt)
            guard("qro", "pe")
            guard("kro", "pe")
            guard("vT", "pe")
            guard("gbf", "ve")

            def _st(sy, g=g):
                kb.w(sy, "ve")
                kb.inc("sq", sy.dma_start(out=ccin[g], in_=gat_sb[:]), 16)
                guard("gat", "sq")
            blk.sync(_st)

        if "gated" in taps:
            t = mktap("gated", [NHL, 128, TK], BF16)

            def _tapg(sy):
                kb.w(sy, "sq")
                kb.inc("sq", sy.dma_start(out=t, in_=ccin.ap()), 16)
            blk.sync(_tapg)

    def _own(sc):
        kb.w(sc, "se")
        kb.w(sc, "ve")
        h = kb.sem["wx"]
        par = kb.par_reg
        if True:
            with sc.If_eq(par, 0):
                sc.dma_start(out=xn_own[:], in_=xn_sb[:, :, 0:OWN]).then_inc(h, 16)
            with sc.Else():
                sc.dma_start(out=xn_own[:], in_=xn_sb[:, :, OWN:TK]).then_inc(h, 16)
        kb.cnt["wx"] += 16
    blk.scalar(_own)


def emit_cc(kb, blk, nc, ccin, ccout):
    now = {s: kb.cnt[s] for s in ("pe", "ve", "se")}

    def _cc(gp):
        kb.w(gp, "sq")
        for s in ("pe", "ve", "se"):
            kb.w(gp, s, now[s])
        kb.inc("cc", gp.collective_compute(
            "AllGather", ALU.bypass,
            replica_groups=[[0, 1], [2, 3], [4, 5], [6, 7]],
            ins=[ccin.ap().rearrange("h p t -> (h p) t").opt()],
            outs=[ccout.ap().rearrange("h p t -> (h p) t").opt()]))
    blk.gpsimd(_cc)


def emit_oproj(kb, blk, nc, X, acc_sb, xo, wo, ccin, ccout, taps, mktap,
               init_acc=False):
    from contextlib import ExitStack
    with ExitStack() as XO:
        gf = XO.enter_context(nc.sbuf_tensor("gf", [128, C16, OWN], BF16, side="right"))
        wob0 = XO.enter_context(nc.sbuf_tensor("wob0", [128, C16, 512], BF16, side="right"))
        wob1 = XO.enter_context(nc.sbuf_tensor("wob1", [128, C16, 512], BF16, side="right"))
        psO = XO.enter_context(nc.psum_tensor("psO", [128, 2, 512], F32))
        wobs = (wob0, wob1)
        now = {s: kb.cnt[s] for s in ("pe", "ve", "se", "gp", "wx", "sq")}
        guards = {k: [(s, kb.cnt[s]) for s in ("pe", "ve", "se", "wx", "sq")]
                  for k in ("psO0", "psO1", "wob0", "wob1")}

        def _gfl(sc):
            kb.w(sc, "cc")
            for s in ("pe", "ve", "se", "gp", "sq", "wx"):
                kb.w(sc, s, now[s])
            h = kb.sem["wx"]
            par = kb.par_reg
            with sc.If_eq(par, 0):
                sc.dma_start(out=gf[:], in_=ccout.ap()[:, :, 0:OWN]
                             .rearrange("h p t -> p h t")).then_inc(h, 16)
            with sc.Else():
                sc.dma_start(out=gf[:], in_=ccout.ap()[:, :, OWN:TK]
                             .rearrange("h p t -> p h t")).then_inc(h, 16)
            kb.cnt["wx"] += 16
        blk.scalar(_gfl)
        wx_gf = kb.cnt["wx"]

        if init_acc:
            def _acc0(sy):
                for s in ("pe", "ve", "se", "gp", "sq", "wx"):
                    kb.w(sy, s, now[s])
                kb.inc("sq", sy.dma_start(out=acc_sb[:, :, 0:512], in_=xo[0]), 16)
                kb.inc("sq", sy.dma_start(out=acc_sb[:, :, 512:1024], in_=xo[1]), 16)
            blk.sync(_acc0)
        sq_acc = kb.cnt["sq"]

        for j in range(4):
            wob = wobs[j % 2]

            def _wol(gp, j=j, first=(j == 0)):
                if first:
                    for s in ("pe", "ve", "se", "sq", "wx"):
                        kb.w(gp, s, now[s])
                for sem, cnt in guards[f"wob{j % 2}"]:
                    kb.w(gp, sem, cnt)
                guards[f"wob{j % 2}"] = []
                kb.inc(("wa", "wb")[j % 2], gp.dma_start(out=wobs[j % 2][:], in_=wo[:, j]), 16)
            blk.gpsimd(_wol)
            for mc4 in range(4):
                m = j * 4 + mc4
                for ts in range(2):
                    sl = (m * 2 + ts) % 2
                    tseg = slice(ts * 512, (ts + 1) * 512)

                    def _omm(te, wob=wob, mc4=mc4, tseg=tseg, sl=sl, m=m, ts=ts):
                        if mc4 == 0 and ts == 0:
                            kb.w(te, "wa")
                            kb.w(te, "wb")
                        if m == 0 and ts == 0:
                            kb.w(te, "wx", wx_gf)
                        for sem, cnt in guards[f"psO{sl}"]:
                            kb.w(te, sem, cnt)
                        guards[f"psO{sl}"] = []
                        for c in range(C16):
                            inst = te.matmul(psO[:, sl, :],
                                             wob[:, c, mc4 * 128:(mc4 + 1) * 128],
                                             gf[:, c, tseg],
                                             start=(c == 0), stop=(c == C16 - 1))
                        kb.inc("pe", inst)
                    blk.tensor(_omm)
                    pe_o = kb.cnt["pe"]

                    def _oacc(v, m=m, tseg=tseg, sl=sl, pe_o=pe_o, first=(j == 0 and mc4 == 0 and ts == 0)):
                        kb.w(v, "pe", pe_o)
                        if first:
                            kb.w(v, "sq", sq_acc)
                        kb.st(v, "ve", v.tensor_tensor(acc_sb[:, m, tseg],
                                                       acc_sb[:, m, tseg],
                                                       psO[:, sl, :], ALU.add))
                        guards[f"psO{sl}"].append(("ve", kb.cnt["ve"]))
                    blk.vector(_oacc)
            guards[f"wob{j % 2}"].append(("pe", kb.cnt["pe"]))

    if "acc_o" in taps:
        t = mktap("acc_o", [128, C16, OWN], F32)

        def _tapo(sy):
            kb.w(sy, "ve")
            kb.inc("sq", sy.dma_start(out=t, in_=acc_sb[:]), 16)
        blk.sync(_tapo)


def emit_moe(kb, blk, nc, X, acc_sb, xn_own, cmb_sb, wup, wdn, ones_sb,
             xo, taps, mktap):
    from contextlib import ExitStack
    with ExitStack() as XM:
        a_blk = XM.enter_context(nc.sbuf_tensor("a_blk", [128, 2, 8, 1024], BF16, side="right"))
        wupb0 = XM.enter_context(nc.sbuf_tensor("wupb0", [128, C16, 256], BF16, side="right"))
        wupb1 = XM.enter_context(nc.sbuf_tensor("wupb1", [128, C16, 256], BF16, side="right"))
        wdnb = XM.enter_context(nc.sbuf_tensor("wdnb", [128, 4, 8, 512], BF16, side="right"))
        cbf = XM.enter_context(nc.sbuf_tensor("cbf", [128, 2, 2, 1024], BF16, side="right"))
        crow = XM.enter_context(nc.sbuf_tensor("crow", [1, 1024], F32, side="right"))
        trelu = XM.enter_context(nc.sbuf_tensor("trelu", [128, 1024], F32, side="right"))
        tsq = XM.enter_context(nc.sbuf_tensor("tsq", [128, 1024], BF16, side="right"))
        psH = XM.enter_context(nc.psum_tensor("psH", [128, 2, 1024], F32))
        psR = XM.enter_context(nc.psum_tensor("psR", [128, 2, 1024], F32))
        wupbs = (wupb0, wupb1)
        now = {s: kb.cnt[s] for s in ("pe", "ve", "se", "gp", "wx", "sq")}
        G = {}

        def need(eng, key):
            for sem, cnt in G.get(key, ()):
                kb.w(eng, sem, cnt)
            G[key] = []

        def guard(key, sem):
            G.setdefault(key, []).append((sem, kb.cnt[sem]))

        for k in ("psH0", "psH1", "psR0", "psR1", "wdn0", "wdn1", "wdn2", "wdn3"):
            G[k] = [(s, now[s]) for s in ("pe", "ve", "se", "wx", "sq")]

        def _acc0(sy):
            for s in ("pe", "ve", "se", "gp", "sq", "wx"):
                kb.w(sy, s, now[s])
            kb.inc("sq", sy.dma_start(out=acc_sb[:, :, 0:512], in_=xo[0]), 16)
            kb.inc("sq", sy.dma_start(out=acc_sb[:, :, 512:1024], in_=xo[1]), 16)
        blk.sync(_acc0)
        sq_acc = kb.cnt["sq"]

        def prep_cb(eb):
            sl2 = eb % 2
            e0p = 2 * eb
            for ep in range(2):
                e = e0p + ep

                def _crow(sc, e=e, first=(eb == 0 and ep == 0)):
                    if first:
                        for s in ("pe", "ve", "se", "gp", "sq", "wx"):
                            kb.w(sc, s, now[s])
                    kb.w(sc, "pe")
                    h = kb.sem["wx"]
                    par = kb.par_reg
                    if True:
                        with sc.If_eq(par, 0):
                            sc.dma_start(out=crow[:], in_=cmb_sb[e:e + 1, 0:OWN]).then_inc(h, 16)
                        with sc.Else():
                            sc.dma_start(out=crow[:], in_=cmb_sb[e:e + 1, OWN:TK]).then_inc(h, 16)
                    kb.cnt["wx"] += 16
                blk.scalar(_crow)
                wx_cr = kb.cnt["wx"]

                def _cbm(te, ep=ep, wx_cr=wx_cr):
                    kb.w(te, "wx", wx_cr)
                    need(te, "psH0")
                    kb.inc("pe", te.matmul(psH[:, 0, 0:512], ones_sb[:],
                                           crow[0:1, 0:512], start=True, stop=True))
                    kb.inc("pe", te.matmul(psH[:, 0, 512:1024], ones_sb[:],
                                           crow[0:1, 512:1024], start=True, stop=True))
                blk.tensor(_cbm)
                pe_cb = kb.cnt["pe"]

                def _cbe(v, ep=ep, pe_cb=pe_cb, sl2=sl2):
                    kb.w(v, "pe", pe_cb)
                    need(v, f"cbf{sl2}{ep}")
                    kb.st(v, "ve", v.tensor_copy(cbf[:, sl2, ep, :], psH[:, 0, :]))
                    guard("psH0", "ve")
                blk.vector(_cbe)

        prep_cb(0)
        for eb in range(8):
            e0 = 2 * eb
            sl2 = eb % 2

            def _wdl_pair(mq):
                for ep in range(2):
                    slw = (mq % 2) * 2 + ep

                    def _wdl(gp, e=e0 + ep, mq=mq, slw=slw):
                        need(gp, f"wdn{slw}")
                        kb.inc(("wc", "wd", "we", "wf")[slw], gp.dma_start(
                            out=wdnb[:, slw], in_=wdn[e, mq]), 16)
                    blk.gpsimd(_wdl)
            _wdl_pair(0)
            _wdl_pair(1)
            if eb + 1 < 8:
                prep_cb(eb + 1)

            for ep in range(2):
                e = e0 + ep
                for qh in range(4):
                    wb = wupbs[qh % 2]

                    def _wul(gp, e=e, qh=qh, first=(eb == 0 and ep == 0 and qh == 0)):
                        if first:
                            for s in ("pe", "ve", "se", "sq", "wx"):
                                kb.w(gp, s, now[s])
                        need(gp, f"wup{qh % 2}")
                        kb.inc(("wa", "wb")[qh % 2], gp.dma_start(out=wupbs[qh % 2][:], in_=wup[e, qh]), 16)
                    blk.gpsimd(_wul)

                    for i2 in range(2):
                        ic = qh * 2 + i2
                        sl = ic % 2

                        def _umm(te, wb=wb, i2=i2, sl=sl):
                            if i2 == 0:
                                kb.w(te, "wa")
                                kb.w(te, "wb")
                            need(te, f"psH{sl}")
                            for ts in range(2):
                                for c in range(C16):
                                    inst = te.matmul(
                                        psH[:, sl, ts * 512:(ts + 1) * 512],
                                        wb[:, c, i2 * 128:(i2 + 1) * 128],
                                        xn_own[:, c, ts * 512:(ts + 1) * 512],
                                        start=(c == 0), stop=(c == C16 - 1))
                            kb.inc("pe", inst)
                        blk.tensor(_umm)
                        pe_u = kb.cnt["pe"]

                        def _rlv(v, sl=sl, pe_u=pe_u):
                            kb.w(v, "pe", pe_u)
                            need(v, "trelu")
                            kb.st(v, "ve", v.tensor_scalar_max(trelu[:],
                                                               psH[:, sl, :], 0.0))
                            guard(f"psH{sl}", "ve")
                        blk.vector(_rlv)
                        ve_rl = kb.cnt["ve"]

                        def _rl(sc, ve_rl=ve_rl):
                            kb.w(sc, "ve", ve_rl)
                            need(sc, "tsq")
                            kb.st(sc, "se", sc.activation(tsq[:], trelu[:], AF.Square))
                            guard("trelu", "se")
                        blk.scalar(_rl)
                        se_rl = kb.cnt["se"]

                        def _am(v, ep=ep, ic=ic, se_rl=se_rl,
                                first=(ep == 0 and qh == 0 and i2 == 0)):
                            kb.w(v, "se", se_rl)
                            if first:
                                need(v, "ablk")
                            kb.st(v, "ve", v.tensor_tensor(a_blk[:, ep, ic, :],
                                                           tsq[:], cbf[:, sl2, ep, :],
                                                           ALU.mult))
                            guard("tsq", "ve")
                            if ic == 7:
                                guard(f"cbf{sl2}{ep}", "ve")
                        blk.vector(_am)
                    guard(f"wup{qh % 2}", "pe")

            for mq in range(4):
                for m4 in range(4):
                    m = mq * 4 + m4
                    sl = m % 2

                    def _dmm(te, mq=mq, m4=m4, sl=sl, ve_a=kb.cnt["ve"]):
                        if m4 == 0:
                            for s in ("wc", "wd", "we", "wf"):
                                kb.w(te, s)
                            kb.w(te, "ve", ve_a)
                        need(te, f"psR{sl}")
                        for ts in range(2):
                            for ep in range(2):
                                slw = (mq % 2) * 2 + ep
                                for ic in range(8):
                                    inst = te.matmul(
                                        psR[:, sl, ts * 512:(ts + 1) * 512],
                                        wdnb[:, slw, ic, m4 * 128:(m4 + 1) * 128],
                                        a_blk[:, ep, ic, ts * 512:(ts + 1) * 512],
                                        start=(ep == 0 and ic == 0),
                                        stop=(ep == 1 and ic == 7))
                        kb.inc("pe", inst)
                        if m4 == 3:
                            for ep in range(2):
                                guard(f"wdn{(mq % 2) * 2 + ep}", "pe")
                    blk.tensor(_dmm)
                    pe_d = kb.cnt["pe"]

                    def _dacc(v, m=m, sl=sl, pe_d=pe_d,
                              first=(eb == 0 and mq == 0 and m4 == 0)):
                        kb.w(v, "pe", pe_d)
                        if first:
                            kb.w(v, "sq", sq_acc)
                        kb.st(v, "ve", v.tensor_tensor(acc_sb[:, m, :],
                                                       acc_sb[:, m, :],
                                                       psR[:, sl, :], ALU.add))
                        guard(f"psR{sl}", "ve")
                    blk.vector(_dacc)
                if mq + 2 < 4:
                    _wdl_pair(mq + 2)
            guard("ablk", "pe")
            G.setdefault("ablk_w", [])

    if "acc_m" in taps:
        t = mktap("acc_m", [128, C16, OWN], F32)

        def _tapm(sy):
            kb.w(sy, "ve")
            kb.inc("sq", sy.dma_start(out=t, in_=acc_sb[:]), 16)
        blk.sync(_tapm)


def emit_shared(kb, blk, nc, X, acc_sb, xn_own, sup, sdn, taps, mktap):
    from contextlib import ExitStack
    with ExitStack() as XS:
        hs = XS.enter_context(nc.sbuf_tensor("hs", [128, 8, 1024], BF16, side="right"))
        supb0 = XS.enter_context(nc.sbuf_tensor("supb0", [128, C16, 512], BF16, side="right"))
        supb1 = XS.enter_context(nc.sbuf_tensor("supb1", [128, C16, 512], BF16, side="right"))
        sdnb0 = XS.enter_context(nc.sbuf_tensor("sdnb0", [128, 8, 512], BF16, side="right"))
        sdnb1 = XS.enter_context(nc.sbuf_tensor("sdnb1", [128, 8, 512], BF16, side="right"))
        strl = XS.enter_context(nc.sbuf_tensor("strl", [128, 1024], F32, side="right"))
        stsq = XS.enter_context(nc.sbuf_tensor("stsq", [128, 1024], F32, side="right"))
        psH = XS.enter_context(nc.psum_tensor("psHs", [128, 2, 1024], F32))
        psR = XS.enter_context(nc.psum_tensor("psRs", [128, 2, 1024], F32))
        supbs = (supb0, supb1)
        sdnbs = (sdnb0, sdnb1)
        now = {s: kb.cnt[s] for s in ("pe", "ve", "se", "gp", "wx", "sq")}
        G = {}

        def need(eng, key):
            for sem, cnt in G.get(key, ()):
                kb.w(eng, sem, cnt)
            G[key] = []

        def guard(key, sem):
            G.setdefault(key, []).append((sem, kb.cnt[sem]))

        for k in ("psH0", "psH1", "psR0", "psR1", "sup0", "sup1", "sdn0", "sdn1"):
            G[k] = [(s, now[s]) for s in ("pe", "ve", "se", "wx", "sq")]

        for ib in range(4):
            for jh in range(2):
                def _sul(gp, ib=ib, jh=jh, first=(ib == 0 and jh == 0)):
                    if first:
                        for s in ("pe", "ve", "se", "sq", "wx"):
                            kb.w(gp, s, now[s])
                    need(gp, f"sup{jh}")
                    kb.inc(("wa", "wb")[jh], gp.dma_start(out=supbs[jh][:], in_=sup[ib * 2 + jh]), 16)
                blk.gpsimd(_sul)

            for jh in range(2):
                for i2 in range(4):
                    ic = jh * 4 + i2
                    sl = ic % 2

                    def _smm(te, jh=jh, i2=i2, sl=sl):
                        if i2 == 0:
                            kb.w(te, "wa")
                            kb.w(te, "wb")
                        need(te, f"psH{sl}")
                        for ts in range(2):
                            for c in range(C16):
                                inst = te.matmul(
                                    psH[:, sl, ts * 512:(ts + 1) * 512],
                                    supbs[jh][:, c, i2 * 128:(i2 + 1) * 128],
                                    xn_own[:, c, ts * 512:(ts + 1) * 512],
                                    start=(c == 0), stop=(c == C16 - 1))
                        kb.inc("pe", inst)
                        if i2 == 3:
                            guard(f"sup{jh}", "pe")
                    blk.tensor(_smm)
                    pe_u = kb.cnt["pe"]

                    def _rlv(v, sl=sl, pe_u=pe_u):
                        kb.w(v, "pe", pe_u)
                        need(v, "strl")
                        kb.st(v, "ve", v.tensor_scalar_max(strl[:],
                                                           psH[:, sl, :], 0.0))
                        guard(f"psH{sl}", "ve")
                    blk.vector(_rlv)
                    ve_rl = kb.cnt["ve"]

                    def _rl(sc, ic=ic, ve_rl=ve_rl):
                        kb.w(sc, "ve", ve_rl)
                        kb.st(sc, "se", sc.activation(hs[:, ic, :], strl[:],
                                                      AF.Square))
                        guard("strl", "se")
                    blk.scalar(_rl)

            for j in range(4):
                def _sdl(gp, ib=ib, j=j):
                    need(gp, f"sdn{j % 2}")
                    kb.inc(("wc", "wd")[j % 2], gp.dma_start(out=sdnbs[j % 2][:], in_=sdn[ib, j]), 16)
                blk.gpsimd(_sdl)
                for m4 in range(4):
                    m = j * 4 + m4
                    sl = m % 2

                    def _dmm(te, j=j, m4=m4, sl=sl, se_a=kb.cnt["se"]):
                        if m4 == 0:
                            kb.w(te, "wc")
                            kb.w(te, "wd")
                            kb.w(te, "se", se_a)
                        need(te, f"psR{sl}")
                        for ts in range(2):
                            for ic in range(8):
                                inst = te.matmul(
                                    psR[:, sl, ts * 512:(ts + 1) * 512],
                                    sdnbs[j % 2][:, ic, m4 * 128:(m4 + 1) * 128],
                                    hs[:, ic, ts * 512:(ts + 1) * 512],
                                    start=(ic == 0), stop=(ic == 7))
                        kb.inc("pe", inst)
                        if m4 == 3:
                            guard(f"sdn{j % 2}", "pe")
                    blk.tensor(_dmm)
                    pe_d = kb.cnt["pe"]

                    def _dacc(v, m=m, sl=sl, pe_d=pe_d):
                        kb.w(v, "pe", pe_d)
                        kb.st(v, "ve", v.tensor_tensor(acc_sb[:, m, :],
                                                       acc_sb[:, m, :],
                                                       psR[:, sl, :], ALU.add))
                        guard(f"psR{sl}", "ve")
                    blk.vector(_dacc)


def stage_inputs(L0_attn_hidden_states, L0_attn_wq, L0_attn_wk, L0_attn_wv,
                 L0_attn_wg, L0_attn_wo, L0_attn_norm_w, L0_ffn_router_weight,
                 L0_ffn_e_bias, L0_ffn_w_up, L0_ffn_w_down, L0_ffn_shared_up,
                 L0_ffn_shared_down):
    f = np.ascontiguousarray
    x = np.asarray(L0_attn_hidden_states, dtype=np.float32)
    wt4 = np.stack([np.asarray(w, np.float32).T for w in
                    (L0_attn_wq, L0_attn_wk, L0_attn_wv, L0_attn_wg)])
    wo_s = f(np.asarray(L0_attn_wo, np.float32).T
             .reshape(C16, 128, 4, 512).transpose(1, 2, 0, 3))
    nw_s = f(np.asarray(L0_attn_norm_w, np.float32).reshape(128, 1))
    rwt_s = f(np.asarray(L0_ffn_router_weight, np.float32).T
              .reshape(C16, 128, NE).transpose(1, 0, 2))
    eb_s = f(np.asarray(L0_ffn_e_bias, np.float32).reshape(NE, 1))
    wup_s = f(np.asarray(L0_ffn_w_up, np.float32)
              .reshape(NE, C16, 128, 4, 256).transpose(0, 3, 2, 1, 4))
    wdn_s = f(np.asarray(L0_ffn_w_down, np.float32)
              .reshape(NE, 8, 128, 4, 512).transpose(0, 3, 2, 1, 4))
    sup_s = f(np.asarray(L0_ffn_shared_up, np.float32)
              .reshape(C16, 128, 8, 512).transpose(2, 1, 0, 3))
    sdn_s = f(np.asarray(L0_ffn_shared_down, np.float32)
              .reshape(4, 8, 128, 4, 512).transpose(0, 3, 2, 1, 4))

    in_maps = []
    for c in range(8):
        b, H = c // 2, c % 2
        xb = x[b]
        xT_s = f(xb.reshape(NLG, LNG, C16, 128).transpose(0, 3, 2, 1))
        xo_s = f(xb[H * OWN:(H + 1) * OWN].reshape(2, 512, C16, 128)
                 .transpose(0, 3, 2, 1))
        wq_s = f(wt4[:, :, H * 1024:(H + 1) * 1024]
                 .reshape(4, C16, 128, NHL, 128).transpose(3, 2, 0, 1, 4))
        in_maps.append({
            "xT": xT_s, "xo": xo_s, "wqkvg": wq_s, "wo": wo_s, "nw": nw_s,
            "rwt": rwt_s, "ebias": eb_s, "wup": wup_s, "wdn": wdn_s,
            "sup": sup_s, "sdn": sdn_s,
        })
    return in_maps


def kernel(**inputs):
    in_maps = stage_inputs(**inputs)
    nc, _ = build()
    res = run_bass_kernel_spmd(nc, in_maps, core_ids=list(range(8)))
    out = np.empty((4, TK, D), dtype=np.float32)
    for c in range(8):
        b, H = c // 2, c % 2
        o = res.results[c]["outT"]
        out[b, H * OWN:(H + 1) * OWN, :] = o.transpose(2, 1, 0).reshape(OWN, D)
    return out


# version 08
# speedup vs baseline: 1.1992x; 1.1992x over previous
import numpy as np
import concourse.bass as bass
import concourse.mybir as mybir
from concourse.bass_utils import run_bass_kernel_spmd
import concourse.bass_utils as _BU

if not getattr(_BU, "_ldw_patched", False):
    _orig_run_command = _BU.run_command

    def _run_command_ldw(argv, **kwargs):
        try:
            argv = ["--enable-ldw-opt=true" if a == "--enable-ldw-opt=false" else a
                    for a in argv]
        except TypeError:
            pass
        return _orig_run_command(argv, **kwargs)

    _BU.run_command = _run_command_ldw
    _BU._ldw_patched = True

F32 = mybir.dt.float32
BF16 = mybir.dt.bfloat16
AF = mybir.ActivationFunctionType
ALU = mybir.AluOpType
AX = mybir.AxisListType

D = 2048
TK = 2048
OWN = 1024
C16 = D // 128
NHL = 8
HD = 128
SCALE = HD ** -0.5
EPS = 1e-5
NE = 16
LNG = 256
NLG = TK // LNG
ROPE_BASE = 10000.0


def _host_consts():
    t = np.arange(TK, dtype=np.float64)
    inv = ROPE_BASE ** (-np.arange(0, HD, 2, dtype=np.float64) / HD)
    ang = np.outer(np.concatenate([inv, inv]), t)
    cosT = np.cos(ang).astype(np.float32)
    sinT = np.sin(ang).astype(np.float32)
    P = np.zeros((128, 128), dtype=np.float32)
    for m in range(64):
        P[m + 64, m] = -1.0
        P[m, m + 64] = 1.0
    id128 = np.eye(128, dtype=np.float32)
    M128 = np.triu(np.full((128, 128), -1e30, dtype=np.float32), k=1)
    ones_row = np.ones((1, 128), dtype=np.float32)
    mvec = np.full((128, 1), 1.0 / TK, dtype=np.float32)
    rvec = np.full((128, 1), 1.0 / HD, dtype=np.float32)
    iota16 = np.tile(np.arange(16, dtype=np.float32), (128, 1))
    iorev = 15.0 - iota16
    return dict(cosT=cosT, sinT=sinT, P=P, id128=id128, M128=M128,
                ones_row=ones_row, mvec=mvec, rvec=rvec,
                iota16=iota16, iorev=iorev)


class KB:
    def __init__(self, nc):
        self.nc = nc
        self.cnt = {}
        self.sem = {}

    def mksem(self, ctx, *names):
        for n in names:
            self.sem[n] = ctx.enter_context(self.nc.semaphore(n))
            self.cnt[n] = 0

    def st(self, eng, sem, inst, n=1):
        inst.then_inc(self.sem[sem], n)
        self.cnt[sem] += n
        eng.wait_ge(self.sem[sem], self.cnt[sem])
        return self.cnt[sem]

    def inc(self, sem, inst, n=1):
        inst.then_inc(self.sem[sem], n)
        self.cnt[sem] += n
        return self.cnt[sem]

    def uid(self):
        self._uid = getattr(self, "_uid", 0) + 1
        return f"r{self._uid}"

    def w(self, eng, sem, at=None):
        eng.wait_ge(self.sem[sem], self.cnt[sem] if at is None else at)


class _ScShim:

    def __init__(self, sc, dmy, kb):
        self._sc = sc
        self._dmy = dmy
        self._kb = kb
        self._pinned = False

    def _ensure(self):
        if not self._pinned:
            self._pinned = True
            d = self._dmy
            kb = self._kb
            kb.st(self._sc, "se", self._sc.activation(
                d[0:1, 0:1], d[0:1, 2:3], AF.Exp, scale=0.0, bias=1.0))
            kb.st(self._sc, "se", self._sc.activation(
                d[0:1, 1:2], d[0:1, 2:3], AF.Ln, scale=0.0, bias=1.0))

    def activation(self, *a, **k):
        self._ensure()
        return self._sc.activation(*a, **k)

    def copy(self, *a, **k):
        self._ensure()
        return self._sc.copy(*a, **k)

    def mul(self, *a, **k):
        self._ensure()
        return self._sc.mul(*a, **k)

    def add(self, *a, **k):
        self._ensure()
        return self._sc.add(*a, **k)

    def __getattr__(self, n):
        return getattr(self._sc, n)


class _FakeBlk:

    def __init__(self, nc, dmy, kb, real):
        self._nc = nc
        self._real = real
        self._shim = None
        self._dmy = dmy
        self._kb = kb

    def mega(self, emitfn):
        def _run(_e):
            self._shim = _ScShim(self._nc.scalar, self._dmy, self._kb)
            emitfn()
        self._real.sync(_run)

    def tensor(self, f):
        f(self._nc.tensor)

    def vector(self, f):
        f(self._nc.vector)

    def gpsimd(self, f):
        f(self._nc.gpsimd)

    def sync(self, f):
        f(self._nc.sync)

    def scalar(self, f):
        f(self._shim)


def bc(ap, like_ap):
    _, out = bass.broadcast_tensor_aps(like_ap, ap)
    return out


def bc3(ps_ap, nmid):
    a = ps_ap
    return bass.AP(a.tensor, a.offset, [a.ap[0], [0, nmid], a.ap[1]])


def build(upto="E", taps=()):
    nc = bass.Bass("TRN2", target_bir_lowering=False)
    kb = KB(nc)
    CH = _host_consts()

    xT = nc.dram_tensor("xT", [NLG, 128, C16, LNG], F32, kind="ExternalInput").ap()
    xo = nc.dram_tensor("xo", [2, 128, C16, 512], F32, kind="ExternalInput").ap()
    wqkvg = nc.dram_tensor("wqkvg", [NHL, 128, 4, C16, 128], F32, kind="ExternalInput").ap()
    wo = nc.dram_tensor("wo", [128, 4, C16, 512], F32, kind="ExternalInput").ap()
    nw = nc.dram_tensor("nw", [128, 1], F32, kind="ExternalInput").ap()
    rwt = nc.dram_tensor("rwt", [128, C16, NE], F32, kind="ExternalInput").ap()
    ebias = nc.dram_tensor("ebias", [NE, 1], F32, kind="ExternalInput").ap()
    wup = nc.dram_tensor("wup", [NE, 4, 128, C16, 256], F32, kind="ExternalInput").ap()
    wdn = nc.dram_tensor("wdn", [NE, 4, 128, 8, 512], F32, kind="ExternalInput").ap()
    sup = nc.dram_tensor("sup", [8, 128, C16, 512], F32, kind="ExternalInput").ap()
    sdn = nc.dram_tensor("sdn", [4, 4, 128, 8, 512], F32, kind="ExternalInput").ap()
    outT = nc.dram_tensor("outT", [128, C16, OWN], F32, kind="ExternalOutput").ap()

    ccin = nc.dram_tensor("ccin", [NHL, 128, TK], BF16, kind="Internal")
    ccout = nc.dram_tensor("ccout", [2 * NHL, 128, TK], BF16, kind="Internal")

    cosT_h = nc.inline_tensor(CH["cosT"], name="cosT").ap()
    sinT_h = nc.inline_tensor(CH["sinT"], name="sinT").ap()
    P_h = nc.inline_tensor(CH["P"], name="Pperm").ap()
    id_h = nc.inline_tensor(CH["id128"], name="id128").ap()
    M128_h = nc.inline_tensor(CH["M128"], name="M128").ap()
    ones_h = nc.inline_tensor(CH["ones_row"], name="onesrow").ap()
    mvec_h = nc.inline_tensor(CH["mvec"], name="mvec").ap()
    rvec_h = nc.inline_tensor(CH["rvec"], name="rvec").ap()
    iota_h = nc.inline_tensor(CH["iota16"], name="iota16").ap()
    iorev_h = nc.inline_tensor(CH["iorev"], name="iorev").ap()

    tap_handles = {}

    def mktap(name, shape, dtype):
        tap_handles[name] = nc.dram_tensor("tap_" + name, shape, dtype,
                                           kind="ExternalOutput").ap()
        return tap_handles[name]

    from contextlib import ExitStack
    with ExitStack() as X:
        E = X.enter_context

        SB = lambda *a, **k: E(nc.sbuf_tensor(*a, side="left", **k))
        P_sb = SB("P_sb", [128, 128], BF16)
        idb_sb = SB("idb_sb", [128, 128], BF16)
        idf_sb = SB("idf_sb", [128, 128], F32)
        ones_sb = SB("ones_sb", [1, 128], F32)
        onesb_sb = SB("onesb_sb", [1, 128], BF16)
        mvec_sb = SB("mvec_sb", [128, 1], BF16)
        rvec_sb = SB("rvec_sb", [128, 1], BF16)
        nw_sb = SB("nw_sb", [128, 1], F32)
        eb_sb = SB("eb_sb", [NE, 1], F32)
        iota_sb = SB("iota_sb", [128, 16], F32)
        iorev_sb = SB("iorev_sb", [128, 16], F32)
        cmb_sb = SB("cmb_sb", [NE, TK], F32)
        rwt_sb = SB("rwt_sb", [128, C16, NE], F32)
        xn_own = SB("xn_own", [128, C16, OWN], BF16)

        dmy_sb = SB("dmy_sb", [1, 4], F32)
        kb.mksem(X, "wg", "wa", "wb", "wc", "wd", "we", "wf", "la", "lb", "wx", "sq", "pe", "ve", "se", "gp", "cc")

        blkR = E(nc.Block())
        blk = _FakeBlk(nc, dmy_sb, kb, blkR)

        def _pid(sc):
            kb.par_reg = sc.alloc_register("parH")
            pid = sc.partition_id()
            sc.reg_alu(kb.par_reg, pid, 2, ALU.mod)
        def _cl(sy):
            for dst, src in ((idf_sb, id_h), (ones_sb, ones_h), (nw_sb, nw),
                             (eb_sb, ebias), (iota_sb, iota_h), (iorev_sb, iorev_h),
                             (rwt_sb, rwt)):
                kb.inc("sq", sy.dma_start(out=dst[:], in_=src), 16)
        def _clg(gp):
            for dst, src in ((P_sb, P_h), (idb_sb, id_h), (mvec_sb, mvec_h),
                             (rvec_sb, rvec_h)):
                kb.inc("wg", gp.dma_start(out=dst[:], in_=src), 16)
            kb.inc("wg", gp.dma_start(out=onesb_sb[:], in_=ones_h), 16)
        def _pre():
            blk.scalar(_pid)
            blk.sync(_cl)
            blk.gpsimd(_clg)
        blk.mega(_pre)
        CONST_WG = kb.cnt["wg"]
        CONST_SQ = kb.cnt["sq"]

        with (
            nc.sbuf_tensor("xn_sb", [128, C16, TK], BF16, side="right") as xn_sb,
            nc.sbuf_tensor("logit_sb", [NE, TK], F32, side="right") as logit_sb,
        ):
            def _phaseA():
                with (
                    nc.sbuf_tensor("xf0", [128, C16, LNG], F32, side="right") as xf0,
                    nc.sbuf_tensor("xf1", [128, C16, LNG], F32, side="right") as xf1,
                    nc.sbuf_tensor("xb", [128, C16, LNG], BF16, side="right") as xb,
                    nc.sbuf_tensor("xsq", [128, C16, LNG], BF16, side="right") as xsq,
                    nc.sbuf_tensor("xnf", [128, C16, LNG], F32, side="right") as xnf,
                    nc.sbuf_tensor("murow", [1, LNG], F32, side="right") as murow,
                    nc.sbuf_tensor("msqrow", [1, LNG], F32, side="right") as msqrow,
                    nc.sbuf_tensor("varrow", [1, LNG], F32, side="right") as varrow,
                    nc.sbuf_tensor("rstd", [1, LNG], F32, side="right") as rstd,
                    nc.psum_tensor("psMu", [1, LNG], F32) as psMu,
                    nc.psum_tensor("psMsq", [1, LNG], F32) as psMsq,
                    nc.psum_tensor("psMuB", [128, LNG], F32) as psMuB,
                    nc.psum_tensor("psRB", [128, LNG], F32) as psRB,
                    nc.psum_tensor("psL", [NE, LNG], F32) as psL,
                ):
                    xfs = (xf0, xf1)
                    xf_w = {}

                    def _ld01(sc):
                        kb.inc("la", sc.dma_start(out=xf0[:], in_=xT[0]), 16)
                        kb.inc("lb", sc.dma_start(out=xf1[:], in_=xT[1]), 16)
                    blk.scalar(_ld01)
                    xf_w[0] = ("la", kb.cnt["la"])
                    xf_w[1] = ("lb", kb.cnt["lb"])

                    for g in range(NLG):
                        gs = slice(g * LNG, (g + 1) * LNG)
                        xf = xfs[g % 2]

                        def _cast(sc, xf=xf, gw=xf_w[g]):
                            kb.w(sc, gw[0], gw[1])
                            kb.st(sc, "se", sc.copy(xb[:], xf[:]))
                            kb.st(sc, "se", sc.activation(xsq[:], xf[:], AF.Square))
                        blk.scalar(_cast)
                        se_cast = kb.cnt["se"]

                        def _stats(te, g=g, se_cast=se_cast):
                            kb.w(te, "se", se_cast)
                            if g == 0:
                                kb.w(te, "wg", CONST_WG)
                                kb.w(te, "sq", CONST_SQ)
                            if g > 0:
                                kb.w(te, "ve")
                            for c in range(C16):
                                i = te.matmul(psMu[:], mvec_sb[:], xb[:, c, :],
                                              start=(c == 0), stop=(c == C16 - 1))
                            kb.inc("pe", i)
                            for c in range(C16):
                                i = te.matmul(psMsq[:], mvec_sb[:], xsq[:, c, :],
                                              start=(c == 0), stop=(c == C16 - 1))
                            kb.inc("pe", i)
                        blk.tensor(_stats)
                        pe_stats = kb.cnt["pe"]

                        def _rows(v, pe_stats=pe_stats):
                            kb.w(v, "pe", pe_stats)
                            kb.st(v, "ve", v.tensor_copy(murow[:], psMu[:]))
                            kb.st(v, "ve", v.tensor_copy(msqrow[:], psMsq[:]))
                            kb.st(v, "ve", v.tensor_tensor(varrow[:], murow[:], murow[:], ALU.mult))
                            kb.st(v, "ve", v.tensor_tensor(varrow[:], msqrow[:], varrow[:], ALU.subtract))
                            kb.st(v, "ve", v.tensor_scalar_add(varrow[:], varrow[:], EPS))
                        blk.vector(_rows)
                        ve_rows = kb.cnt["ve"]

                        def _sqrt(sc, ve_rows=ve_rows):
                            kb.w(sc, "ve", ve_rows)
                            kb.st(sc, "se", sc.activation(rstd[:], varrow[:], AF.Ln))
                            kb.st(sc, "se", sc.activation(rstd[:], rstd[:], AF.Exp,
                                                          scale=-0.5))
                        blk.scalar(_sqrt)
                        se_sqrt = kb.cnt["se"]

                        def _bcast(te, se_sqrt=se_sqrt, g=g):
                            kb.w(te, "se", se_sqrt)
                            if g > 0:
                                kb.w(te, "ve")
                            kb.inc("pe", te.matmul(psMuB[:], ones_sb[:], murow[:],
                                                   start=True, stop=True))
                            kb.inc("pe", te.matmul(psRB[:], ones_sb[:], rstd[:],
                                                   start=True, stop=True))
                        blk.tensor(_bcast)
                        pe_b = kb.cnt["pe"]

                        def _norm(v, pe_b=pe_b, xf=xf):
                            kb.w(v, "pe", pe_b)
                            kb.st(v, "ve", v.tensor_tensor(xnf[:], xf[:], bc3(psMuB[:], C16),
                                                           ALU.subtract))
                            kb.st(v, "ve", v.tensor_tensor(xnf[:], xnf[:], bc3(psRB[:], C16),
                                                           ALU.mult))
                        blk.vector(_norm)
                        ve_norm = kb.cnt["ve"]

                        if g + 2 < NLG:
                            def _ld2(sc, g=g, ve_norm=ve_norm):
                                kb.w(sc, "ve", ve_norm)
                                kb.inc(("la", "lb")[g % 2],
                                       sc.dma_start(out=xfs[g % 2][:], in_=xT[g + 2]), 16)
                            blk.scalar(_ld2)
                            xf_w[g + 2] = (("la", "lb")[g % 2],
                                           kb.cnt[("la", "lb")[g % 2]])

                        def _store(sc, g=g, ve_norm=ve_norm):
                            kb.w(sc, "ve", ve_norm)
                            kb.st(sc, "se", sc.copy(xn_sb[:, :, gs], xnf[:]))
                        blk.scalar(_store)

                        def _router(te, g=g, ve_norm=ve_norm):
                            kb.w(te, "ve", ve_norm)
                            if g > 0:
                                kb.w(te, "ve")
                            for c in range(C16):
                                i = te.matmul(psL[:], rwt_sb[:, c, :], xnf[:, c, :],
                                              start=(c == 0), stop=(c == C16 - 1))
                            kb.inc("pe", i)
                        blk.tensor(_router)
                        pe_r = kb.cnt["pe"]

                        def _evl(v, g=g, pe_r=pe_r):
                            kb.w(v, "pe", pe_r)
                            kb.st(v, "ve", v.tensor_copy(logit_sb[:, gs], psL[:]))
                        blk.vector(_evl)

                if "xn" in taps or "logits" in taps:
                    txn = mktap("xn", [128, C16, TK], BF16) if "xn" in taps else None
                    tlg = mktap("logits", [NE, TK], F32) if "logits" in taps else None

                    def _tap(sy):
                        kb.w(sy, "se")
                        kb.w(sy, "ve")
                        if txn is not None:
                            kb.inc("sq", sy.dma_start(out=txn, in_=xn_sb[:]), 16)
                        if tlg is not None:
                            kb.inc("sq", sy.dma_start(out=tlg, in_=logit_sb[:]), 16)
                    blk.sync(_tap)

            blk.mega(_phaseA)

            if upto >= "B":
                blk.mega(lambda: emit_routing(kb, blk, nc, X, logit_sb, cmb_sb,
                                              eb_sb, iota_sb, iorev_sb, idf_sb,
                                              ones_sb, taps, mktap))

            if upto >= "C":
                blk.mega(lambda: emit_attention(kb, blk, nc, X, xn_sb, xn_own,
                                                wqkvg, cosT_h, sinT_h, P_sb,
                                                idb_sb, idf_sb, ones_sb, M128_h,
                                                nw_sb, rvec_sb, onesb_sb, ccin,
                                                taps, mktap))

        if upto >= "D":
            acc_sb = SB("acc_sb", [128, C16, OWN], F32)
            blk.mega(lambda: emit_cc(kb, blk, nc, ccin, ccout))
            if upto >= "E":
                blk.mega(lambda: emit_moe(kb, blk, nc, X, acc_sb, xn_own,
                                          cmb_sb, wup, wdn, ones_sb, xo, taps,
                                          mktap))
            blk.mega(lambda: emit_oproj(kb, blk, nc, X, acc_sb, xo, wo, ccin,
                                        ccout, taps, mktap,
                                        init_acc=(upto < "E")))
            if upto >= "E":
                blk.mega(lambda: emit_shared(kb, blk, nc, X, acc_sb, xn_own,
                                             sup, sdn, taps, mktap))

            def _out(sy):
                kb.w(sy, "ve")
                kb.w(sy, "se")
                kb.inc("sq", sy.dma_start(out=outT, in_=acc_sb[:]), 16)
                kb.w(sy, "sq")
            blk.mega(lambda: blk.sync(_out))
        else:
            def _fin(sy):
                kb.w(sy, "sq")
                kb.w(sy, "wx")
                kb.w(sy, "gp")
            blk.mega(lambda: blk.sync(_fin))

    return nc, tap_handles


def emit_routing(kb, blk, nc, X, logit_sb, cmb_sb, eb_sb, iota_sb, iorev_sb,
                 idf_sb, ones_sb, taps, mktap):
    E = X.enter_context
    with (
        nc.sbuf_tensor("sig_sb", [NE, TK], F32, side="right") as sig_sb,
        nc.sbuf_tensor("sbb_sb", [NE, TK], F32, side="right") as sbb_sb,
        nc.sbuf_tensor("ss_t", [128, 16], F32, side="right") as ss_t,
        nc.sbuf_tensor("sb_t", [128, 16], F32, side="right") as sb_t,
        nc.sbuf_tensor("r1", [128, 4], F32, side="right") as r1,
        nc.sbuf_tensor("r2", [128, 4], F32, side="right") as r2,
        nc.sbuf_tensor("r3", [128, 4], F32, side="right") as r3,
        nc.sbuf_tensor("r4", [128, 4], F32, side="right") as r4,
        nc.sbuf_tensor("gsum", [128, 4], F32, side="right") as gsum,
        nc.sbuf_tensor("gsec", [128, 1], F32, side="right") as gsec,
        nc.sbuf_tensor("msk", [128, 16], F32, side="right") as msk,
        nc.sbuf_tensor("oh1", [128, 16], F32, side="right") as oh1,
        nc.sbuf_tensor("oh2", [128, 16], F32, side="right") as oh2,
        nc.sbuf_tensor("mrow", [128, 1], F32, side="right") as mrow,
        nc.sbuf_tensor("w1", [128, 1], F32, side="right") as w1,
        nc.sbuf_tensor("w2", [128, 1], F32, side="right") as w2,
        nc.sbuf_tensor("cmt", [128, 16], F32, side="right") as cmt,
        nc.psum_tensor("psTr", [128, 512], F32) as psTr,
        nc.psum_tensor("psTb", [NE, 512], F32) as psTb,
    ):
        def _sig(sc):
            kb.w(sc, "ve")
            kb.w(sc, "pe")
            kb.w(sc, "wx")
            kb.st(sc, "se", sc.activation(sig_sb[:], logit_sb[:], AF.Exp,
                                          scale=-1.0))
        blk.scalar(_sig)
        se_e = kb.cnt["se"]

        def _sigv(v, se_e=se_e):
            kb.w(v, "se", se_e)
            kb.st(v, "ve", v.tensor_scalar_add(sig_sb[:], sig_sb[:], 1.0))
            kb.st(v, "ve", v.reciprocal(sig_sb[:], sig_sb[:]))
        blk.vector(_sigv)
        ve_sig = kb.cnt["ve"]

        def _sig2(sc, ve_sig=ve_sig):
            kb.w(sc, "ve", ve_sig)
            kb.st(sc, "se", sc.add(sbb_sb[:], sig_sb[:], eb_sb[:]))
        blk.scalar(_sig2)
        se_sig = kb.cnt["se"]

        for tt in range(TK // 128):
            ts = slice(tt * 128, (tt + 1) * 128)

            def _tr(te, ts=ts, se_sig=se_sig, tt=tt):
                kb.w(te, "se", se_sig)
                if tt > 0:
                    kb.w(te, "ve")
                kb.inc("pe", te.transpose(psTr[:, 0:16], sig_sb[:, ts],
                                          idf_sb[0:16, 0:16]))
                kb.inc("pe", te.transpose(psTr[:, 16:32], sbb_sb[:, ts],
                                          idf_sb[0:16, 0:16]))
            blk.tensor(_tr)
            pe_tr = kb.cnt["pe"]

            def _route(v, pe_tr=pe_tr):
                S = lambda inst: kb.st(v, "ve", inst)
                kb.w(v, "pe", pe_tr)
                S(v.tensor_copy(ss_t[:], psTr[:, 0:16]))
                S(v.tensor_copy(sb_t[:], psTr[:, 16:32]))
                a = [sb_t[:, i::4] for i in range(4)]
                S(v.tensor_max(r1[:], a[0], a[1]))
                S(v.tensor_tensor(r2[:], a[0], a[1], ALU.min))
                S(v.tensor_max(r3[:], a[2], a[3]))
                S(v.tensor_tensor(r4[:], a[2], a[3], ALU.min))
                S(v.tensor_max(gsum[:], r1[:], r3[:]))
                S(v.tensor_tensor(r1[:], r1[:], r3[:], ALU.min))
                S(v.tensor_max(r2[:], r2[:], r4[:]))
                S(v.tensor_max(r1[:], r1[:], r2[:]))
                S(v.tensor_add(gsum[:], gsum[:], r1[:]))
                b = [gsum[:, i:i + 1] for i in range(4)]
                S(v.tensor_max(r2[:, 0:1], b[0], b[1]))
                S(v.tensor_tensor(r2[:, 1:2], b[0], b[1], ALU.min))
                S(v.tensor_max(r2[:, 2:3], b[2], b[3]))
                S(v.tensor_tensor(r2[:, 3:4], b[2], b[3], ALU.min))
                S(v.tensor_tensor(r3[:, 0:1], r2[:, 0:1], r2[:, 2:3], ALU.min))
                S(v.tensor_max(r3[:, 1:2], r2[:, 1:2], r2[:, 3:4]))
                S(v.tensor_max(gsec[:], r3[:, 0:1], r3[:, 1:2]))
                S(v.tensor_tensor(r4[:], gsum[:], bc(gsec.ap(), gsum.ap()), ALU.is_ge))
                gm = bass.AP(r4, 0, [r4.ap().ap[0], [1, 4], [0, 4]])
                sb4 = bass.AP(sb_t, 0, [sb_t.ap().ap[0], [4, 4], [1, 4]])
                mk4 = bass.AP(msk, 0, [msk.ap().ap[0], [4, 4], [1, 4]])
                S(v.tensor_tensor(mk4, sb4, gm, ALU.mult))
                S(v.reduce_max(mrow[:], msk[:], axis=AX.X))
                S(v.tensor_tensor(oh1[:], msk[:], bc(mrow.ap(), msk.ap()), ALU.is_equal))
                S(v.tensor_tensor(oh1[:], oh1[:], iorev_sb[:], ALU.mult))
                S(v.reduce_max(mrow[:], oh1[:], axis=AX.X))
                S(v.tensor_scalar(mrow[:], mrow[:], -1.0, 15.0, ALU.mult, ALU.add))
                S(v.tensor_tensor(oh1[:], iota_sb[:], bc(mrow.ap(), iota_sb.ap()),
                                  ALU.is_equal))
                S(v.scalar_tensor_tensor(msk[:], oh1[:], -1e30, msk[:],
                                         ALU.mult, ALU.add))
                S(v.reduce_max(mrow[:], msk[:], axis=AX.X))
                S(v.tensor_tensor(oh2[:], msk[:], bc(mrow.ap(), msk.ap()), ALU.is_equal))
                S(v.tensor_tensor(oh2[:], oh2[:], iorev_sb[:], ALU.mult))
                S(v.reduce_max(mrow[:], oh2[:], axis=AX.X))
                S(v.tensor_scalar(mrow[:], mrow[:], -1.0, 15.0, ALU.mult, ALU.add))
                S(v.tensor_tensor(oh2[:], iota_sb[:], bc(mrow.ap(), iota_sb.ap()),
                                  ALU.is_equal))
                S(v.tensor_tensor(cmt[:], oh1[:], ss_t[:], ALU.mult))
                S(v.reduce_sum(w1[:], cmt[:], axis=AX.X))
                S(v.tensor_tensor(cmt[:], oh2[:], ss_t[:], ALU.mult))
                S(v.reduce_sum(w2[:], cmt[:], axis=AX.X))
                S(v.tensor_add(mrow[:], w1[:], w2[:]))
                S(v.tensor_scalar_add(mrow[:], mrow[:], 1e-20))
                S(v.reciprocal(mrow[:], mrow[:]))
                S(v.tensor_tensor(w1[:], w1[:], mrow[:], ALU.mult))
                S(v.tensor_tensor(w2[:], w2[:], mrow[:], ALU.mult))
                S(v.tensor_tensor(cmt[:], oh1[:], bc(w1.ap(), oh1.ap()), ALU.mult))
                S(v.scalar_tensor_tensor(oh2[:], oh2[:], 1.0, bc(w2.ap(), oh2.ap()),
                                         ALU.mult, ALU.mult))
                S(v.tensor_add(cmt[:], cmt[:], oh2[:]))
            blk.vector(_route)
            ve_rt = kb.cnt["ve"]

            def _trb(te, ts=ts, ve_rt=ve_rt):
                kb.w(te, "ve", ve_rt)
                kb.inc("pe", te.transpose(psTb[:, 0:128], cmt[:], idf_sb[:]))
            blk.tensor(_trb)
            pe_tb = kb.cnt["pe"]

            def _evc(v, ts=ts, pe_tb=pe_tb):
                kb.w(v, "pe", pe_tb)
                kb.st(v, "ve", v.tensor_copy(cmb_sb[:, ts], psTb[0:NE, 0:128]))
            blk.vector(_evc)

    if "cmb" in taps:
        t = mktap("cmb", [NE, TK], F32)

        def _tapc(sy):
            kb.w(sy, "ve")
            kb.inc("sq", sy.dma_start(out=t, in_=cmb_sb[:]), 16)
        blk.sync(_tapc)


def emit_attention(kb, blk, nc, X, xn_sb, xn_own, wqkvg, cosT_h, sinT_h,
                   P_sb, idb_sb, idf_sb, ones_sb, M128_h, nw_sb, rvec_sb,
                   onesb_sb, ccin, taps, mktap):
    NT = TK // 128
    from contextlib import ExitStack
    with ExitStack() as XA:
        cos_sb = XA.enter_context(nc.sbuf_tensor("cos_sb", [128, TK], F32, side="right"))
        sin_sb = XA.enter_context(nc.sbuf_tensor("sin_sb", [128, TK], F32, side="right"))
        M128_sb = XA.enter_context(nc.sbuf_tensor("M128_sb", [128, 128], F32, side="right"))
        wblk0 = XA.enter_context(nc.sbuf_tensor("wblk0", [128, 2, C16, 128], BF16, side="right"))
        wblk1 = XA.enter_context(nc.sbuf_tensor("wblk1", [128, 2, C16, 128], BF16, side="right"))
        qro = XA.enter_context(nc.sbuf_tensor("qro", [128, TK], BF16, side="right"))
        kro = XA.enter_context(nc.sbuf_tensor("kro", [128, TK], BF16, side="right"))
        vT = XA.enter_context(nc.sbuf_tensor("vT", [128, NT, 128], BF16, side="right"))
        gbf = XA.enter_context(nc.sbuf_tensor("gbf", [128, TK], F32, side="right"))
        stage = XA.enter_context(nc.sbuf_tensor("stage", [128, 512], BF16, side="right"))
        qsc1 = XA.enter_context(nc.sbuf_tensor("qsc1", [128, 512], F32, side="right"))
        qsc2 = XA.enter_context(nc.sbuf_tensor("qsc2", [128, 512], F32, side="right"))
        e0 = XA.enter_context(nc.sbuf_tensor("e0", [128, TK], BF16, side="right"))
        e1 = XA.enter_context(nc.sbuf_tensor("e1", [128, TK], BF16, side="right"))
        eT2 = XA.enter_context(nc.sbuf_tensor("eT2", [128, 2, NT, 128], BF16, side="right"))
        mcols = XA.enter_context(nc.sbuf_tensor("mcols", [128, 8], F32, side="right"))
        bcols = XA.enter_context(nc.sbuf_tensor("bcols", [128, 8], F32, side="right"))
        scols = XA.enter_context(nc.sbuf_tensor("scols", [128, 8], F32, side="right"))
        acols = XA.enter_context(nc.sbuf_tensor("acols", [128, 2], F32, side="right"))
        srow_sb = XA.enter_context(nc.sbuf_tensor("srow_sb", [1, 256], F32, side="right"))
        arow_sb = XA.enter_context(nc.sbuf_tensor("arow_sb", [1, 256], F32, side="right"))
        crow_sb = XA.enter_context(nc.sbuf_tensor("crow_sb", [1, 256], F32, side="right"))
        sq_sb = XA.enter_context(nc.sbuf_tensor("sq_sb", [128, 256], BF16, side="right"))
        av2_sb = XA.enter_context(nc.sbuf_tensor("av2_sb", [128, 256], F32, side="right"))
        t1_sb = XA.enter_context(nc.sbuf_tensor("t1_sb", [128, 256], F32, side="right"))
        sgt_sb = XA.enter_context(nc.sbuf_tensor("sgt_sb", [128, 256], F32, side="right"))
        gat_sb = XA.enter_context(nc.sbuf_tensor("gat_sb", [128, TK], BF16, side="right"))
        psS = XA.enter_context(nc.psum_tensor("psS", [128, 2, 1024], F32))
        psAV = XA.enter_context(nc.psum_tensor("psAV", [128, 512], F32))
        psRow = XA.enter_context(nc.psum_tensor("psRow", [128, 512], F32))
        psT0 = XA.enter_context(nc.psum_tensor("psT0", [128, 512], BF16))
        psT1 = XA.enter_context(nc.psum_tensor("psT1", [128, 512], BF16))
        psTb = (psT0, psT1)
        psAV0 = psAV[:, 0:128]
        psAV1 = psAV[:, 128:256]
        e_par = (e0, e1)
        av_ps = (psAV0, psAV1)
        av_full = psAV
        wbs = (wblk0, wblk1)
        G = {}
        now = {s: kb.cnt[s] for s in ("pe", "ve", "se", "gp", "wx", "sq")}
        for k in ("psS0", "psS1", "psTb0", "psTb1", "av", "psRa", "psRb"):
            G[k] = [("pe", now["pe"]), ("ve", now["ve"]), ("se", now["se"])]

        def need(eng, key):
            for sem, cnt in G.get(key, ()):
                kb.w(eng, sem, cnt)
            G[key] = []

        def guard(key, sem):
            G.setdefault(key, []).append((sem, kb.cnt[sem]))

        def _acl(sy):
            for s in ("pe", "ve", "se", "gp", "wx", "sq"):
                kb.w(sy, s, now[s])
            kb.inc("sq", sy.dma_start(out=cos_sb[:], in_=cosT_h), 16)
            kb.inc("sq", sy.dma_start(out=sin_sb[:], in_=sinT_h), 16)
            kb.inc("sq", sy.dma_start(out=M128_sb[:], in_=M128_h), 16)
        blk.sync(_acl)
        ACL = kb.cnt["sq"]

        def _wl0(gp):
            for s in ("pe", "ve", "se", "sq", "wx"):
                kb.w(gp, s, now[s])
            kb.inc("wa", gp.dma_start(out=wblk0[:], in_=wqkvg[0][:, 0:2]), 16)
            kb.inc("wb", gp.dma_start(out=wblk1[:], in_=wqkvg[0][:, 2:4]), 16)
        blk.gpsimd(_wl0)

        for g in range(NHL):

            for mat, name in ((0, "q"), (1, "k"), (2, "v"), (3, "g")):
                for i in range(4):
                    seg = slice(i * 512, (i + 1) * 512)
                    s = i % 2
                    pp = psS[:, s, 0:512]
                    pr = psS[:, s, 512:1024]
                    skey = f"psS{s}"

                    wb = wbs[mat // 2]

                    def _mm(te, mat=mat, seg=seg, pp=pp, skey=skey, g=g, wb=wb,
                            i=i):
                        if i == 0:
                            kb.w(te, "wa")
                            kb.w(te, "wb")
                        need(te, skey)
                        for c in range(C16):
                            inst = te.matmul(pp, wb[:, mat % 2, c, :],
                                             xn_sb[:, c, seg],
                                             start=(c == 0), stop=(c == C16 - 1))
                        kb.inc("pe", inst)
                    blk.tensor(_mm)
                    pe_mm = kb.cnt["pe"]

                    if name in ("q", "k"):
                        def _stg(sc, pp=pp, pe_mm=pe_mm):
                            kb.w(sc, "pe", pe_mm)
                            need(sc, "stage")
                            kb.st(sc, "se", sc.copy(stage[:], pp))
                        blk.scalar(_stg)
                        se_stage = kb.cnt["se"]

                        def _rot(te, pr=pr, se_stage=se_stage):
                            kb.w(te, "se", se_stage)
                            kb.inc("pe", te.matmul(pr, P_sb[:], stage[:],
                                                   start=True, stop=True))
                            guard("stage", "pe")
                        blk.tensor(_rot)
                        pe_rot = kb.cnt["pe"]

                        def _mul(v, pp=pp, pr=pr, seg=seg, pe_rot=pe_rot,
                                 skey=skey, g=g, i=i):
                            if g == 0 and i == 0:
                                kb.w(v, "sq", ACL)
                            kb.w(v, "pe", pe_rot)
                            need(v, "qsc")
                            kb.st(v, "ve", v.tensor_tensor(qsc1[:], pp,
                                                           cos_sb[:, seg], ALU.mult))
                            kb.st(v, "ve", v.tensor_tensor(qsc2[:], pr,
                                                           sin_sb[:, seg], ALU.mult))
                            guard(skey, "ve")
                        blk.vector(_mul)
                        ve_mul = kb.cnt["ve"]

                        dst, dkey = (qro, "qro") if name == "q" else (kro, "kro")

                        def _add(gp, dst=dst, dkey=dkey, seg=seg, ve_mul=ve_mul, i=i):
                            kb.w(gp, "ve", ve_mul)
                            if i == 0:
                                need(gp, dkey)
                            kb.st(gp, "gp", gp.tensor_add(dst[:, seg], qsc1[:], qsc2[:]))
                            guard("qsc", "gp")
                        blk.gpsimd(_add)
                        if name == "k" and i == 3 and g + 1 < NHL:
                            guard("wblk0", "pe")

                            def _wlq(gp, g=g):
                                need(gp, "wblk0")
                                kb.inc("wa", gp.dma_start(
                                    out=wblk0[:], in_=wqkvg[g + 1][:, 0:2]), 16)
                            blk.gpsimd(_wlq)
                    elif name == "v":
                        def _stg(sc, pp=pp, pe_mm=pe_mm, skey=skey):
                            kb.w(sc, "pe", pe_mm)
                            need(sc, "stage")
                            kb.st(sc, "se", sc.copy(stage[:], pp))
                            guard(skey, "se")
                        blk.scalar(_stg)
                        se_stage = kb.cnt["se"]

                        for j in range(4):
                            tc = i * 4 + j
                            pb = psTb[(j // 2) % 2]
                            po = (j % 2) * 128

                            def _tp(te, j=j, se_stage=se_stage, pb=pb, po=po):
                                if j % 2 == 0:
                                    need(te, f"psTb{(j // 2) % 2}")
                                if j == 0:
                                    kb.w(te, "se", se_stage)
                                kb.inc("pe", te.transpose(
                                    pb[:, po:po + 128],
                                    stage[:, j * 128:(j + 1) * 128], idb_sb[:]))
                                if j == 3:
                                    guard("stage", "pe")
                            blk.tensor(_tp)
                            pe_tp = kb.cnt["pe"]

                            if j % 2 == 1:
                                def _ev(v, j=j, tc=tc, pe_tp=pe_tp, pb=pb):
                                    kb.w(v, "pe", pe_tp)
                                    if tc <= 1:
                                        need(v, "vT")
                                    kb.st(v, "ve", v.tensor_copy(
                                        vT[:, tc - 1:tc + 1, :], pb[:, 0:256]))
                                    guard(f"psTb{(j // 2) % 2}", "ve")
                                blk.vector(_ev)
                    else:
                        def _stg(sc, pp=pp, seg=seg, pe_mm=pe_mm, skey=skey, i=i):
                            kb.w(sc, "pe", pe_mm)
                            if i == 0:
                                need(sc, "gbf")
                            kb.st(sc, "se", sc.copy(gbf[:, seg], pp))
                            guard(skey, "se")
                        blk.scalar(_stg)
                        if i == 3 and g + 1 < NHL:
                            guard("wblk1", "pe")

                            def _wlv(gp, g=g):
                                need(gp, "wblk1")
                                kb.inc("wb", gp.dma_start(
                                    out=wblk1[:], in_=wqkvg[g + 1][:, 2:4]), 16)
                            blk.gpsimd(_wlv)


            cc_chunk = 0
            for b2 in range(NT // 2):
                q0 = 2 * b2
                tile_info = []
                for qj in (0, 1):
                    gq = q0 + qj
                    par = gq % 2
                    ebuf = e_par[par]
                    qseg = slice(gq * 128, (gq + 1) * 128)
                    nkt = gq + 1
                    two = nkt > 8
                    chunks = [(0, min(nkt, 8))] + ([(8, nkt)] if two else [])
                    mb4 = 4 * qj
                    mA = mcols[:, mb4:mb4 + 1]
                    mB = mcols[:, mb4 + 1:mb4 + 2]
                    mfin = mcols[:, mb4 + 2:mb4 + 3]
                    bA = bcols[:, mb4:mb4 + 1]
                    bfin = bcols[:, mb4 + 2:mb4 + 3]
                    sA = scols[:, mb4:mb4 + 1]
                    sB = scols[:, mb4 + 1:mb4 + 2]
                    sfin = scols[:, mb4 + 2:mb4 + 3]
                    acol = acols[:, qj:qj + 1]

                    for ci, (k0, k1) in enumerate(chunks):
                        w = (k1 - k0) * 128
                        slot = cc_chunk % 2
                        cc_chunk += 1
                        mc = (mA, mB)[ci]

                        def _smm(te, k0=k0, slot=slot, qseg=qseg, w=w,
                                 firstb=(qj == 0 and ci == 0)):
                            if firstb:
                                kb.w(te, "gp")
                            need(te, f"psS{slot}")
                            off = 0
                            while off < w:
                                ww = min(512, w - off)
                                inst = te.matmul(
                                    psS[:, slot, off:off + ww], qro[:, qseg],
                                    kro[:, k0 * 128 + off:k0 * 128 + off + ww],
                                    start=True, stop=True)
                                off += ww
                            kb.inc("pe", inst)
                        blk.tensor(_smm)
                        pe_s = kb.cnt["pe"]
                        diag = (k1 - 1 == gq)

                        def _msk(v, slot=slot, w=w, pe_s=pe_s, diag=diag, mc=mc,
                                 first=(g == 0 and gq == 0)):
                            if first:
                                kb.w(v, "sq", ACL)
                            kb.w(v, "pe", pe_s)
                            if diag:
                                kb.st(v, "ve", v.tensor_add(
                                    psS[:, slot, w - 128:w],
                                    psS[:, slot, w - 128:w], M128_sb[:]))
                            kb.st(v, "ve", v.reduce_max(mc, psS[:, slot, 0:w],
                                                        axis=AX.X))
                        blk.vector(_msk)

                        if ci == 0:
                            def _bA(v, bA=bA, mA=mA):
                                kb.st(v, "ve", v.tensor_scalar_mul(bA, mA, -SCALE))
                            blk.vector(_bA)
                            ve_b = kb.cnt["ve"]
                            bias_use = bA
                        else:
                            def _mrg1(v, mA=mA, mB=mB, mfin=mfin, bfin=bfin):
                                kb.w(v, "pe")
                                kb.st(v, "ve", v.tensor_max(mfin, mA, mB))
                                kb.st(v, "ve", v.tensor_scalar_mul(bfin, mfin, -SCALE))
                            blk.vector(_mrg1)
                            ve_b = kb.cnt["ve"]
                            bias_use = bfin

                        def _exp(sc, slot=slot, w=w, k0=k0, ve_b=ve_b, ebuf=ebuf,
                                 bias_use=bias_use, scol=(sA, sB)[ci], par=par):
                            kb.w(sc, "ve", ve_b)
                            if k0 == 0:
                                need(sc, f"e{par}")
                            kb.st(sc, "se", sc.activation(
                                ebuf[:, k0 * 128:k0 * 128 + w], psS[:, slot, 0:w],
                                AF.Exp, bias=bias_use, scale=SCALE, accum_out=scol))
                            guard(f"psS{slot}", "se")
                        blk.scalar(_exp)
                    se_exp = kb.cnt["se"]

                    if two:
                        def _alp(sc, mA=mA, bfin=bfin, acol=acol):
                            kb.st(sc, "se", sc.activation(acol, mA, AF.Exp,
                                                          bias=bfin, scale=SCALE))
                        blk.scalar(_alp)
                        se_a = kb.cnt["se"]

                        def _mrg2(v, ebuf=ebuf, acol=acol, sA=sA, sB=sB,
                                  sfin=sfin, se_a=se_a):
                            kb.w(v, "se", se_a)
                            kb.st(v, "ve", v.tensor_scalar_mul(
                                ebuf[:, 0:1024], ebuf[:, 0:1024], acol))
                            kb.st(v, "ve", v.tensor_tensor(sfin, sA, acol, ALU.mult))
                            kb.st(v, "ve", v.tensor_add(sfin, sfin, sB))
                        blk.vector(_mrg2)
                    else:
                        def _mrg0(v, sA=sA, sfin=sfin, se_exp=se_exp):
                            kb.w(v, "se", se_exp)
                            kb.w(v, "pe")
                            kb.st(v, "ve", v.tensor_copy(sfin, sA))
                        blk.vector(_mrg0)
                    tile_info.append((qj, par, nkt, kb.cnt["ve"], kb.cnt["se"]))

                for (qj, par, nkt, ve_esc, se_tile) in tile_info:
                    ebuf = e_par[par]
                    for kt in range(nkt):
                        bkn = (kt // 2) % 2
                        pb = psTb[bkn]
                        po = (kt % 2) * 128

                        def _tp(te, kt=kt, ebuf=ebuf, ve_esc=ve_esc,
                                se_tile=se_tile, pb=pb, po=po, bkn=bkn):
                            if kt % 2 == 0:
                                need(te, f"psTb{bkn}")
                            if kt == 0:
                                kb.w(te, "se", se_tile)
                                kb.w(te, "ve", ve_esc)
                            kb.inc("pe", te.transpose(
                                pb[:, po:po + 128],
                                ebuf[:, kt * 128:(kt + 1) * 128], idb_sb[:]))
                        blk.tensor(_tp)
                        pe_tp = kb.cnt["pe"]

                        if kt % 2 == 1 or kt == nkt - 1:
                            k0e = kt - (kt % 2)
                            wid = (kt % 2) + 1

                            def _ev(v, kt=kt, qj=qj, pe_tp=pe_tp, pb=pb,
                                    k0e=k0e, wid=wid, bkn=bkn):
                                kb.w(v, "pe", pe_tp)
                                if k0e == 0:
                                    need(v, f"eT{qj}")
                                kb.st(v, "ve", v.tensor_copy(
                                    eT2[:, qj, k0e:k0e + wid, :],
                                    pb[:, 0:wid * 128]))
                                guard(f"psTb{bkn}", "ve")
                            blk.vector(_ev)
                    guard(f"e{par}", "pe")

                def _z(v, q0=q0):
                    kb.st(v, "ve", v.memset(eT2[:, 0, q0 + 1, :], 0))
                blk.vector(_z)
                ve_eT = kb.cnt["ve"]
                lim1 = q0 + 2

                def _av(te, lim1=lim1, ve_eT=ve_eT):
                    kb.w(te, "ve", ve_eT)
                    need(te, "av")
                    for kt in range(lim1):
                        inst = te.matmul(av_full[:, 0:256], vT[:, kt, :],
                                         eT2[:, 0:2, kt, :],
                                         start=(kt == 0),
                                         stop=(kt == lim1 - 1))
                    kb.inc("pe", inst)
                    guard("eT0", "pe")
                    guard("eT1", "pe")
                blk.tensor(_av)
                pe_av = kb.cnt["pe"]

                bseg = slice(q0 * 128, (q0 + 2) * 128)
                sfin0 = scols[:, 2:3]
                sfin1 = scols[:, 6:7]
                s2ps = psRow[0:1, 256:512]
                sr0 = psRow[0:1, 0:128]
                sr1 = psRow[0:1, 128:256]
                cbps = psRow[:, 0:256]

                def _gsq(v, pe_av=pe_av, bseg=bseg):
                    kb.w(v, "pe", pe_av)
                    need(v, "av")
                    kb.st(v, "ve", v.tensor_copy(av2_sb[:], av_full[:, 0:256]))
                    guard("av", "ve")
                    kb.st(v, "ve", v.tensor_tensor(sq_sb[:], av2_sb[:], av2_sb[:],
                                                   ALU.mult))
                blk.vector(_gsq)
                ve_gsq = kb.cnt["ve"]

                def _gs2(te, ve_gsq=ve_gsq, sfin0=sfin0, sfin1=sfin1):
                    kb.w(te, "ve", ve_gsq)
                    need(te, "psRa")
                    kb.inc("pe", te.matmul(s2ps, rvec_sb[:], sq_sb[:],
                                           start=True, stop=True))
                    kb.inc("pe", te.transpose(sr0, sfin0, idf_sb[:]))
                    kb.inc("pe", te.transpose(sr1, sfin1, idf_sb[:]))
                blk.tensor(_gs2)
                pe_s2 = kb.cnt["pe"]

                def _grow(v, pe_s2=pe_s2):
                    kb.w(v, "pe", pe_s2)
                    kb.st(v, "ve", v.tensor_copy(srow_sb[:], psRow[0:1, 0:256]))
                    kb.st(v, "ve", v.tensor_tensor(arow_sb[:], srow_sb[:],
                                                   srow_sb[:], ALU.mult))
                    kb.st(v, "ve", v.scalar_tensor_tensor(arow_sb[:], arow_sb[:],
                                                          EPS, s2ps, ALU.mult,
                                                          ALU.add))
                    guard("psRa", "ve")
                blk.vector(_grow)
                ve_row = kb.cnt["ve"]

                def _gln(sc, ve_row=ve_row):
                    kb.w(sc, "ve", ve_row)
                    kb.st(sc, "se", sc.activation(crow_sb[:], arow_sb[:], AF.Ln))
                    kb.st(sc, "se", sc.activation(crow_sb[:], crow_sb[:], AF.Exp,
                                                  scale=-0.5))
                    kb.st(sc, "se", sc.activation(sgt_sb[:], gbf[:, bseg], AF.Exp,
                                                  scale=-1.0))
                blk.scalar(_gln)
                se_ln = kb.cnt["se"]

                def _gcb(te, se_ln=se_ln):
                    kb.w(te, "se", se_ln)
                    need(te, "psRb")
                    kb.inc("pe", te.matmul(psRow[:, 0:128], ones_sb[:],
                                           crow_sb[:, 0:128], start=True, stop=True))
                    kb.inc("pe", te.matmul(psRow[:, 128:256], ones_sb[:],
                                           crow_sb[:, 128:256], start=True, stop=True))
                blk.tensor(_gcb)
                pe_cb = kb.cnt["pe"]

                def _gt(v, pe_cb=pe_cb, bseg=bseg,
                        first=(b2 == 0)):
                    kb.w(v, "pe", pe_cb)
                    if first:
                        need(v, "gat")
                    kb.st(v, "ve", v.tensor_scalar_add(sgt_sb[:], sgt_sb[:], 1.0))
                    kb.st(v, "ve", v.reciprocal(sgt_sb[:], sgt_sb[:]))
                    kb.st(v, "ve", v.tensor_tensor(sgt_sb[:], sgt_sb[:],
                                                   gbf[:, bseg], ALU.mult))
                    kb.st(v, "ve", v.tensor_scalar_mul(t1_sb[:], av2_sb[:],
                                                       nw_sb[:]))
                    kb.st(v, "ve", v.tensor_tensor(t1_sb[:], t1_sb[:], cbps,
                                                   ALU.mult))
                    guard("psRb", "ve")
                    kb.st(v, "ve", v.tensor_tensor(gat_sb[:, bseg], t1_sb[:],
                                                   sgt_sb[:], ALU.mult))
                blk.vector(_gt)
            guard("qro", "pe")
            guard("kro", "pe")
            guard("vT", "pe")
            guard("gbf", "ve")

            def _st(sy, g=g):
                kb.w(sy, "ve")
                kb.inc("sq", sy.dma_start(out=ccin[g], in_=gat_sb[:]), 16)
                guard("gat", "sq")
            blk.sync(_st)

        if "gated" in taps:
            t = mktap("gated", [NHL, 128, TK], BF16)

            def _tapg(sy):
                kb.w(sy, "sq")
                kb.inc("sq", sy.dma_start(out=t, in_=ccin.ap()), 16)
            blk.sync(_tapg)

    def _own(sc):
        kb.w(sc, "se")
        kb.w(sc, "ve")
        h = kb.sem["wx"]
        par = kb.par_reg
        if True:
            with sc.If_eq(par, 0):
                sc.dma_start(out=xn_own[:], in_=xn_sb[:, :, 0:OWN]).then_inc(h, 16)
            with sc.Else():
                sc.dma_start(out=xn_own[:], in_=xn_sb[:, :, OWN:TK]).then_inc(h, 16)
        kb.cnt["wx"] += 16
    blk.scalar(_own)


def emit_cc(kb, blk, nc, ccin, ccout):
    now = {s: kb.cnt[s] for s in ("pe", "ve", "se")}

    def _cc(gp):
        kb.w(gp, "sq")
        for s in ("pe", "ve", "se"):
            kb.w(gp, s, now[s])
        kb.inc("cc", gp.collective_compute(
            "AllGather", ALU.bypass,
            replica_groups=[[0, 1], [2, 3], [4, 5], [6, 7]],
            ins=[ccin.ap().rearrange("h p t -> (h p) t").opt()],
            outs=[ccout.ap().rearrange("h p t -> (h p) t").opt()]))
    blk.gpsimd(_cc)


def emit_oproj(kb, blk, nc, X, acc_sb, xo, wo, ccin, ccout, taps, mktap,
               init_acc=False):
    from contextlib import ExitStack
    with ExitStack() as XO:
        gf = XO.enter_context(nc.sbuf_tensor("gf", [128, C16, OWN], BF16, side="right"))
        wob0 = XO.enter_context(nc.sbuf_tensor("wob0", [128, C16, 512], BF16, side="right"))
        wob1 = XO.enter_context(nc.sbuf_tensor("wob1", [128, C16, 512], BF16, side="right"))
        psO = XO.enter_context(nc.psum_tensor("psO", [128, 2, 512], F32))
        wobs = (wob0, wob1)
        now = {s: kb.cnt[s] for s in ("pe", "ve", "se", "gp", "wx", "sq")}
        guards = {k: [(s, kb.cnt[s]) for s in ("pe", "ve", "se", "wx", "sq")]
                  for k in ("psO0", "psO1", "wob0", "wob1")}

        def _gfl(sc):
            kb.w(sc, "cc")
            for s in ("pe", "ve", "se", "gp", "sq", "wx"):
                kb.w(sc, s, now[s])
            h = kb.sem["wx"]
            par = kb.par_reg
            with sc.If_eq(par, 0):
                sc.dma_start(out=gf[:], in_=ccout.ap()[:, :, 0:OWN]
                             .rearrange("h p t -> p h t")).then_inc(h, 16)
            with sc.Else():
                sc.dma_start(out=gf[:], in_=ccout.ap()[:, :, OWN:TK]
                             .rearrange("h p t -> p h t")).then_inc(h, 16)
            kb.cnt["wx"] += 16
        blk.scalar(_gfl)
        wx_gf = kb.cnt["wx"]

        if init_acc:
            def _acc0(sy):
                for s in ("pe", "ve", "se", "gp", "sq", "wx"):
                    kb.w(sy, s, now[s])
                kb.inc("sq", sy.dma_start(out=acc_sb[:, :, 0:512], in_=xo[0]), 16)
                kb.inc("sq", sy.dma_start(out=acc_sb[:, :, 512:1024], in_=xo[1]), 16)
            blk.sync(_acc0)
        sq_acc = kb.cnt["sq"]

        for j in range(4):
            wob = wobs[j % 2]

            def _wol(gp, j=j, first=(j == 0)):
                if first:
                    for s in ("pe", "ve", "se", "sq", "wx"):
                        kb.w(gp, s, now[s])
                for sem, cnt in guards[f"wob{j % 2}"]:
                    kb.w(gp, sem, cnt)
                guards[f"wob{j % 2}"] = []
                kb.inc(("wa", "wb")[j % 2], gp.dma_start(out=wobs[j % 2][:], in_=wo[:, j]), 16)
            blk.gpsimd(_wol)
            for mc4 in range(4):
                m = j * 4 + mc4
                for ts in range(2):
                    sl = (m * 2 + ts) % 2
                    tseg = slice(ts * 512, (ts + 1) * 512)

                    def _omm(te, wob=wob, mc4=mc4, tseg=tseg, sl=sl, m=m, ts=ts):
                        if mc4 == 0 and ts == 0:
                            kb.w(te, "wa")
                            kb.w(te, "wb")
                        if m == 0 and ts == 0:
                            kb.w(te, "wx", wx_gf)
                        for sem, cnt in guards[f"psO{sl}"]:
                            kb.w(te, sem, cnt)
                        guards[f"psO{sl}"] = []
                        for c in range(C16):
                            inst = te.matmul(psO[:, sl, :],
                                             wob[:, c, mc4 * 128:(mc4 + 1) * 128],
                                             gf[:, c, tseg],
                                             start=(c == 0), stop=(c == C16 - 1))
                        kb.inc("pe", inst)
                    blk.tensor(_omm)
                    pe_o = kb.cnt["pe"]

                    def _oacc(v, m=m, tseg=tseg, sl=sl, pe_o=pe_o, first=(j == 0 and mc4 == 0 and ts == 0)):
                        kb.w(v, "pe", pe_o)
                        if first:
                            kb.w(v, "sq", sq_acc)
                        kb.st(v, "ve", v.tensor_tensor(acc_sb[:, m, tseg],
                                                       acc_sb[:, m, tseg],
                                                       psO[:, sl, :], ALU.add))
                        guards[f"psO{sl}"].append(("ve", kb.cnt["ve"]))
                    blk.vector(_oacc)
            guards[f"wob{j % 2}"].append(("pe", kb.cnt["pe"]))

    if "acc_o" in taps:
        t = mktap("acc_o", [128, C16, OWN], F32)

        def _tapo(sy):
            kb.w(sy, "ve")
            kb.inc("sq", sy.dma_start(out=t, in_=acc_sb[:]), 16)
        blk.sync(_tapo)


def emit_moe(kb, blk, nc, X, acc_sb, xn_own, cmb_sb, wup, wdn, ones_sb,
             xo, taps, mktap):
    from contextlib import ExitStack
    with ExitStack() as XM:
        a_blk = XM.enter_context(nc.sbuf_tensor("a_blk", [128, 2, 8, 1024], BF16, side="right"))
        wupb0 = XM.enter_context(nc.sbuf_tensor("wupb0", [128, C16, 256], BF16, side="right"))
        wupb1 = XM.enter_context(nc.sbuf_tensor("wupb1", [128, C16, 256], BF16, side="right"))
        wdnb = XM.enter_context(nc.sbuf_tensor("wdnb", [128, 4, 8, 512], BF16, side="right"))
        cbf = XM.enter_context(nc.sbuf_tensor("cbf", [128, 2, 2, 1024], BF16, side="right"))
        crow = XM.enter_context(nc.sbuf_tensor("crow", [1, 1024], F32, side="right"))
        trelu = XM.enter_context(nc.sbuf_tensor("trelu", [128, 1024], F32, side="right"))
        tsq = XM.enter_context(nc.sbuf_tensor("tsq", [128, 1024], BF16, side="right"))
        psH = XM.enter_context(nc.psum_tensor("psH", [128, 2, 1024], F32))
        psR = XM.enter_context(nc.psum_tensor("psR", [128, 2, 1024], F32))
        wupbs = (wupb0, wupb1)
        now = {s: kb.cnt[s] for s in ("pe", "ve", "se", "gp", "wx", "sq")}
        G = {}

        def need(eng, key):
            for sem, cnt in G.get(key, ()):
                kb.w(eng, sem, cnt)
            G[key] = []

        def guard(key, sem):
            G.setdefault(key, []).append((sem, kb.cnt[sem]))

        for k in ("psH0", "psH1", "psR0", "psR1", "wdn0", "wdn1", "wdn2", "wdn3"):
            G[k] = [(s, now[s]) for s in ("pe", "ve", "se", "wx", "sq")]

        def _acc0(sy):
            for s in ("pe", "ve", "se", "gp", "sq", "wx"):
                kb.w(sy, s, now[s])
            kb.inc("sq", sy.dma_start(out=acc_sb[:, :, 0:512], in_=xo[0]), 16)
            kb.inc("sq", sy.dma_start(out=acc_sb[:, :, 512:1024], in_=xo[1]), 16)
        blk.sync(_acc0)
        sq_acc = kb.cnt["sq"]

        def prep_cb(eb):
            sl2 = eb % 2
            e0p = 2 * eb
            for ep in range(2):
                e = e0p + ep

                def _crow(sc, e=e, first=(eb == 0 and ep == 0)):
                    if first:
                        for s in ("pe", "ve", "se", "gp", "sq", "wx"):
                            kb.w(sc, s, now[s])
                    kb.w(sc, "pe")
                    h = kb.sem["wx"]
                    par = kb.par_reg
                    if True:
                        with sc.If_eq(par, 0):
                            sc.dma_start(out=crow[:], in_=cmb_sb[e:e + 1, 0:OWN]).then_inc(h, 16)
                        with sc.Else():
                            sc.dma_start(out=crow[:], in_=cmb_sb[e:e + 1, OWN:TK]).then_inc(h, 16)
                    kb.cnt["wx"] += 16
                blk.scalar(_crow)
                wx_cr = kb.cnt["wx"]

                def _cbm(te, ep=ep, wx_cr=wx_cr):
                    kb.w(te, "wx", wx_cr)
                    need(te, "psH0")
                    kb.inc("pe", te.matmul(psH[:, 0, 0:512], ones_sb[:],
                                           crow[0:1, 0:512], start=True, stop=True))
                    kb.inc("pe", te.matmul(psH[:, 0, 512:1024], ones_sb[:],
                                           crow[0:1, 512:1024], start=True, stop=True))
                blk.tensor(_cbm)
                pe_cb = kb.cnt["pe"]

                def _cbe(v, ep=ep, pe_cb=pe_cb, sl2=sl2):
                    kb.w(v, "pe", pe_cb)
                    need(v, f"cbf{sl2}{ep}")
                    kb.st(v, "ve", v.tensor_copy(cbf[:, sl2, ep, :], psH[:, 0, :]))
                    guard("psH0", "ve")
                blk.vector(_cbe)

        prep_cb(0)
        for eb in range(8):
            e0 = 2 * eb
            sl2 = eb % 2

            def _wdl_pair(mq):
                for ep in range(2):
                    slw = (mq % 2) * 2 + ep

                    def _wdl(gp, e=e0 + ep, mq=mq, slw=slw):
                        need(gp, f"wdn{slw}")
                        kb.inc(("wc", "wd", "we", "wf")[slw], gp.dma_start(
                            out=wdnb[:, slw], in_=wdn[e, mq]), 16)
                    blk.gpsimd(_wdl)
            _wdl_pair(0)
            _wdl_pair(1)
            if eb + 1 < 8:
                prep_cb(eb + 1)

            for ep in range(2):
                e = e0 + ep
                for qh in range(4):
                    wb = wupbs[qh % 2]

                    def _wul(gp, e=e, qh=qh, first=(eb == 0 and ep == 0 and qh == 0)):
                        if first:
                            for s in ("pe", "ve", "se", "sq", "wx"):
                                kb.w(gp, s, now[s])
                        need(gp, f"wup{qh % 2}")
                        kb.inc(("wa", "wb")[qh % 2], gp.dma_start(out=wupbs[qh % 2][:], in_=wup[e, qh]), 16)
                    blk.gpsimd(_wul)

                    for i2 in range(2):
                        ic = qh * 2 + i2
                        sl = ic % 2

                        def _umm(te, wb=wb, i2=i2, sl=sl):
                            if i2 == 0:
                                kb.w(te, "wa")
                                kb.w(te, "wb")
                            need(te, f"psH{sl}")
                            for ts in range(2):
                                for c in range(C16):
                                    inst = te.matmul(
                                        psH[:, sl, ts * 512:(ts + 1) * 512],
                                        wb[:, c, i2 * 128:(i2 + 1) * 128],
                                        xn_own[:, c, ts * 512:(ts + 1) * 512],
                                        start=(c == 0), stop=(c == C16 - 1))
                            kb.inc("pe", inst)
                        blk.tensor(_umm)
                        pe_u = kb.cnt["pe"]

                        def _rlv(v, sl=sl, pe_u=pe_u):
                            kb.w(v, "pe", pe_u)
                            need(v, "trelu")
                            kb.st(v, "ve", v.tensor_scalar_max(trelu[:],
                                                               psH[:, sl, :], 0.0))
                            guard(f"psH{sl}", "ve")
                        blk.vector(_rlv)
                        ve_rl = kb.cnt["ve"]

                        def _rl(sc, ve_rl=ve_rl):
                            kb.w(sc, "ve", ve_rl)
                            need(sc, "tsq")
                            kb.st(sc, "se", sc.activation(tsq[:], trelu[:], AF.Square))
                            guard("trelu", "se")
                        blk.scalar(_rl)
                        se_rl = kb.cnt["se"]

                        def _am(v, ep=ep, ic=ic, se_rl=se_rl,
                                first=(ep == 0 and qh == 0 and i2 == 0)):
                            kb.w(v, "se", se_rl)
                            if first:
                                need(v, "ablk")
                            kb.st(v, "ve", v.tensor_tensor(a_blk[:, ep, ic, :],
                                                           tsq[:], cbf[:, sl2, ep, :],
                                                           ALU.mult))
                            guard("tsq", "ve")
                            if ic == 7:
                                guard(f"cbf{sl2}{ep}", "ve")
                        blk.vector(_am)
                    guard(f"wup{qh % 2}", "pe")

            for mq in range(4):
                for m4 in range(4):
                    m = mq * 4 + m4
                    sl = m % 2

                    def _dmm(te, mq=mq, m4=m4, sl=sl, ve_a=kb.cnt["ve"]):
                        if m4 == 0:
                            for s in ("wc", "wd", "we", "wf"):
                                kb.w(te, s)
                            kb.w(te, "ve", ve_a)
                        need(te, f"psR{sl}")
                        for ts in range(2):
                            for ep in range(2):
                                slw = (mq % 2) * 2 + ep
                                for ic in range(8):
                                    inst = te.matmul(
                                        psR[:, sl, ts * 512:(ts + 1) * 512],
                                        wdnb[:, slw, ic, m4 * 128:(m4 + 1) * 128],
                                        a_blk[:, ep, ic, ts * 512:(ts + 1) * 512],
                                        start=(ep == 0 and ic == 0),
                                        stop=(ep == 1 and ic == 7))
                        kb.inc("pe", inst)
                        if m4 == 3:
                            for ep in range(2):
                                guard(f"wdn{(mq % 2) * 2 + ep}", "pe")
                    blk.tensor(_dmm)
                    pe_d = kb.cnt["pe"]

                    def _dacc(v, m=m, sl=sl, pe_d=pe_d,
                              first=(eb == 0 and mq == 0 and m4 == 0)):
                        kb.w(v, "pe", pe_d)
                        if first:
                            kb.w(v, "sq", sq_acc)
                        kb.st(v, "ve", v.tensor_tensor(acc_sb[:, m, :],
                                                       acc_sb[:, m, :],
                                                       psR[:, sl, :], ALU.add))
                        guard(f"psR{sl}", "ve")
                    blk.vector(_dacc)
                if mq + 2 < 4:
                    _wdl_pair(mq + 2)
            guard("ablk", "pe")
            G.setdefault("ablk_w", [])

    if "acc_m" in taps:
        t = mktap("acc_m", [128, C16, OWN], F32)

        def _tapm(sy):
            kb.w(sy, "ve")
            kb.inc("sq", sy.dma_start(out=t, in_=acc_sb[:]), 16)
        blk.sync(_tapm)


def emit_shared(kb, blk, nc, X, acc_sb, xn_own, sup, sdn, taps, mktap):
    from contextlib import ExitStack
    with ExitStack() as XS:
        hs = XS.enter_context(nc.sbuf_tensor("hs", [128, 8, 1024], BF16, side="right"))
        supb0 = XS.enter_context(nc.sbuf_tensor("supb0", [128, C16, 512], BF16, side="right"))
        supb1 = XS.enter_context(nc.sbuf_tensor("supb1", [128, C16, 512], BF16, side="right"))
        sdnb0 = XS.enter_context(nc.sbuf_tensor("sdnb0", [128, 8, 512], BF16, side="right"))
        sdnb1 = XS.enter_context(nc.sbuf_tensor("sdnb1", [128, 8, 512], BF16, side="right"))
        strl = XS.enter_context(nc.sbuf_tensor("strl", [128, 1024], F32, side="right"))
        stsq = XS.enter_context(nc.sbuf_tensor("stsq", [128, 1024], F32, side="right"))
        psH = XS.enter_context(nc.psum_tensor("psHs", [128, 2, 1024], F32))
        psR = XS.enter_context(nc.psum_tensor("psRs", [128, 2, 1024], F32))
        supbs = (supb0, supb1)
        sdnbs = (sdnb0, sdnb1)
        now = {s: kb.cnt[s] for s in ("pe", "ve", "se", "gp", "wx", "sq")}
        G = {}

        def need(eng, key):
            for sem, cnt in G.get(key, ()):
                kb.w(eng, sem, cnt)
            G[key] = []

        def guard(key, sem):
            G.setdefault(key, []).append((sem, kb.cnt[sem]))

        for k in ("psH0", "psH1", "psR0", "psR1", "sup0", "sup1", "sdn0", "sdn1"):
            G[k] = [(s, now[s]) for s in ("pe", "ve", "se", "wx", "sq")]

        for ib in range(4):
            for jh in range(2):
                def _sul(gp, ib=ib, jh=jh, first=(ib == 0 and jh == 0)):
                    if first:
                        for s in ("pe", "ve", "se", "sq", "wx"):
                            kb.w(gp, s, now[s])
                    need(gp, f"sup{jh}")
                    kb.inc(("wa", "wb")[jh], gp.dma_start(out=supbs[jh][:], in_=sup[ib * 2 + jh]), 16)
                blk.gpsimd(_sul)

            for jh in range(2):
                for i2 in range(4):
                    ic = jh * 4 + i2
                    sl = ic % 2

                    def _smm(te, jh=jh, i2=i2, sl=sl):
                        if i2 == 0:
                            kb.w(te, "wa")
                            kb.w(te, "wb")
                        need(te, f"psH{sl}")
                        for ts in range(2):
                            for c in range(C16):
                                inst = te.matmul(
                                    psH[:, sl, ts * 512:(ts + 1) * 512],
                                    supbs[jh][:, c, i2 * 128:(i2 + 1) * 128],
                                    xn_own[:, c, ts * 512:(ts + 1) * 512],
                                    start=(c == 0), stop=(c == C16 - 1))
                        kb.inc("pe", inst)
                        if i2 == 3:
                            guard(f"sup{jh}", "pe")
                    blk.tensor(_smm)
                    pe_u = kb.cnt["pe"]

                    def _rlv(v, sl=sl, pe_u=pe_u):
                        kb.w(v, "pe", pe_u)
                        need(v, "strl")
                        kb.st(v, "ve", v.tensor_scalar_max(strl[:],
                                                           psH[:, sl, :], 0.0))
                        guard(f"psH{sl}", "ve")
                    blk.vector(_rlv)
                    ve_rl = kb.cnt["ve"]

                    def _rl(sc, ic=ic, ve_rl=ve_rl):
                        kb.w(sc, "ve", ve_rl)
                        kb.st(sc, "se", sc.activation(hs[:, ic, :], strl[:],
                                                      AF.Square))
                        guard("strl", "se")
                    blk.scalar(_rl)

            for j in range(4):
                def _sdl(gp, ib=ib, j=j):
                    need(gp, f"sdn{j % 2}")
                    kb.inc(("wc", "wd")[j % 2], gp.dma_start(out=sdnbs[j % 2][:], in_=sdn[ib, j]), 16)
                blk.gpsimd(_sdl)
                for m4 in range(4):
                    m = j * 4 + m4
                    sl = m % 2

                    def _dmm(te, j=j, m4=m4, sl=sl, se_a=kb.cnt["se"]):
                        if m4 == 0:
                            kb.w(te, "wc")
                            kb.w(te, "wd")
                            kb.w(te, "se", se_a)
                        need(te, f"psR{sl}")
                        for ts in range(2):
                            for ic in range(8):
                                inst = te.matmul(
                                    psR[:, sl, ts * 512:(ts + 1) * 512],
                                    sdnbs[j % 2][:, ic, m4 * 128:(m4 + 1) * 128],
                                    hs[:, ic, ts * 512:(ts + 1) * 512],
                                    start=(ic == 0), stop=(ic == 7))
                        kb.inc("pe", inst)
                        if m4 == 3:
                            guard(f"sdn{j % 2}", "pe")
                    blk.tensor(_dmm)
                    pe_d = kb.cnt["pe"]

                    def _dacc(v, m=m, sl=sl, pe_d=pe_d):
                        kb.w(v, "pe", pe_d)
                        kb.st(v, "ve", v.tensor_tensor(acc_sb[:, m, :],
                                                       acc_sb[:, m, :],
                                                       psR[:, sl, :], ALU.add))
                        guard(f"psR{sl}", "ve")
                    blk.vector(_dacc)


def stage_inputs(L0_attn_hidden_states, L0_attn_wq, L0_attn_wk, L0_attn_wv,
                 L0_attn_wg, L0_attn_wo, L0_attn_norm_w, L0_ffn_router_weight,
                 L0_ffn_e_bias, L0_ffn_w_up, L0_ffn_w_down, L0_ffn_shared_up,
                 L0_ffn_shared_down):
    f = np.ascontiguousarray
    x = np.asarray(L0_attn_hidden_states, dtype=np.float32)
    wt4 = np.stack([np.asarray(w, np.float32).T for w in
                    (L0_attn_wq, L0_attn_wk, L0_attn_wv, L0_attn_wg)])
    wo_s = f(np.asarray(L0_attn_wo, np.float32).T
             .reshape(C16, 128, 4, 512).transpose(1, 2, 0, 3))
    nw_s = f(np.asarray(L0_attn_norm_w, np.float32).reshape(128, 1))
    rwt_s = f(np.asarray(L0_ffn_router_weight, np.float32).T
              .reshape(C16, 128, NE).transpose(1, 0, 2))
    eb_s = f(np.asarray(L0_ffn_e_bias, np.float32).reshape(NE, 1))
    wup_s = f(np.asarray(L0_ffn_w_up, np.float32)
              .reshape(NE, C16, 128, 4, 256).transpose(0, 3, 2, 1, 4))
    wdn_s = f(np.asarray(L0_ffn_w_down, np.float32)
              .reshape(NE, 8, 128, 4, 512).transpose(0, 3, 2, 1, 4))
    sup_s = f(np.asarray(L0_ffn_shared_up, np.float32)
              .reshape(C16, 128, 8, 512).transpose(2, 1, 0, 3))
    sdn_s = f(np.asarray(L0_ffn_shared_down, np.float32)
              .reshape(4, 8, 128, 4, 512).transpose(0, 3, 2, 1, 4))

    in_maps = []
    for c in range(8):
        b, H = c // 2, c % 2
        xb = x[b]
        xT_s = f(xb.reshape(NLG, LNG, C16, 128).transpose(0, 3, 2, 1))
        xo_s = f(xb[H * OWN:(H + 1) * OWN].reshape(2, 512, C16, 128)
                 .transpose(0, 3, 2, 1))
        wq_s = f(wt4[:, :, H * 1024:(H + 1) * 1024]
                 .reshape(4, C16, 128, NHL, 128).transpose(3, 2, 0, 1, 4))
        in_maps.append({
            "xT": xT_s, "xo": xo_s, "wqkvg": wq_s, "wo": wo_s, "nw": nw_s,
            "rwt": rwt_s, "ebias": eb_s, "wup": wup_s, "wdn": wdn_s,
            "sup": sup_s, "sdn": sdn_s,
        })
    return in_maps


def kernel(**inputs):
    in_maps = stage_inputs(**inputs)
    nc, _ = build()
    res = run_bass_kernel_spmd(nc, in_maps, core_ids=list(range(8)))
    out = np.empty((4, TK, D), dtype=np.float32)
    for c in range(8):
        b, H = c // 2, c % 2
        o = res.results[c]["outT"]
        out[b, H * OWN:(H + 1) * OWN, :] = o.transpose(2, 1, 0).reshape(OWN, D)
    return out


# version 09 (final)
# speedup vs baseline: 1.3060x; 1.0890x over previous
import numpy as np
import concourse.bass as bass
import concourse.mybir as mybir
from concourse.bass_utils import run_bass_kernel_spmd
import concourse.bass_utils as _BU

if not getattr(_BU, "_ldw_patched", False):
    _orig_run_command = _BU.run_command

    def _run_command_ldw(argv, **kwargs):
        try:
            argv = ["--enable-ldw-opt=true" if a == "--enable-ldw-opt=false" else a
                    for a in argv]
        except TypeError:
            pass
        return _orig_run_command(argv, **kwargs)

    _BU.run_command = _run_command_ldw
    _BU._ldw_patched = True

F32 = mybir.dt.float32
BF16 = mybir.dt.bfloat16
AF = mybir.ActivationFunctionType
ALU = mybir.AluOpType
AX = mybir.AxisListType

D = 2048
TK = 2048
OWN = 1024
C16 = D // 128
NHL = 8
HD = 128
SCALE = HD ** -0.5
EPS = 1e-5
NE = 16
LNG = 256
NLG = TK // LNG
ROPE_BASE = 10000.0


def _host_consts():
    t = np.arange(TK, dtype=np.float64)
    inv = ROPE_BASE ** (-np.arange(0, HD, 2, dtype=np.float64) / HD)
    ang = np.outer(np.concatenate([inv, inv]), t)
    cosT = np.cos(ang).astype(np.float32)
    sinT = np.sin(ang).astype(np.float32)
    P = np.zeros((128, 128), dtype=np.float32)
    for m in range(64):
        P[m + 64, m] = -1.0
        P[m, m + 64] = 1.0
    id128 = np.eye(128, dtype=np.float32)
    M128 = np.triu(np.full((128, 128), -1e30, dtype=np.float32), k=1)
    ones_row = np.ones((1, 128), dtype=np.float32)
    mvec = np.full((128, 1), 1.0 / TK, dtype=np.float32)
    rvec = np.full((128, 1), 1.0 / HD, dtype=np.float32)
    iota16 = np.tile(np.arange(16, dtype=np.float32), (128, 1))
    iorev = 15.0 - iota16
    return dict(cosT=cosT, sinT=sinT, P=P, id128=id128, M128=M128,
                ones_row=ones_row, mvec=mvec, rvec=rvec,
                iota16=iota16, iorev=iorev)


class KB:
    def __init__(self, nc):
        self.nc = nc
        self.cnt = {}
        self.sem = {}

    def mksem(self, ctx, *names):
        for n in names:
            self.sem[n] = ctx.enter_context(self.nc.semaphore(n))
            self.cnt[n] = 0

    def st(self, eng, sem, inst, n=1):
        inst.then_inc(self.sem[sem], n)
        self.cnt[sem] += n
        eng.wait_ge(self.sem[sem], self.cnt[sem])
        return self.cnt[sem]

    def inc(self, sem, inst, n=1):
        inst.then_inc(self.sem[sem], n)
        self.cnt[sem] += n
        return self.cnt[sem]

    def uid(self):
        self._uid = getattr(self, "_uid", 0) + 1
        return f"r{self._uid}"

    def w(self, eng, sem, at=None):
        eng.wait_ge(self.sem[sem], self.cnt[sem] if at is None else at)


class _ScShim:

    def __init__(self, sc, dmy, kb):
        self._sc = sc
        self._dmy = dmy
        self._kb = kb
        self._pinned = False

    def _ensure(self):
        if not self._pinned:
            self._pinned = True
            d = self._dmy
            kb = self._kb
            kb.st(self._sc, "se", self._sc.activation(
                d[0:1, 0:1], d[0:1, 2:3], AF.Exp, scale=0.0, bias=1.0))
            kb.st(self._sc, "se", self._sc.activation(
                d[0:1, 1:2], d[0:1, 2:3], AF.Ln, scale=0.0, bias=1.0))

    def activation(self, *a, **k):
        self._ensure()
        return self._sc.activation(*a, **k)

    def copy(self, *a, **k):
        self._ensure()
        return self._sc.copy(*a, **k)

    def mul(self, *a, **k):
        self._ensure()
        return self._sc.mul(*a, **k)

    def add(self, *a, **k):
        self._ensure()
        return self._sc.add(*a, **k)

    def __getattr__(self, n):
        return getattr(self._sc, n)


class _FakeBlk:

    def __init__(self, nc, dmy, kb, real):
        self._nc = nc
        self._real = real
        self._shim = None
        self._dmy = dmy
        self._kb = kb

    def mega(self, emitfn):
        def _run(_e):
            self._shim = _ScShim(self._nc.scalar, self._dmy, self._kb)
            emitfn()
        self._real.sync(_run)

    def tensor(self, f):
        f(self._nc.tensor)

    def vector(self, f):
        f(self._nc.vector)

    def gpsimd(self, f):
        f(self._nc.gpsimd)

    def sync(self, f):
        f(self._nc.sync)

    def scalar(self, f):
        f(self._shim)


def bc(ap, like_ap):
    _, out = bass.broadcast_tensor_aps(like_ap, ap)
    return out


def bc3(ps_ap, nmid):
    a = ps_ap
    return bass.AP(a.tensor, a.offset, [a.ap[0], [0, nmid], a.ap[1]])


def build(upto="E", taps=()):
    nc = bass.Bass("TRN2", target_bir_lowering=False)
    kb = KB(nc)
    CH = _host_consts()

    xT = nc.dram_tensor("xT", [NLG, 128, C16, LNG], F32, kind="ExternalInput").ap()
    xo = nc.dram_tensor("xo", [2, 128, C16, 512], F32, kind="ExternalInput").ap()
    wqkvg = nc.dram_tensor("wqkvg", [NHL, 128, 4, C16, 128], F32, kind="ExternalInput").ap()
    wo = nc.dram_tensor("wo", [128, 4, C16, 512], F32, kind="ExternalInput").ap()
    nw = nc.dram_tensor("nw", [128, 1], F32, kind="ExternalInput").ap()
    rwt = nc.dram_tensor("rwt", [128, C16, NE], F32, kind="ExternalInput").ap()
    ebias = nc.dram_tensor("ebias", [NE, 1], F32, kind="ExternalInput").ap()
    wup = nc.dram_tensor("wup", [NE, 4, 128, C16, 256], F32, kind="ExternalInput").ap()
    wdn = nc.dram_tensor("wdn", [NE, 4, 128, 8, 512], F32, kind="ExternalInput").ap()
    sup = nc.dram_tensor("sup", [8, 128, C16, 512], F32, kind="ExternalInput").ap()
    sdn = nc.dram_tensor("sdn", [4, 4, 128, 8, 512], F32, kind="ExternalInput").ap()
    outT = nc.dram_tensor("outT", [128, C16, OWN], F32, kind="ExternalOutput").ap()

    ccin = nc.dram_tensor("ccin", [NHL, 128, TK], BF16, kind="Internal")
    ccout = nc.dram_tensor("ccout", [2 * NHL, 128, TK], BF16, kind="Internal")

    cosT_h = nc.inline_tensor(CH["cosT"], name="cosT").ap()
    sinT_h = nc.inline_tensor(CH["sinT"], name="sinT").ap()
    P_h = nc.inline_tensor(CH["P"], name="Pperm").ap()
    id_h = nc.inline_tensor(CH["id128"], name="id128").ap()
    M128_h = nc.inline_tensor(CH["M128"], name="M128").ap()
    ones_h = nc.inline_tensor(CH["ones_row"], name="onesrow").ap()
    mvec_h = nc.inline_tensor(CH["mvec"], name="mvec").ap()
    rvec_h = nc.inline_tensor(CH["rvec"], name="rvec").ap()
    iota_h = nc.inline_tensor(CH["iota16"], name="iota16").ap()
    iorev_h = nc.inline_tensor(CH["iorev"], name="iorev").ap()

    tap_handles = {}

    def mktap(name, shape, dtype):
        tap_handles[name] = nc.dram_tensor("tap_" + name, shape, dtype,
                                           kind="ExternalOutput").ap()
        return tap_handles[name]

    from contextlib import ExitStack
    with ExitStack() as X:
        E = X.enter_context

        SB = lambda *a, **k: E(nc.sbuf_tensor(*a, side="left", **k))
        P_sb = SB("P_sb", [128, 128], BF16)
        idb_sb = SB("idb_sb", [128, 128], BF16)
        idf_sb = SB("idf_sb", [128, 128], F32)
        ones_sb = SB("ones_sb", [1, 128], F32)
        onesb_sb = SB("onesb_sb", [1, 128], BF16)
        mvec_sb = SB("mvec_sb", [128, 1], BF16)
        rvec_sb = SB("rvec_sb", [128, 1], BF16)
        nw_sb = SB("nw_sb", [128, 1], F32)
        eb_sb = SB("eb_sb", [NE, 1], F32)
        iota_sb = SB("iota_sb", [128, 16], F32)
        iorev_sb = SB("iorev_sb", [128, 16], F32)
        cmb_sb = SB("cmb_sb", [NE, TK], F32)
        rwt_sb = SB("rwt_sb", [128, C16, NE], F32)
        xn_own = SB("xn_own", [128, C16, OWN], BF16)

        dmy_sb = SB("dmy_sb", [1, 4], F32)
        kb.mksem(X, "wg", "wa", "wb", "wc", "wd", "we", "wf", "la", "lb", "wx", "sq", "pe", "ve", "se", "gp", "cc")

        blkR = E(nc.Block())
        blk = _FakeBlk(nc, dmy_sb, kb, blkR)

        def _pid(sc):
            kb.par_reg = sc.alloc_register("parH")
            pid = sc.partition_id()
            sc.reg_alu(kb.par_reg, pid, 2, ALU.mod)
        def _cl(sy):
            for dst, src in ((idf_sb, id_h), (ones_sb, ones_h), (nw_sb, nw),
                             (eb_sb, ebias), (iota_sb, iota_h), (iorev_sb, iorev_h),
                             (rwt_sb, rwt)):
                kb.inc("sq", sy.dma_start(out=dst[:], in_=src), 16)
        def _clg(gp):
            for dst, src in ((P_sb, P_h), (idb_sb, id_h), (mvec_sb, mvec_h),
                             (rvec_sb, rvec_h)):
                kb.inc("wg", gp.dma_start(out=dst[:], in_=src), 16)
            kb.inc("wg", gp.dma_start(out=onesb_sb[:], in_=ones_h), 16)
        def _pre():
            blk.scalar(_pid)
            blk.sync(_cl)
            blk.gpsimd(_clg)
        blk.mega(_pre)
        CONST_WG = kb.cnt["wg"]
        CONST_SQ = kb.cnt["sq"]

        with (
            nc.sbuf_tensor("xn_sb", [128, C16, TK], BF16, side="right") as xn_sb,
            nc.sbuf_tensor("logit_sb", [NE, TK], F32, side="right") as logit_sb,
        ):
            def _phaseA():
                with (
                    nc.sbuf_tensor("xf0", [128, C16, LNG], F32, side="right") as xf0,
                    nc.sbuf_tensor("xf1", [128, C16, LNG], F32, side="right") as xf1,
                    nc.sbuf_tensor("xb", [128, C16, LNG], BF16, side="right") as xb,
                    nc.sbuf_tensor("xsq", [128, C16, LNG], BF16, side="right") as xsq,
                    nc.sbuf_tensor("xnf", [128, C16, LNG], F32, side="right") as xnf,
                    nc.sbuf_tensor("murow", [1, LNG], F32, side="right") as murow,
                    nc.sbuf_tensor("msqrow", [1, LNG], F32, side="right") as msqrow,
                    nc.sbuf_tensor("varrow", [1, LNG], F32, side="right") as varrow,
                    nc.sbuf_tensor("rstd", [1, LNG], F32, side="right") as rstd,
                    nc.psum_tensor("psMu", [1, LNG], F32) as psMu,
                    nc.psum_tensor("psMsq", [1, LNG], F32) as psMsq,
                    nc.psum_tensor("psMuB", [128, LNG], F32) as psMuB,
                    nc.psum_tensor("psRB", [128, LNG], F32) as psRB,
                    nc.psum_tensor("psL", [NE, LNG], F32) as psL,
                ):
                    xfs = (xf0, xf1)
                    xf_w = {}

                    def _ld01(sc):
                        kb.inc("la", sc.dma_start(out=xf0[:], in_=xT[0]), 16)
                        kb.inc("lb", sc.dma_start(out=xf1[:], in_=xT[1]), 16)
                    blk.scalar(_ld01)
                    xf_w[0] = ("la", kb.cnt["la"])
                    xf_w[1] = ("lb", kb.cnt["lb"])

                    for g in range(NLG):
                        gs = slice(g * LNG, (g + 1) * LNG)
                        xf = xfs[g % 2]

                        def _cast(sc, xf=xf, gw=xf_w[g]):
                            kb.w(sc, gw[0], gw[1])
                            kb.st(sc, "se", sc.copy(xb[:], xf[:]))
                            kb.st(sc, "se", sc.activation(xsq[:], xf[:], AF.Square))
                        blk.scalar(_cast)
                        se_cast = kb.cnt["se"]

                        def _stats(te, g=g, se_cast=se_cast):
                            kb.w(te, "se", se_cast)
                            if g == 0:
                                kb.w(te, "wg", CONST_WG)
                                kb.w(te, "sq", CONST_SQ)
                            if g > 0:
                                kb.w(te, "ve")
                            for c in range(C16):
                                i = te.matmul(psMu[:], mvec_sb[:], xb[:, c, :],
                                              start=(c == 0), stop=(c == C16 - 1))
                            kb.inc("pe", i)
                            for c in range(C16):
                                i = te.matmul(psMsq[:], mvec_sb[:], xsq[:, c, :],
                                              start=(c == 0), stop=(c == C16 - 1))
                            kb.inc("pe", i)
                        blk.tensor(_stats)
                        pe_stats = kb.cnt["pe"]

                        def _rows(v, pe_stats=pe_stats):
                            kb.w(v, "pe", pe_stats)
                            kb.st(v, "ve", v.tensor_copy(murow[:], psMu[:]))
                            kb.st(v, "ve", v.tensor_copy(msqrow[:], psMsq[:]))
                            kb.st(v, "ve", v.tensor_tensor(varrow[:], murow[:], murow[:], ALU.mult))
                            kb.st(v, "ve", v.tensor_tensor(varrow[:], msqrow[:], varrow[:], ALU.subtract))
                            kb.st(v, "ve", v.tensor_scalar_add(varrow[:], varrow[:], EPS))
                        blk.vector(_rows)
                        ve_rows = kb.cnt["ve"]

                        def _sqrt(sc, ve_rows=ve_rows):
                            kb.w(sc, "ve", ve_rows)
                            kb.st(sc, "se", sc.activation(rstd[:], varrow[:], AF.Ln))
                            kb.st(sc, "se", sc.activation(rstd[:], rstd[:], AF.Exp,
                                                          scale=-0.5))
                        blk.scalar(_sqrt)
                        se_sqrt = kb.cnt["se"]

                        def _bcast(te, se_sqrt=se_sqrt, g=g):
                            kb.w(te, "se", se_sqrt)
                            if g > 0:
                                kb.w(te, "ve")
                            kb.inc("pe", te.matmul(psMuB[:], ones_sb[:], murow[:],
                                                   start=True, stop=True))
                            kb.inc("pe", te.matmul(psRB[:], ones_sb[:], rstd[:],
                                                   start=True, stop=True))
                        blk.tensor(_bcast)
                        pe_b = kb.cnt["pe"]

                        def _norm(v, pe_b=pe_b, xf=xf):
                            kb.w(v, "pe", pe_b)
                            kb.st(v, "ve", v.tensor_tensor(xnf[:], xf[:], bc3(psMuB[:], C16),
                                                           ALU.subtract))
                            kb.st(v, "ve", v.tensor_tensor(xnf[:], xnf[:], bc3(psRB[:], C16),
                                                           ALU.mult))
                        blk.vector(_norm)
                        ve_norm = kb.cnt["ve"]

                        if g + 2 < NLG:
                            def _ld2(sc, g=g, ve_norm=ve_norm):
                                kb.w(sc, "ve", ve_norm)
                                kb.inc(("la", "lb")[g % 2],
                                       sc.dma_start(out=xfs[g % 2][:], in_=xT[g + 2]), 16)
                            blk.scalar(_ld2)
                            xf_w[g + 2] = (("la", "lb")[g % 2],
                                           kb.cnt[("la", "lb")[g % 2]])

                        def _store(sc, g=g, ve_norm=ve_norm):
                            kb.w(sc, "ve", ve_norm)
                            kb.st(sc, "se", sc.copy(xn_sb[:, :, gs], xnf[:]))
                        blk.scalar(_store)

                        def _router(te, g=g, ve_norm=ve_norm):
                            kb.w(te, "ve", ve_norm)
                            if g > 0:
                                kb.w(te, "ve")
                            for c in range(C16):
                                i = te.matmul(psL[:], rwt_sb[:, c, :], xnf[:, c, :],
                                              start=(c == 0), stop=(c == C16 - 1))
                            kb.inc("pe", i)
                        blk.tensor(_router)
                        pe_r = kb.cnt["pe"]

                        def _evl(v, g=g, pe_r=pe_r):
                            kb.w(v, "pe", pe_r)
                            kb.st(v, "ve", v.tensor_copy(logit_sb[:, gs], psL[:]))
                        blk.vector(_evl)

                if "xn" in taps or "logits" in taps:
                    txn = mktap("xn", [128, C16, TK], BF16) if "xn" in taps else None
                    tlg = mktap("logits", [NE, TK], F32) if "logits" in taps else None

                    def _tap(sy):
                        kb.w(sy, "se")
                        kb.w(sy, "ve")
                        if txn is not None:
                            kb.inc("sq", sy.dma_start(out=txn, in_=xn_sb[:]), 16)
                        if tlg is not None:
                            kb.inc("sq", sy.dma_start(out=tlg, in_=logit_sb[:]), 16)
                    blk.sync(_tap)

            blk.mega(_phaseA)

            if upto >= "B":
                blk.mega(lambda: emit_routing(kb, blk, nc, X, logit_sb, cmb_sb,
                                              eb_sb, iota_sb, iorev_sb, idf_sb,
                                              ones_sb, taps, mktap))

            if upto >= "C":
                blk.mega(lambda: emit_attention(kb, blk, nc, X, xn_sb, xn_own,
                                                wqkvg, cosT_h, sinT_h, P_sb,
                                                idb_sb, idf_sb, ones_sb, M128_h,
                                                nw_sb, rvec_sb, onesb_sb, ccin,
                                                taps, mktap))

        if upto >= "D":
            acc_sb = SB("acc_sb", [128, C16, OWN], F32)
            blk.mega(lambda: emit_cc(kb, blk, nc, ccin, ccout))
            if upto >= "E":
                blk.mega(lambda: emit_moe(kb, blk, nc, X, acc_sb, xn_own,
                                          cmb_sb, wup, wdn, ones_sb, xo, taps,
                                          mktap))
            blk.mega(lambda: emit_oproj(kb, blk, nc, X, acc_sb, xo, wo, ccin,
                                        ccout, taps, mktap,
                                        init_acc=(upto < "E")))
            if upto >= "E":
                blk.mega(lambda: emit_shared(kb, blk, nc, X, acc_sb, xn_own,
                                             sup, sdn, taps, mktap))

            def _out(sy):
                kb.w(sy, "ve")
                kb.w(sy, "se")
                kb.inc("sq", sy.dma_start(out=outT, in_=acc_sb[:]), 16)
                kb.w(sy, "sq")
            blk.mega(lambda: blk.sync(_out))
        else:
            def _fin(sy):
                kb.w(sy, "sq")
                kb.w(sy, "wx")
                kb.w(sy, "gp")
            blk.mega(lambda: blk.sync(_fin))

    return nc, tap_handles


def emit_routing(kb, blk, nc, X, logit_sb, cmb_sb, eb_sb, iota_sb, iorev_sb,
                 idf_sb, ones_sb, taps, mktap):
    E = X.enter_context
    with (
        nc.sbuf_tensor("sig_sb", [NE, TK], F32, side="right") as sig_sb,
        nc.sbuf_tensor("sbb_sb", [NE, TK], F32, side="right") as sbb_sb,
        nc.sbuf_tensor("ss_t", [128, 16], F32, side="right") as ss_t,
        nc.sbuf_tensor("sb_t", [128, 16], F32, side="right") as sb_t,
        nc.sbuf_tensor("r1", [128, 4], F32, side="right") as r1,
        nc.sbuf_tensor("r2", [128, 4], F32, side="right") as r2,
        nc.sbuf_tensor("r3", [128, 4], F32, side="right") as r3,
        nc.sbuf_tensor("r4", [128, 4], F32, side="right") as r4,
        nc.sbuf_tensor("gsum", [128, 4], F32, side="right") as gsum,
        nc.sbuf_tensor("gsec", [128, 1], F32, side="right") as gsec,
        nc.sbuf_tensor("msk", [128, 16], F32, side="right") as msk,
        nc.sbuf_tensor("oh1", [128, 16], F32, side="right") as oh1,
        nc.sbuf_tensor("oh2", [128, 16], F32, side="right") as oh2,
        nc.sbuf_tensor("mrow", [128, 1], F32, side="right") as mrow,
        nc.sbuf_tensor("w1", [128, 1], F32, side="right") as w1,
        nc.sbuf_tensor("w2", [128, 1], F32, side="right") as w2,
        nc.sbuf_tensor("cmt", [128, 16], F32, side="right") as cmt,
        nc.psum_tensor("psTr", [128, 512], F32) as psTr,
        nc.psum_tensor("psTb", [NE, 512], F32) as psTb,
    ):
        def _sig(sc):
            kb.w(sc, "ve")
            kb.w(sc, "pe")
            kb.w(sc, "wx")
            kb.st(sc, "se", sc.activation(sig_sb[:], logit_sb[:], AF.Exp,
                                          scale=-1.0))
        blk.scalar(_sig)
        se_e = kb.cnt["se"]

        def _sigv(v, se_e=se_e):
            kb.w(v, "se", se_e)
            kb.st(v, "ve", v.tensor_scalar_add(sig_sb[:], sig_sb[:], 1.0))
            kb.st(v, "ve", v.reciprocal(sig_sb[:], sig_sb[:]))
        blk.vector(_sigv)
        ve_sig = kb.cnt["ve"]

        def _sig2(sc, ve_sig=ve_sig):
            kb.w(sc, "ve", ve_sig)
            kb.st(sc, "se", sc.add(sbb_sb[:], sig_sb[:], eb_sb[:]))
        blk.scalar(_sig2)
        se_sig = kb.cnt["se"]

        for tt in range(TK // 128):
            ts = slice(tt * 128, (tt + 1) * 128)

            def _tr(te, ts=ts, se_sig=se_sig, tt=tt):
                kb.w(te, "se", se_sig)
                if tt > 0:
                    kb.w(te, "ve")
                kb.inc("pe", te.transpose(psTr[:, 0:16], sig_sb[:, ts],
                                          idf_sb[0:16, 0:16]))
                kb.inc("pe", te.transpose(psTr[:, 16:32], sbb_sb[:, ts],
                                          idf_sb[0:16, 0:16]))
            blk.tensor(_tr)
            pe_tr = kb.cnt["pe"]

            def _route(v, pe_tr=pe_tr):
                S = lambda inst: kb.st(v, "ve", inst)
                kb.w(v, "pe", pe_tr)
                S(v.tensor_copy(ss_t[:], psTr[:, 0:16]))
                S(v.tensor_copy(sb_t[:], psTr[:, 16:32]))
                a = [sb_t[:, i::4] for i in range(4)]
                S(v.tensor_max(r1[:], a[0], a[1]))
                S(v.tensor_tensor(r2[:], a[0], a[1], ALU.min))
                S(v.tensor_max(r3[:], a[2], a[3]))
                S(v.tensor_tensor(r4[:], a[2], a[3], ALU.min))
                S(v.tensor_max(gsum[:], r1[:], r3[:]))
                S(v.tensor_tensor(r1[:], r1[:], r3[:], ALU.min))
                S(v.tensor_max(r2[:], r2[:], r4[:]))
                S(v.tensor_max(r1[:], r1[:], r2[:]))
                S(v.tensor_add(gsum[:], gsum[:], r1[:]))
                b = [gsum[:, i:i + 1] for i in range(4)]
                S(v.tensor_max(r2[:, 0:1], b[0], b[1]))
                S(v.tensor_tensor(r2[:, 1:2], b[0], b[1], ALU.min))
                S(v.tensor_max(r2[:, 2:3], b[2], b[3]))
                S(v.tensor_tensor(r2[:, 3:4], b[2], b[3], ALU.min))
                S(v.tensor_tensor(r3[:, 0:1], r2[:, 0:1], r2[:, 2:3], ALU.min))
                S(v.tensor_max(r3[:, 1:2], r2[:, 1:2], r2[:, 3:4]))
                S(v.tensor_max(gsec[:], r3[:, 0:1], r3[:, 1:2]))
                S(v.tensor_tensor(r4[:], gsum[:], bc(gsec.ap(), gsum.ap()), ALU.is_ge))
                gm = bass.AP(r4, 0, [r4.ap().ap[0], [1, 4], [0, 4]])
                sb4 = bass.AP(sb_t, 0, [sb_t.ap().ap[0], [4, 4], [1, 4]])
                mk4 = bass.AP(msk, 0, [msk.ap().ap[0], [4, 4], [1, 4]])
                S(v.tensor_tensor(mk4, sb4, gm, ALU.mult))
                S(v.reduce_max(mrow[:], msk[:], axis=AX.X))
                S(v.tensor_tensor(oh1[:], msk[:], bc(mrow.ap(), msk.ap()), ALU.is_equal))
                S(v.tensor_tensor(oh1[:], oh1[:], iorev_sb[:], ALU.mult))
                S(v.reduce_max(mrow[:], oh1[:], axis=AX.X))
                S(v.tensor_scalar(mrow[:], mrow[:], -1.0, 15.0, ALU.mult, ALU.add))
                S(v.tensor_tensor(oh1[:], iota_sb[:], bc(mrow.ap(), iota_sb.ap()),
                                  ALU.is_equal))
                S(v.scalar_tensor_tensor(msk[:], oh1[:], -1e30, msk[:],
                                         ALU.mult, ALU.add))
                S(v.reduce_max(mrow[:], msk[:], axis=AX.X))
                S(v.tensor_tensor(oh2[:], msk[:], bc(mrow.ap(), msk.ap()), ALU.is_equal))
                S(v.tensor_tensor(oh2[:], oh2[:], iorev_sb[:], ALU.mult))
                S(v.reduce_max(mrow[:], oh2[:], axis=AX.X))
                S(v.tensor_scalar(mrow[:], mrow[:], -1.0, 15.0, ALU.mult, ALU.add))
                S(v.tensor_tensor(oh2[:], iota_sb[:], bc(mrow.ap(), iota_sb.ap()),
                                  ALU.is_equal))
                S(v.tensor_tensor(cmt[:], oh1[:], ss_t[:], ALU.mult))
                S(v.reduce_sum(w1[:], cmt[:], axis=AX.X))
                S(v.tensor_tensor(cmt[:], oh2[:], ss_t[:], ALU.mult))
                S(v.reduce_sum(w2[:], cmt[:], axis=AX.X))
                S(v.tensor_add(mrow[:], w1[:], w2[:]))
                S(v.tensor_scalar_add(mrow[:], mrow[:], 1e-20))
                S(v.reciprocal(mrow[:], mrow[:]))
                S(v.tensor_tensor(w1[:], w1[:], mrow[:], ALU.mult))
                S(v.tensor_tensor(w2[:], w2[:], mrow[:], ALU.mult))
                S(v.tensor_tensor(cmt[:], oh1[:], bc(w1.ap(), oh1.ap()), ALU.mult))
                S(v.scalar_tensor_tensor(oh2[:], oh2[:], 1.0, bc(w2.ap(), oh2.ap()),
                                         ALU.mult, ALU.mult))
                S(v.tensor_add(cmt[:], cmt[:], oh2[:]))
            blk.vector(_route)
            ve_rt = kb.cnt["ve"]

            def _trb(te, ts=ts, ve_rt=ve_rt):
                kb.w(te, "ve", ve_rt)
                kb.inc("pe", te.transpose(psTb[:, 0:128], cmt[:], idf_sb[:]))
            blk.tensor(_trb)
            pe_tb = kb.cnt["pe"]

            def _evc(v, ts=ts, pe_tb=pe_tb):
                kb.w(v, "pe", pe_tb)
                kb.st(v, "ve", v.tensor_copy(cmb_sb[:, ts], psTb[0:NE, 0:128]))
            blk.vector(_evc)

    if "cmb" in taps:
        t = mktap("cmb", [NE, TK], F32)

        def _tapc(sy):
            kb.w(sy, "ve")
            kb.inc("sq", sy.dma_start(out=t, in_=cmb_sb[:]), 16)
        blk.sync(_tapc)


def emit_attention(kb, blk, nc, X, xn_sb, xn_own, wqkvg, cosT_h, sinT_h,
                   P_sb, idb_sb, idf_sb, ones_sb, M128_h, nw_sb, rvec_sb,
                   onesb_sb, ccin, taps, mktap):
    NT = TK // 128
    from contextlib import ExitStack
    with ExitStack() as XA:
        cos_sb = XA.enter_context(nc.sbuf_tensor("cos_sb", [128, TK], F32, side="right"))
        sin_sb = XA.enter_context(nc.sbuf_tensor("sin_sb", [128, TK], F32, side="right"))
        M128_sb = XA.enter_context(nc.sbuf_tensor("M128_sb", [128, 128], F32, side="right"))
        wblk0 = XA.enter_context(nc.sbuf_tensor("wblk0", [128, 2, C16, 128], BF16, side="right"))
        wblk1 = XA.enter_context(nc.sbuf_tensor("wblk1", [128, 2, C16, 128], BF16, side="right"))
        qro = XA.enter_context(nc.sbuf_tensor("qro", [128, TK], BF16, side="right"))
        kro = XA.enter_context(nc.sbuf_tensor("kro", [128, TK], BF16, side="right"))
        vT = XA.enter_context(nc.sbuf_tensor("vT", [128, NT, 128], BF16, side="right"))
        gbf = XA.enter_context(nc.sbuf_tensor("gbf", [128, TK], F32, side="right"))
        stage = XA.enter_context(nc.sbuf_tensor("stage", [128, 512], BF16, side="right"))
        qsc1 = XA.enter_context(nc.sbuf_tensor("qsc1", [128, 512], F32, side="right"))
        qsc2 = XA.enter_context(nc.sbuf_tensor("qsc2", [128, 512], F32, side="right"))
        e0 = XA.enter_context(nc.sbuf_tensor("e0", [128, TK], BF16, side="right"))
        e1 = XA.enter_context(nc.sbuf_tensor("e1", [128, TK], BF16, side="right"))
        eT2 = XA.enter_context(nc.sbuf_tensor("eT2", [128, 2, NT, 128], BF16, side="right"))
        mcols = XA.enter_context(nc.sbuf_tensor("mcols", [128, 8], F32, side="right"))
        bcols = XA.enter_context(nc.sbuf_tensor("bcols", [128, 8], F32, side="right"))
        scols = XA.enter_context(nc.sbuf_tensor("scols", [128, 8], F32, side="right"))
        acols = XA.enter_context(nc.sbuf_tensor("acols", [128, 2], F32, side="right"))
        srow_sb = XA.enter_context(nc.sbuf_tensor("srow_sb", [1, 256], F32, side="right"))
        arow_sb = XA.enter_context(nc.sbuf_tensor("arow_sb", [1, 256], F32, side="right"))
        crow_sb = XA.enter_context(nc.sbuf_tensor("crow_sb", [1, 256], F32, side="right"))
        sq_sb = XA.enter_context(nc.sbuf_tensor("sq_sb", [128, 256], BF16, side="right"))
        av2_sb = XA.enter_context(nc.sbuf_tensor("av2_sb", [128, 256], F32, side="right"))
        t1_sb = XA.enter_context(nc.sbuf_tensor("t1_sb", [128, 256], F32, side="right"))
        sgt_sb = XA.enter_context(nc.sbuf_tensor("sgt_sb", [128, 256], F32, side="right"))
        gat_sb = XA.enter_context(nc.sbuf_tensor("gat_sb", [128, TK], BF16, side="right"))
        psS = XA.enter_context(nc.psum_tensor("psS", [128, 2, 1024], F32))
        psAV = XA.enter_context(nc.psum_tensor("psAV", [128, 512], F32))
        psRow = XA.enter_context(nc.psum_tensor("psRow", [128, 512], F32))
        psT0 = XA.enter_context(nc.psum_tensor("psT0", [128, 512], BF16))
        psT1 = XA.enter_context(nc.psum_tensor("psT1", [128, 512], BF16))
        psTb = (psT0, psT1)
        psAV0 = psAV[:, 0:128]
        psAV1 = psAV[:, 128:256]
        e_par = (e0, e1)
        av_ps = (psAV0, psAV1)
        av_full = psAV
        wbs = (wblk0, wblk1)
        G = {}
        now = {s: kb.cnt[s] for s in ("pe", "ve", "se", "gp", "wx", "sq")}
        for k in ("psS0", "psS1", "psTb0", "psTb1", "av", "psRa", "psRb"):
            G[k] = [("pe", now["pe"]), ("ve", now["ve"]), ("se", now["se"])]

        def need(eng, key):
            for sem, cnt in G.get(key, ()):
                kb.w(eng, sem, cnt)
            G[key] = []

        def guard(key, sem):
            G.setdefault(key, []).append((sem, kb.cnt[sem]))

        def _acl(sy):
            for s in ("pe", "ve", "se", "gp", "wx", "sq"):
                kb.w(sy, s, now[s])
            kb.inc("sq", sy.dma_start(out=cos_sb[:], in_=cosT_h), 16)
            kb.inc("sq", sy.dma_start(out=sin_sb[:], in_=sinT_h), 16)
            kb.inc("sq", sy.dma_start(out=M128_sb[:], in_=M128_h), 16)
        blk.sync(_acl)
        ACL = kb.cnt["sq"]

        def _wl0(gp):
            for s in ("pe", "ve", "se", "sq", "wx"):
                kb.w(gp, s, now[s])
            kb.inc("wa", gp.dma_start(out=wblk0[:], in_=wqkvg[0][:, 0:2]), 16)
            kb.inc("wb", gp.dma_start(out=wblk1[:], in_=wqkvg[0][:, 2:4]), 16)
        blk.gpsimd(_wl0)
        qk_cnt = {0: kb.cnt["wa"]}
        vg_cnt = {0: kb.cnt["wb"]}

        for g in range(NHL):

            for mat, name in ((0, "q"), (1, "k"), (2, "v"), (3, "g")):
                for i in range(4):
                    seg = slice(i * 512, (i + 1) * 512)
                    s = i % 2
                    pp = psS[:, s, 0:512]
                    pr = psS[:, s, 512:1024]
                    skey = f"psS{s}"

                    wb = wbs[mat // 2]

                    def _mm(te, mat=mat, seg=seg, pp=pp, skey=skey, g=g, wb=wb,
                            i=i):
                        if i == 0:
                            if mat < 2:
                                kb.w(te, "wa", qk_cnt[g])
                            else:
                                kb.w(te, "wb", vg_cnt[g])
                        need(te, skey)
                        for c in range(C16):
                            inst = te.matmul(pp, wb[:, mat % 2, c, :],
                                             xn_sb[:, c, seg],
                                             start=(c == 0), stop=(c == C16 - 1))
                        kb.inc("pe", inst)
                    blk.tensor(_mm)
                    pe_mm = kb.cnt["pe"]

                    if name in ("q", "k"):
                        def _stg(sc, pp=pp, pe_mm=pe_mm):
                            kb.w(sc, "pe", pe_mm)
                            need(sc, "stage")
                            kb.st(sc, "se", sc.copy(stage[:], pp))
                        blk.scalar(_stg)
                        se_stage = kb.cnt["se"]

                        def _rot(te, pr=pr, se_stage=se_stage):
                            kb.w(te, "se", se_stage)
                            kb.inc("pe", te.matmul(pr, P_sb[:], stage[:],
                                                   start=True, stop=True))
                            guard("stage", "pe")
                        blk.tensor(_rot)
                        pe_rot = kb.cnt["pe"]

                        def _mul(v, pp=pp, pr=pr, seg=seg, pe_rot=pe_rot,
                                 skey=skey, g=g, i=i):
                            if g == 0 and i == 0:
                                kb.w(v, "sq", ACL)
                            kb.w(v, "pe", pe_rot)
                            need(v, "qsc")
                            kb.st(v, "ve", v.tensor_tensor(qsc1[:], pp,
                                                           cos_sb[:, seg], ALU.mult))
                            kb.st(v, "ve", v.tensor_tensor(qsc2[:], pr,
                                                           sin_sb[:, seg], ALU.mult))
                            guard(skey, "ve")
                        blk.vector(_mul)
                        ve_mul = kb.cnt["ve"]

                        dst, dkey = (qro, "qro") if name == "q" else (kro, "kro")

                        def _add(gp, dst=dst, dkey=dkey, seg=seg, ve_mul=ve_mul, i=i):
                            kb.w(gp, "ve", ve_mul)
                            if i == 0:
                                need(gp, dkey)
                            kb.st(gp, "gp", gp.tensor_add(dst[:, seg], qsc1[:], qsc2[:]))
                            guard("qsc", "gp")
                        blk.gpsimd(_add)
                        if name == "k" and i == 3 and g + 1 < NHL:
                            guard("wblk0", "pe")

                            def _wlq(gp, g=g):
                                need(gp, "wblk0")
                                kb.inc("wa", gp.dma_start(
                                    out=wblk0[:], in_=wqkvg[g + 1][:, 0:2]), 16)
                            blk.gpsimd(_wlq)
                            qk_cnt[g + 1] = kb.cnt["wa"]
                    elif name == "v":
                        def _stg(sc, pp=pp, pe_mm=pe_mm, skey=skey):
                            kb.w(sc, "pe", pe_mm)
                            need(sc, "stage")
                            kb.st(sc, "se", sc.copy(stage[:], pp))
                            guard(skey, "se")
                        blk.scalar(_stg)
                        se_stage = kb.cnt["se"]

                        for j in range(4):
                            tc = i * 4 + j
                            pb = psTb[(j // 2) % 2]
                            po = (j % 2) * 128

                            def _tp(te, j=j, se_stage=se_stage, pb=pb, po=po):
                                if j % 2 == 0:
                                    need(te, f"psTb{(j // 2) % 2}")
                                if j == 0:
                                    kb.w(te, "se", se_stage)
                                kb.inc("pe", te.transpose(
                                    pb[:, po:po + 128],
                                    stage[:, j * 128:(j + 1) * 128], idb_sb[:]))
                                if j == 3:
                                    guard("stage", "pe")
                            blk.tensor(_tp)
                            pe_tp = kb.cnt["pe"]

                            if j % 2 == 1:
                                def _ev(v, j=j, tc=tc, pe_tp=pe_tp, pb=pb):
                                    kb.w(v, "pe", pe_tp)
                                    if tc <= 1:
                                        need(v, "vT")
                                    kb.st(v, "ve", v.tensor_copy(
                                        vT[:, tc - 1:tc + 1, :], pb[:, 0:256]))
                                    guard(f"psTb{(j // 2) % 2}", "ve")
                                blk.vector(_ev)
                    else:
                        def _stg(sc, pp=pp, seg=seg, pe_mm=pe_mm, skey=skey, i=i):
                            kb.w(sc, "pe", pe_mm)
                            if i == 0:
                                need(sc, "gbf")
                            kb.st(sc, "se", sc.copy(gbf[:, seg], pp))
                            guard(skey, "se")
                        blk.scalar(_stg)
                        if i == 3 and g + 1 < NHL:
                            guard("wblk1", "pe")

                            def _wlv(gp, g=g):
                                need(gp, "wblk1")
                                kb.inc("wb", gp.dma_start(
                                    out=wblk1[:], in_=wqkvg[g + 1][:, 2:4]), 16)
                            blk.gpsimd(_wlv)
                            vg_cnt[g + 1] = kb.cnt["wb"]


            cc_chunk = 0
            for b2 in range(NT // 2):
                q0 = 2 * b2
                tile_info = []
                for qj in (0, 1):
                    gq = q0 + qj
                    par = gq % 2
                    ebuf = e_par[par]
                    qseg = slice(gq * 128, (gq + 1) * 128)
                    nkt = gq + 1
                    two = nkt > 8
                    chunks = [(0, min(nkt, 8))] + ([(8, nkt)] if two else [])
                    mb4 = 4 * qj
                    mA = mcols[:, mb4:mb4 + 1]
                    mB = mcols[:, mb4 + 1:mb4 + 2]
                    mfin = mcols[:, mb4 + 2:mb4 + 3]
                    bA = bcols[:, mb4:mb4 + 1]
                    bfin = bcols[:, mb4 + 2:mb4 + 3]
                    sA = scols[:, mb4:mb4 + 1]
                    sB = scols[:, mb4 + 1:mb4 + 2]
                    sfin = scols[:, mb4 + 2:mb4 + 3]
                    acol = acols[:, qj:qj + 1]

                    for ci, (k0, k1) in enumerate(chunks):
                        w = (k1 - k0) * 128
                        slot = cc_chunk % 2
                        cc_chunk += 1
                        mc = (mA, mB)[ci]

                        def _smm(te, k0=k0, slot=slot, qseg=qseg, w=w,
                                 firstb=(qj == 0 and ci == 0)):
                            if firstb:
                                kb.w(te, "gp")
                            need(te, f"psS{slot}")
                            off = 0
                            while off < w:
                                ww = min(512, w - off)
                                inst = te.matmul(
                                    psS[:, slot, off:off + ww], qro[:, qseg],
                                    kro[:, k0 * 128 + off:k0 * 128 + off + ww],
                                    start=True, stop=True)
                                off += ww
                            kb.inc("pe", inst)
                        blk.tensor(_smm)
                        pe_s = kb.cnt["pe"]
                        diag = (k1 - 1 == gq)

                        def _msk(v, slot=slot, w=w, pe_s=pe_s, diag=diag, mc=mc,
                                 first=(g == 0 and gq == 0)):
                            if first:
                                kb.w(v, "sq", ACL)
                            kb.w(v, "pe", pe_s)
                            if diag:
                                kb.st(v, "ve", v.tensor_add(
                                    psS[:, slot, w - 128:w],
                                    psS[:, slot, w - 128:w], M128_sb[:]))
                            kb.st(v, "ve", v.reduce_max(mc, psS[:, slot, 0:w],
                                                        axis=AX.X))
                        blk.vector(_msk)

                        if ci == 0:
                            def _bA(v, bA=bA, mA=mA):
                                kb.st(v, "ve", v.tensor_scalar_mul(bA, mA, -SCALE))
                            blk.vector(_bA)
                            ve_b = kb.cnt["ve"]
                            bias_use = bA
                        else:
                            def _mrg1(v, mA=mA, mB=mB, mfin=mfin, bfin=bfin):
                                kb.w(v, "pe")
                                kb.st(v, "ve", v.tensor_max(mfin, mA, mB))
                                kb.st(v, "ve", v.tensor_scalar_mul(bfin, mfin, -SCALE))
                            blk.vector(_mrg1)
                            ve_b = kb.cnt["ve"]
                            bias_use = bfin

                        def _exp(sc, slot=slot, w=w, k0=k0, ve_b=ve_b, ebuf=ebuf,
                                 bias_use=bias_use, scol=(sA, sB)[ci], par=par):
                            kb.w(sc, "ve", ve_b)
                            if k0 == 0:
                                need(sc, f"e{par}")
                            kb.st(sc, "se", sc.activation(
                                ebuf[:, k0 * 128:k0 * 128 + w], psS[:, slot, 0:w],
                                AF.Exp, bias=bias_use, scale=SCALE, accum_out=scol))
                            guard(f"psS{slot}", "se")
                        blk.scalar(_exp)
                    se_exp = kb.cnt["se"]

                    if two:
                        def _alp(sc, mA=mA, bfin=bfin, acol=acol):
                            kb.st(sc, "se", sc.activation(acol, mA, AF.Exp,
                                                          bias=bfin, scale=SCALE))
                        blk.scalar(_alp)
                        se_a = kb.cnt["se"]

                        def _mrg2(v, ebuf=ebuf, acol=acol, sA=sA, sB=sB,
                                  sfin=sfin, se_a=se_a):
                            kb.w(v, "se", se_a)
                            kb.st(v, "ve", v.tensor_scalar_mul(
                                ebuf[:, 0:1024], ebuf[:, 0:1024], acol))
                            kb.st(v, "ve", v.tensor_tensor(sfin, sA, acol, ALU.mult))
                            kb.st(v, "ve", v.tensor_add(sfin, sfin, sB))
                        blk.vector(_mrg2)
                    else:
                        def _mrg0(v, sA=sA, sfin=sfin, se_exp=se_exp):
                            kb.w(v, "se", se_exp)
                            kb.w(v, "pe")
                            kb.st(v, "ve", v.tensor_copy(sfin, sA))
                        blk.vector(_mrg0)
                    tile_info.append((qj, par, nkt, kb.cnt["ve"], kb.cnt["se"]))

                for (qj, par, nkt, ve_esc, se_tile) in tile_info:
                    ebuf = e_par[par]
                    for kt in range(nkt):
                        bkn = (kt // 2) % 2
                        pb = psTb[bkn]
                        po = (kt % 2) * 128

                        def _tp(te, kt=kt, ebuf=ebuf, ve_esc=ve_esc,
                                se_tile=se_tile, pb=pb, po=po, bkn=bkn):
                            if kt % 2 == 0:
                                need(te, f"psTb{bkn}")
                            if kt == 0:
                                kb.w(te, "se", se_tile)
                                kb.w(te, "ve", ve_esc)
                            kb.inc("pe", te.transpose(
                                pb[:, po:po + 128],
                                ebuf[:, kt * 128:(kt + 1) * 128], idb_sb[:]))
                        blk.tensor(_tp)
                        pe_tp = kb.cnt["pe"]

                        if kt % 2 == 1 or kt == nkt - 1:
                            k0e = kt - (kt % 2)
                            wid = (kt % 2) + 1

                            def _ev(v, kt=kt, qj=qj, pe_tp=pe_tp, pb=pb,
                                    k0e=k0e, wid=wid, bkn=bkn):
                                kb.w(v, "pe", pe_tp)
                                if k0e == 0:
                                    need(v, f"eT{qj}")
                                kb.st(v, "ve", v.tensor_copy(
                                    eT2[:, qj, k0e:k0e + wid, :],
                                    pb[:, 0:wid * 128]))
                                guard(f"psTb{bkn}", "ve")
                            blk.vector(_ev)
                    guard(f"e{par}", "pe")

                def _z(v, q0=q0):
                    kb.st(v, "ve", v.memset(eT2[:, 0, q0 + 1, :], 0))
                blk.vector(_z)
                ve_eT = kb.cnt["ve"]
                lim1 = q0 + 2

                def _av(te, lim1=lim1, ve_eT=ve_eT):
                    kb.w(te, "ve", ve_eT)
                    need(te, "av")
                    for kt in range(lim1):
                        inst = te.matmul(av_full[:, 0:256], vT[:, kt, :],
                                         eT2[:, 0:2, kt, :],
                                         start=(kt == 0),
                                         stop=(kt == lim1 - 1))
                    kb.inc("pe", inst)
                    guard("eT0", "pe")
                    guard("eT1", "pe")
                blk.tensor(_av)
                pe_av = kb.cnt["pe"]

                bseg = slice(q0 * 128, (q0 + 2) * 128)
                sfin0 = scols[:, 2:3]
                sfin1 = scols[:, 6:7]
                s2ps = psRow[0:1, 256:512]
                sr0 = psRow[0:1, 0:128]
                sr1 = psRow[0:1, 128:256]
                cbps = psRow[:, 0:256]

                def _gsq(v, pe_av=pe_av, bseg=bseg):
                    kb.w(v, "pe", pe_av)
                    need(v, "av")
                    kb.st(v, "ve", v.tensor_copy(av2_sb[:], av_full[:, 0:256]))
                    guard("av", "ve")
                    kb.st(v, "ve", v.tensor_tensor(sq_sb[:], av2_sb[:], av2_sb[:],
                                                   ALU.mult))
                blk.vector(_gsq)
                ve_gsq = kb.cnt["ve"]

                def _gs2(te, ve_gsq=ve_gsq, sfin0=sfin0, sfin1=sfin1):
                    kb.w(te, "ve", ve_gsq)
                    need(te, "psRa")
                    kb.inc("pe", te.matmul(s2ps, rvec_sb[:], sq_sb[:],
                                           start=True, stop=True))
                    kb.inc("pe", te.transpose(sr0, sfin0, idf_sb[:]))
                    kb.inc("pe", te.transpose(sr1, sfin1, idf_sb[:]))
                blk.tensor(_gs2)
                pe_s2 = kb.cnt["pe"]

                def _grow(v, pe_s2=pe_s2):
                    kb.w(v, "pe", pe_s2)
                    kb.st(v, "ve", v.tensor_copy(srow_sb[:], psRow[0:1, 0:256]))
                    kb.st(v, "ve", v.tensor_tensor(arow_sb[:], srow_sb[:],
                                                   srow_sb[:], ALU.mult))
                    kb.st(v, "ve", v.scalar_tensor_tensor(arow_sb[:], arow_sb[:],
                                                          EPS, s2ps, ALU.mult,
                                                          ALU.add))
                    guard("psRa", "ve")
                blk.vector(_grow)
                ve_row = kb.cnt["ve"]

                def _gln(sc, ve_row=ve_row):
                    kb.w(sc, "ve", ve_row)
                    kb.st(sc, "se", sc.activation(crow_sb[:], arow_sb[:], AF.Ln))
                    kb.st(sc, "se", sc.activation(crow_sb[:], crow_sb[:], AF.Exp,
                                                  scale=-0.5))
                    kb.st(sc, "se", sc.activation(sgt_sb[:], gbf[:, bseg], AF.Exp,
                                                  scale=-1.0))
                blk.scalar(_gln)
                se_ln = kb.cnt["se"]

                def _gcb(te, se_ln=se_ln):
                    kb.w(te, "se", se_ln)
                    need(te, "psRb")
                    kb.inc("pe", te.matmul(psRow[:, 0:128], ones_sb[:],
                                           crow_sb[:, 0:128], start=True, stop=True))
                    kb.inc("pe", te.matmul(psRow[:, 128:256], ones_sb[:],
                                           crow_sb[:, 128:256], start=True, stop=True))
                blk.tensor(_gcb)
                pe_cb = kb.cnt["pe"]

                def _gt(v, pe_cb=pe_cb, bseg=bseg,
                        first=(b2 == 0)):
                    kb.w(v, "pe", pe_cb)
                    if first:
                        need(v, "gat")
                    kb.st(v, "ve", v.tensor_scalar_add(sgt_sb[:], sgt_sb[:], 1.0))
                    kb.st(v, "ve", v.reciprocal(sgt_sb[:], sgt_sb[:]))
                    kb.st(v, "ve", v.tensor_tensor(sgt_sb[:], sgt_sb[:],
                                                   gbf[:, bseg], ALU.mult))
                    kb.st(v, "ve", v.tensor_scalar_mul(t1_sb[:], av2_sb[:],
                                                       nw_sb[:]))
                    kb.st(v, "ve", v.tensor_tensor(t1_sb[:], t1_sb[:], cbps,
                                                   ALU.mult))
                    guard("psRb", "ve")
                    kb.st(v, "ve", v.tensor_tensor(gat_sb[:, bseg], t1_sb[:],
                                                   sgt_sb[:], ALU.mult))
                blk.vector(_gt)
            guard("qro", "pe")
            guard("kro", "pe")
            guard("vT", "pe")
            guard("gbf", "ve")

            def _st(sy, g=g):
                kb.w(sy, "ve")
                kb.inc("sq", sy.dma_start(out=ccin[g], in_=gat_sb[:]), 16)
                guard("gat", "sq")
            blk.sync(_st)

        if "gated" in taps:
            t = mktap("gated", [NHL, 128, TK], BF16)

            def _tapg(sy):
                kb.w(sy, "sq")
                kb.inc("sq", sy.dma_start(out=t, in_=ccin.ap()), 16)
            blk.sync(_tapg)

    def _own(sc):
        kb.w(sc, "se")
        kb.w(sc, "ve")
        h = kb.sem["wx"]
        par = kb.par_reg
        if True:
            with sc.If_eq(par, 0):
                sc.dma_start(out=xn_own[:], in_=xn_sb[:, :, 0:OWN]).then_inc(h, 16)
            with sc.Else():
                sc.dma_start(out=xn_own[:], in_=xn_sb[:, :, OWN:TK]).then_inc(h, 16)
        kb.cnt["wx"] += 16
    blk.scalar(_own)


def emit_cc(kb, blk, nc, ccin, ccout):
    now = {s: kb.cnt[s] for s in ("pe", "ve", "se")}

    def _cc(gp):
        kb.w(gp, "sq")
        for s in ("pe", "ve", "se"):
            kb.w(gp, s, now[s])
        kb.inc("cc", gp.collective_compute(
            "AllGather", ALU.bypass,
            replica_groups=[[0, 1], [2, 3], [4, 5], [6, 7]],
            ins=[ccin.ap().rearrange("h p t -> (h p) t").opt()],
            outs=[ccout.ap().rearrange("h p t -> (h p) t").opt()]))
    blk.gpsimd(_cc)


def emit_oproj(kb, blk, nc, X, acc_sb, xo, wo, ccin, ccout, taps, mktap,
               init_acc=False):
    from contextlib import ExitStack
    with ExitStack() as XO:
        gf = XO.enter_context(nc.sbuf_tensor("gf", [128, C16, OWN], BF16, side="right"))
        wob0 = XO.enter_context(nc.sbuf_tensor("wob0", [128, C16, 512], BF16, side="right"))
        wob1 = XO.enter_context(nc.sbuf_tensor("wob1", [128, C16, 512], BF16, side="right"))
        psO = XO.enter_context(nc.psum_tensor("psO", [128, 2, 512], F32))
        wobs = (wob0, wob1)
        now = {s: kb.cnt[s] for s in ("pe", "ve", "se", "gp", "wx", "sq")}
        guards = {k: [(s, kb.cnt[s]) for s in ("pe", "ve", "se", "wx", "sq")]
                  for k in ("psO0", "psO1", "wob0", "wob1")}

        def _gfl(sc):
            kb.w(sc, "cc")
            for s in ("pe", "ve", "se", "gp", "sq", "wx"):
                kb.w(sc, s, now[s])
            h = kb.sem["wx"]
            par = kb.par_reg
            with sc.If_eq(par, 0):
                sc.dma_start(out=gf[:], in_=ccout.ap()[:, :, 0:OWN]
                             .rearrange("h p t -> p h t")).then_inc(h, 16)
            with sc.Else():
                sc.dma_start(out=gf[:], in_=ccout.ap()[:, :, OWN:TK]
                             .rearrange("h p t -> p h t")).then_inc(h, 16)
            kb.cnt["wx"] += 16
        blk.scalar(_gfl)
        wx_gf = kb.cnt["wx"]

        if init_acc:
            def _acc0(sy):
                for s in ("pe", "ve", "se", "gp", "sq", "wx"):
                    kb.w(sy, s, now[s])
                kb.inc("sq", sy.dma_start(out=acc_sb[:, :, 0:512], in_=xo[0]), 16)
                kb.inc("sq", sy.dma_start(out=acc_sb[:, :, 512:1024], in_=xo[1]), 16)
            blk.sync(_acc0)
        sq_acc = kb.cnt["sq"]

        for j in range(4):
            wob = wobs[j % 2]

            def _wol(gp, j=j, first=(j == 0)):
                if first:
                    for s in ("pe", "ve", "se", "sq", "wx"):
                        kb.w(gp, s, now[s])
                for sem, cnt in guards[f"wob{j % 2}"]:
                    kb.w(gp, sem, cnt)
                guards[f"wob{j % 2}"] = []
                kb.inc(("wa", "wb")[j % 2], gp.dma_start(out=wobs[j % 2][:], in_=wo[:, j]), 16)
            blk.gpsimd(_wol)
            for mc4 in range(4):
                m = j * 4 + mc4
                for ts in range(2):
                    sl = (m * 2 + ts) % 2
                    tseg = slice(ts * 512, (ts + 1) * 512)

                    def _omm(te, wob=wob, mc4=mc4, tseg=tseg, sl=sl, m=m, ts=ts):
                        if mc4 == 0 and ts == 0:
                            kb.w(te, "wa")
                            kb.w(te, "wb")
                        if m == 0 and ts == 0:
                            kb.w(te, "wx", wx_gf)
                        for sem, cnt in guards[f"psO{sl}"]:
                            kb.w(te, sem, cnt)
                        guards[f"psO{sl}"] = []
                        for c in range(C16):
                            inst = te.matmul(psO[:, sl, :],
                                             wob[:, c, mc4 * 128:(mc4 + 1) * 128],
                                             gf[:, c, tseg],
                                             start=(c == 0), stop=(c == C16 - 1))
                        kb.inc("pe", inst)
                    blk.tensor(_omm)
                    pe_o = kb.cnt["pe"]

                    def _oacc(v, m=m, tseg=tseg, sl=sl, pe_o=pe_o, first=(j == 0 and mc4 == 0 and ts == 0)):
                        kb.w(v, "pe", pe_o)
                        if first:
                            kb.w(v, "sq", sq_acc)
                        kb.st(v, "ve", v.tensor_tensor(acc_sb[:, m, tseg],
                                                       acc_sb[:, m, tseg],
                                                       psO[:, sl, :], ALU.add))
                        guards[f"psO{sl}"].append(("ve", kb.cnt["ve"]))
                    blk.vector(_oacc)
            guards[f"wob{j % 2}"].append(("pe", kb.cnt["pe"]))

    if "acc_o" in taps:
        t = mktap("acc_o", [128, C16, OWN], F32)

        def _tapo(sy):
            kb.w(sy, "ve")
            kb.inc("sq", sy.dma_start(out=t, in_=acc_sb[:]), 16)
        blk.sync(_tapo)


def emit_moe(kb, blk, nc, X, acc_sb, xn_own, cmb_sb, wup, wdn, ones_sb,
             xo, taps, mktap):
    from contextlib import ExitStack
    with ExitStack() as XM:
        a_blk = XM.enter_context(nc.sbuf_tensor("a_blk", [128, 2, 8, 1024], BF16, side="right"))
        wupb0 = XM.enter_context(nc.sbuf_tensor("wupb0", [128, C16, 256], BF16, side="right"))
        wupb1 = XM.enter_context(nc.sbuf_tensor("wupb1", [128, C16, 256], BF16, side="right"))
        wdnb = XM.enter_context(nc.sbuf_tensor("wdnb", [128, 4, 8, 512], BF16, side="right"))
        cbf = XM.enter_context(nc.sbuf_tensor("cbf", [128, 2, 2, 1024], BF16, side="right"))
        crow = XM.enter_context(nc.sbuf_tensor("crow", [1, 1024], F32, side="right"))
        trelu = XM.enter_context(nc.sbuf_tensor("trelu", [128, 1024], F32, side="right"))
        tsq = XM.enter_context(nc.sbuf_tensor("tsq", [128, 1024], BF16, side="right"))
        psH = XM.enter_context(nc.psum_tensor("psH", [128, 2, 1024], F32))
        psR = XM.enter_context(nc.psum_tensor("psR", [128, 2, 1024], F32))
        wupbs = (wupb0, wupb1)
        now = {s: kb.cnt[s] for s in ("pe", "ve", "se", "gp", "wx", "sq")}
        G = {}

        def need(eng, key):
            for sem, cnt in G.get(key, ()):
                kb.w(eng, sem, cnt)
            G[key] = []

        def guard(key, sem):
            G.setdefault(key, []).append((sem, kb.cnt[sem]))

        for k in ("psH0", "psH1", "psR0", "psR1", "wdn0", "wdn1", "wdn2", "wdn3"):
            G[k] = [(s, now[s]) for s in ("pe", "ve", "se", "wx", "sq")]

        def _acc0(sy):
            for s in ("pe", "ve", "se", "gp", "sq", "wx"):
                kb.w(sy, s, now[s])
            kb.inc("sq", sy.dma_start(out=acc_sb[:, :, 0:512], in_=xo[0]), 16)
            kb.inc("sq", sy.dma_start(out=acc_sb[:, :, 512:1024], in_=xo[1]), 16)
        blk.sync(_acc0)
        sq_acc = kb.cnt["sq"]

        def prep_cb(eb):
            sl2 = eb % 2
            e0p = 2 * eb
            for ep in range(2):
                e = e0p + ep

                def _crow(sc, e=e, first=(eb == 0 and ep == 0)):
                    if first:
                        for s in ("pe", "ve", "se", "gp", "sq", "wx"):
                            kb.w(sc, s, now[s])
                    kb.w(sc, "pe")
                    h = kb.sem["wx"]
                    par = kb.par_reg
                    if True:
                        with sc.If_eq(par, 0):
                            sc.dma_start(out=crow[:], in_=cmb_sb[e:e + 1, 0:OWN]).then_inc(h, 16)
                        with sc.Else():
                            sc.dma_start(out=crow[:], in_=cmb_sb[e:e + 1, OWN:TK]).then_inc(h, 16)
                    kb.cnt["wx"] += 16
                blk.scalar(_crow)
                wx_cr = kb.cnt["wx"]

                def _cbm(te, ep=ep, wx_cr=wx_cr):
                    kb.w(te, "wx", wx_cr)
                    need(te, "psH0")
                    kb.inc("pe", te.matmul(psH[:, 0, 0:512], ones_sb[:],
                                           crow[0:1, 0:512], start=True, stop=True))
                    kb.inc("pe", te.matmul(psH[:, 0, 512:1024], ones_sb[:],
                                           crow[0:1, 512:1024], start=True, stop=True))
                blk.tensor(_cbm)
                pe_cb = kb.cnt["pe"]

                def _cbe(v, ep=ep, pe_cb=pe_cb, sl2=sl2):
                    kb.w(v, "pe", pe_cb)
                    need(v, f"cbf{sl2}{ep}")
                    kb.st(v, "ve", v.tensor_copy(cbf[:, sl2, ep, :], psH[:, 0, :]))
                    guard("psH0", "ve")
                blk.vector(_cbe)

        prep_cb(0)
        for eb in range(8):
            e0 = 2 * eb
            sl2 = eb % 2

            wdl_cnt = {}

            def _wdl_pair(mq):
                rec = []
                for ep in range(2):
                    slw = (mq % 2) * 2 + ep

                    def _wdl(gp, e=e0 + ep, mq=mq, slw=slw):
                        need(gp, f"wdn{slw}")
                        kb.inc(("wc", "wd", "we", "wf")[slw], gp.dma_start(
                            out=wdnb[:, slw], in_=wdn[e, mq]), 16)
                    blk.gpsimd(_wdl)
                    s = ("wc", "wd", "we", "wf")[slw]
                    rec.append((s, kb.cnt[s]))
                wdl_cnt[mq] = rec
            _wdl_pair(0)
            _wdl_pair(1)
            if eb + 1 < 8:
                prep_cb(eb + 1)

            for ep in range(2):
                e = e0 + ep
                for qh in range(4):
                    wb = wupbs[qh % 2]

                    def _wul(gp, e=e, qh=qh, first=(eb == 0 and ep == 0 and qh == 0)):
                        if first:
                            for s in ("pe", "ve", "se", "sq", "wx"):
                                kb.w(gp, s, now[s])
                        need(gp, f"wup{qh % 2}")
                        kb.inc(("wa", "wb")[qh % 2], gp.dma_start(out=wupbs[qh % 2][:], in_=wup[e, qh]), 16)
                    blk.gpsimd(_wul)

                    for i2 in range(2):
                        ic = qh * 2 + i2
                        sl = ic % 2

                        def _umm(te, wb=wb, i2=i2, sl=sl):
                            if i2 == 0:
                                kb.w(te, "wa")
                                kb.w(te, "wb")
                            need(te, f"psH{sl}")
                            for ts in range(2):
                                for c in range(C16):
                                    inst = te.matmul(
                                        psH[:, sl, ts * 512:(ts + 1) * 512],
                                        wb[:, c, i2 * 128:(i2 + 1) * 128],
                                        xn_own[:, c, ts * 512:(ts + 1) * 512],
                                        start=(c == 0), stop=(c == C16 - 1))
                            kb.inc("pe", inst)
                        blk.tensor(_umm)
                        pe_u = kb.cnt["pe"]

                        def _rlv(v, sl=sl, pe_u=pe_u):
                            kb.w(v, "pe", pe_u)
                            need(v, "trelu")
                            kb.st(v, "ve", v.tensor_scalar_max(trelu[:],
                                                               psH[:, sl, :], 0.0))
                            guard(f"psH{sl}", "ve")
                        blk.vector(_rlv)
                        ve_rl = kb.cnt["ve"]

                        def _rl(sc, ve_rl=ve_rl):
                            kb.w(sc, "ve", ve_rl)
                            need(sc, "tsq")
                            kb.st(sc, "se", sc.activation(tsq[:], trelu[:], AF.Square))
                            guard("trelu", "se")
                        blk.scalar(_rl)
                        se_rl = kb.cnt["se"]

                        def _am(v, ep=ep, ic=ic, se_rl=se_rl,
                                first=(ep == 0 and qh == 0 and i2 == 0)):
                            kb.w(v, "se", se_rl)
                            if first:
                                need(v, "ablk")
                            kb.st(v, "ve", v.tensor_tensor(a_blk[:, ep, ic, :],
                                                           tsq[:], cbf[:, sl2, ep, :],
                                                           ALU.mult))
                            guard("tsq", "ve")
                            if ic == 7:
                                guard(f"cbf{sl2}{ep}", "ve")
                        blk.vector(_am)
                    guard(f"wup{qh % 2}", "pe")

            for mq in range(4):
                for m4 in range(4):
                    m = mq * 4 + m4
                    sl = m % 2

                    def _dmm(te, mq=mq, m4=m4, sl=sl, ve_a=kb.cnt["ve"],
                             wcc=None):
                        if m4 == 0:
                            for s, c in wdl_cnt[mq]:
                                kb.w(te, s, c)
                            kb.w(te, "ve", ve_a)
                        need(te, f"psR{sl}")
                        for ts in range(2):
                            for ep in range(2):
                                slw = (mq % 2) * 2 + ep
                                for ic in range(8):
                                    inst = te.matmul(
                                        psR[:, sl, ts * 512:(ts + 1) * 512],
                                        wdnb[:, slw, ic, m4 * 128:(m4 + 1) * 128],
                                        a_blk[:, ep, ic, ts * 512:(ts + 1) * 512],
                                        start=(ep == 0 and ic == 0),
                                        stop=(ep == 1 and ic == 7))
                        kb.inc("pe", inst)
                        if m4 == 3:
                            for ep in range(2):
                                guard(f"wdn{(mq % 2) * 2 + ep}", "pe")
                    blk.tensor(_dmm)
                    pe_d = kb.cnt["pe"]

                    def _dacc(v, m=m, sl=sl, pe_d=pe_d,
                              first=(eb == 0 and mq == 0 and m4 == 0)):
                        kb.w(v, "pe", pe_d)
                        if first:
                            kb.w(v, "sq", sq_acc)
                        kb.st(v, "ve", v.tensor_tensor(acc_sb[:, m, :],
                                                       acc_sb[:, m, :],
                                                       psR[:, sl, :], ALU.add))
                        guard(f"psR{sl}", "ve")
                    blk.vector(_dacc)
                if mq + 2 < 4:
                    _wdl_pair(mq + 2)
            guard("ablk", "pe")
            G.setdefault("ablk_w", [])

    if "acc_m" in taps:
        t = mktap("acc_m", [128, C16, OWN], F32)

        def _tapm(sy):
            kb.w(sy, "ve")
            kb.inc("sq", sy.dma_start(out=t, in_=acc_sb[:]), 16)
        blk.sync(_tapm)


def emit_shared(kb, blk, nc, X, acc_sb, xn_own, sup, sdn, taps, mktap):
    from contextlib import ExitStack
    with ExitStack() as XS:
        hs = XS.enter_context(nc.sbuf_tensor("hs", [128, 8, 1024], BF16, side="right"))
        supb0 = XS.enter_context(nc.sbuf_tensor("supb0", [128, C16, 512], BF16, side="right"))
        supb1 = XS.enter_context(nc.sbuf_tensor("supb1", [128, C16, 512], BF16, side="right"))
        sdnb0 = XS.enter_context(nc.sbuf_tensor("sdnb0", [128, 8, 512], BF16, side="right"))
        sdnb1 = XS.enter_context(nc.sbuf_tensor("sdnb1", [128, 8, 512], BF16, side="right"))
        strl = XS.enter_context(nc.sbuf_tensor("strl", [128, 1024], F32, side="right"))
        stsq = XS.enter_context(nc.sbuf_tensor("stsq", [128, 1024], F32, side="right"))
        psH = XS.enter_context(nc.psum_tensor("psHs", [128, 2, 1024], F32))
        psR = XS.enter_context(nc.psum_tensor("psRs", [128, 2, 1024], F32))
        supbs = (supb0, supb1)
        sdnbs = (sdnb0, sdnb1)
        now = {s: kb.cnt[s] for s in ("pe", "ve", "se", "gp", "wx", "sq")}
        G = {}

        def need(eng, key):
            for sem, cnt in G.get(key, ()):
                kb.w(eng, sem, cnt)
            G[key] = []

        def guard(key, sem):
            G.setdefault(key, []).append((sem, kb.cnt[sem]))

        for k in ("psH0", "psH1", "psR0", "psR1", "sup0", "sup1", "sdn0", "sdn1"):
            G[k] = [(s, now[s]) for s in ("pe", "ve", "se", "wx", "sq")]

        for ib in range(4):
            for jh in range(2):
                def _sul(gp, ib=ib, jh=jh, first=(ib == 0 and jh == 0)):
                    if first:
                        for s in ("pe", "ve", "se", "sq", "wx"):
                            kb.w(gp, s, now[s])
                    need(gp, f"sup{jh}")
                    kb.inc(("wa", "wb")[jh], gp.dma_start(out=supbs[jh][:], in_=sup[ib * 2 + jh]), 16)
                blk.gpsimd(_sul)

            for jh in range(2):
                for i2 in range(4):
                    ic = jh * 4 + i2
                    sl = ic % 2

                    def _smm(te, jh=jh, i2=i2, sl=sl):
                        if i2 == 0:
                            kb.w(te, "wa")
                            kb.w(te, "wb")
                        need(te, f"psH{sl}")
                        for ts in range(2):
                            for c in range(C16):
                                inst = te.matmul(
                                    psH[:, sl, ts * 512:(ts + 1) * 512],
                                    supbs[jh][:, c, i2 * 128:(i2 + 1) * 128],
                                    xn_own[:, c, ts * 512:(ts + 1) * 512],
                                    start=(c == 0), stop=(c == C16 - 1))
                        kb.inc("pe", inst)
                        if i2 == 3:
                            guard(f"sup{jh}", "pe")
                    blk.tensor(_smm)
                    pe_u = kb.cnt["pe"]

                    def _rlv(v, sl=sl, pe_u=pe_u):
                        kb.w(v, "pe", pe_u)
                        need(v, "strl")
                        kb.st(v, "ve", v.tensor_scalar_max(strl[:],
                                                           psH[:, sl, :], 0.0))
                        guard(f"psH{sl}", "ve")
                    blk.vector(_rlv)
                    ve_rl = kb.cnt["ve"]

                    def _rl(sc, ic=ic, ve_rl=ve_rl):
                        kb.w(sc, "ve", ve_rl)
                        kb.st(sc, "se", sc.activation(hs[:, ic, :], strl[:],
                                                      AF.Square))
                        guard("strl", "se")
                    blk.scalar(_rl)

            for j in range(4):
                def _sdl(gp, ib=ib, j=j):
                    need(gp, f"sdn{j % 2}")
                    kb.inc(("wc", "wd")[j % 2], gp.dma_start(out=sdnbs[j % 2][:], in_=sdn[ib, j]), 16)
                blk.gpsimd(_sdl)
                for m4 in range(4):
                    m = j * 4 + m4
                    sl = m % 2

                    def _dmm(te, j=j, m4=m4, sl=sl, se_a=kb.cnt["se"]):
                        if m4 == 0:
                            kb.w(te, "wc")
                            kb.w(te, "wd")
                            kb.w(te, "se", se_a)
                        need(te, f"psR{sl}")
                        for ts in range(2):
                            for ic in range(8):
                                inst = te.matmul(
                                    psR[:, sl, ts * 512:(ts + 1) * 512],
                                    sdnbs[j % 2][:, ic, m4 * 128:(m4 + 1) * 128],
                                    hs[:, ic, ts * 512:(ts + 1) * 512],
                                    start=(ic == 0), stop=(ic == 7))
                        kb.inc("pe", inst)
                        if m4 == 3:
                            guard(f"sdn{j % 2}", "pe")
                    blk.tensor(_dmm)
                    pe_d = kb.cnt["pe"]

                    def _dacc(v, m=m, sl=sl, pe_d=pe_d):
                        kb.w(v, "pe", pe_d)
                        kb.st(v, "ve", v.tensor_tensor(acc_sb[:, m, :],
                                                       acc_sb[:, m, :],
                                                       psR[:, sl, :], ALU.add))
                        guard(f"psR{sl}", "ve")
                    blk.vector(_dacc)


def stage_inputs(L0_attn_hidden_states, L0_attn_wq, L0_attn_wk, L0_attn_wv,
                 L0_attn_wg, L0_attn_wo, L0_attn_norm_w, L0_ffn_router_weight,
                 L0_ffn_e_bias, L0_ffn_w_up, L0_ffn_w_down, L0_ffn_shared_up,
                 L0_ffn_shared_down):
    f = np.ascontiguousarray
    x = np.asarray(L0_attn_hidden_states, dtype=np.float32)
    wt4 = np.stack([np.asarray(w, np.float32).T for w in
                    (L0_attn_wq, L0_attn_wk, L0_attn_wv, L0_attn_wg)])
    wo_s = f(np.asarray(L0_attn_wo, np.float32).T
             .reshape(C16, 128, 4, 512).transpose(1, 2, 0, 3))
    nw_s = f(np.asarray(L0_attn_norm_w, np.float32).reshape(128, 1))
    rwt_s = f(np.asarray(L0_ffn_router_weight, np.float32).T
              .reshape(C16, 128, NE).transpose(1, 0, 2))
    eb_s = f(np.asarray(L0_ffn_e_bias, np.float32).reshape(NE, 1))
    wup_s = f(np.asarray(L0_ffn_w_up, np.float32)
              .reshape(NE, C16, 128, 4, 256).transpose(0, 3, 2, 1, 4))
    wdn_s = f(np.asarray(L0_ffn_w_down, np.float32)
              .reshape(NE, 8, 128, 4, 512).transpose(0, 3, 2, 1, 4))
    sup_s = f(np.asarray(L0_ffn_shared_up, np.float32)
              .reshape(C16, 128, 8, 512).transpose(2, 1, 0, 3))
    sdn_s = f(np.asarray(L0_ffn_shared_down, np.float32)
              .reshape(4, 8, 128, 4, 512).transpose(0, 3, 2, 1, 4))

    in_maps = []
    for c in range(8):
        b, H = c // 2, c % 2
        xb = x[b]
        xT_s = f(xb.reshape(NLG, LNG, C16, 128).transpose(0, 3, 2, 1))
        xo_s = f(xb[H * OWN:(H + 1) * OWN].reshape(2, 512, C16, 128)
                 .transpose(0, 3, 2, 1))
        wq_s = f(wt4[:, :, H * 1024:(H + 1) * 1024]
                 .reshape(4, C16, 128, NHL, 128).transpose(3, 2, 0, 1, 4))
        in_maps.append({
            "xT": xT_s, "xo": xo_s, "wqkvg": wq_s, "wo": wo_s, "nw": nw_s,
            "rwt": rwt_s, "ebias": eb_s, "wup": wup_s, "wdn": wdn_s,
            "sup": sup_s, "sdn": sdn_s,
        })
    return in_maps


def kernel(**inputs):
    in_maps = stage_inputs(**inputs)
    nc, _ = build()
    res = run_bass_kernel_spmd(nc, in_maps, core_ids=list(range(8)))
    out = np.empty((4, TK, D), dtype=np.float32)
    for c in range(8):
        b, H = c // 2, c % 2
        o = res.results[c]["outT"]
        out[b, H * OWN:(H + 1) * OWN, :] = o.transpose(2, 1, 0).reshape(OWN, D)
    return out
